# version 01
# baseline (speedup 1.0000x reference)
import math
import numpy as np
import ml_dtypes
import concourse.bass as bass
import concourse.mybir as mybir
from concourse.bass_utils import run_bass_kernel_spmd
from concourse.alu_op_type import AluOpType as OP

F32 = mybir.dt.float32
BF16 = mybir.dt.bfloat16
AF = mybir.ActivationFunctionType
AX = mybir.AxisListType
f32 = np.float32
bf16 = ml_dtypes.bfloat16

NCORE = 8
BATCH = 16
SEQS = 2
T = 2048
NTOK = SEQS * T
D = 2048
TS = 128
NSLICE = NTOK // TS
SUP = 512
NSUP = NTOK // SUP
NST = 256
HFF = 8192
NH = 16
DQK = 192
DROPE = 64
DNOPE = 128
KVL = 512
DV = 128
NE = 8
RL = 16
SCALE_QK = 1.0 / math.sqrt(DQK)
EPS_LN = 1e-5
EPS_RMS = 1.1920928955078125e-07
NEG_BIG = -1e30
TWO_PI = 2.0 * math.pi
MAGIC = 12582912.0

DBG_DUMP = None
PHASES_LIMIT = 99


_const_cache = {}

def host_constants():
    if _const_cache:
        return _const_cache
    trow = np.broadcast_to(np.arange(T, dtype=np.float64), (128, T)).astype(f32).copy()
    freqs = 1.0 / (10000.0 ** (np.arange(0, DROPE, 2, dtype=np.float64) / DROPE))
    tpos = np.arange(T, dtype=np.float64)[:, None] * freqs[None, :]
    ropec = np.tile(np.cos(tpos), (1, NH)).astype(f32)
    ropes = np.tile(np.sin(tpos), (1, NH)).astype(f32)
    m = np.arange(1024) - 384
    tri = np.where(np.arange(128)[:, None] > m[None, :], f32(NEG_BIG), f32(0.0)).astype(f32)
    onesbf = np.ones((128, 128), dtype=bf16)
    onesf = np.ones((128, 128), dtype=f32)
    identbf = np.eye(128, dtype=bf16)
    identf = np.eye(128, dtype=f32)
    sel2 = np.zeros((NE, 128), dtype=bf16)
    for e in range(NE):
        sel2[e, e * RL:(e + 1) * RL] = 2.0
    sel2f = sel2.astype(f32)
    qropec = (ropec.astype(np.float64) * SCALE_QK).astype(f32)
    qropes = (ropes.astype(np.float64) * SCALE_QK).astype(f32)
    _const_cache.update(dict(QROPEC=qropec, QROPES=qropes,
                             TROW=trow, ROPEC=ropec, ROPES=ropes, TRI=tri,
                             ONESBF=onesbf, ONESF=onesf, IDENTBF=identbf,
                             IDENTF=identf, SEL2=sel2, SEL2F=sel2f))
    return _const_cache


class Sem:
    def __init__(self, nc, name):
        self.h = nc.alloc_semaphore(name)
        self.n = 0

    def dinc(self, inst):
        inst.then_inc(self.h, 16)
        self.n += 16
        return self.n

    def einc(self, inst):
        inst.then_inc(self.h, 1)
        self.n += 1
        return self.n


class PSem:

    def __init__(self, c, a, b):
        self.s = [c[a], c[b]]
        self.tgt = {}

    @staticmethod
    def _par(key):
        if isinstance(key, int):
            return key % 2
        if isinstance(key, str):
            return 0
        return key[0] % 2

    def pre(self, keys):
        cnt = [self.s[0].n, self.s[1].n]
        for key in keys:
            p = self._par(key)
            cnt[p] += 16
            self.tgt[key] = (p, cnt[p])

    def issue(self, key, inst):
        p = self._par(key)
        n = self.s[p].dinc(inst)
        if key in self.tgt:
            assert self.tgt[key] == (p, n) or self.tgt[key][1] >= n, (key, self.tgt[key], n)
        self.tgt[key] = (p, max(n, self.tgt.get(key, (p, 0))[1]))
        return n

    def wait(self, eng, key):
        p, n = self.tgt[key]
        eng.wait_ge(self.s[p].h, n)


def mk_chain(eng, sem):
    def _c(inst):
        n = sem.einc(inst)
        eng.wait_ge(sem.h, n)
        return inst
    return _c


class Arena:
    def __init__(self, nc, f32_cols):
        self.t = nc.alloc_sbuf_tensor("arena", [128, f32_cols], F32)
        self.cols = f32_cols
        self.off = 0

    def reset(self):
        self.off = 0

    def alloc(self, parts, cols, dt=F32):
        per = 4 if dt == F32 else 2
        ncol_f32 = (cols * per + 3) // 4
        self.off = (self.off + 7) & ~7
        assert self.off + ncol_f32 <= self.cols, \
            f"arena overflow {self.off}+{ncol_f32}>{self.cols}"
        base = self.off
        self.off += ncol_f32
        if dt != F32:
            return self.t[:].bitcast(dt)[0:parts, base * 2: base * 2 + cols]
        return self.t[0:parts, base: base + cols]


def build_program(n_phases=99, dump=None):
    nc = bass.Bass("TRN2", target_bir_lowering=False)
    c = {}
    c["nc"] = nc

    def dram(name, shape, dt, kind):
        h = nc.dram_tensor(name, shape, dt, kind=kind)
        c[name] = h
        return h

    dram("x", [NTOK, D], F32, "ExternalInput")
    dram("nu_log", [NST], F32, "ExternalInput")
    dram("theta_log", [NST], F32, "ExternalInput")
    dram("B_re", [NST, D], F32, "ExternalInput")
    dram("B_im", [NST, D], F32, "ExternalInput")
    dram("C_re", [D, NST], F32, "ExternalInput")
    dram("C_im", [D, NST], F32, "ExternalInput")
    dram("Dvec", [D], F32, "ExternalInput")
    dram("proj1", [HFF, D], F32, "ExternalInput")
    dram("proj1_bias", [HFF], F32, "ExternalInput")
    dram("proj2", [D, HFF], F32, "ExternalInput")
    dram("proj2_bias", [D], F32, "ExternalInput")
    dram("Wq", [D, NH * DQK], F32, "ExternalInput")
    dram("Wkv_a", [D, KVL + DROPE], F32, "ExternalInput")
    dram("kv_norm_w", [KVL], F32, "ExternalInput")
    dram("Wkv_b", [KVL, NH * (DNOPE + DV)], F32, "ExternalInput")
    dram("Wo", [NH * DV, D], F32, "ExternalInput")
    dram("base_w", [D, D], F32, "ExternalInput")
    dram("base_b", [D], F32, "ExternalInput")
    dram("router_w", [NE, D], F32, "ExternalInput")
    dram("router_b", [NE], F32, "ExternalInput")
    dram("lora_A", [NE, D, RL], F32, "ExternalInput")
    dram("lora_B", [NE, RL, D], F32, "ExternalInput")
    dram("TROW", [128, T], F32, "ExternalInput")
    dram("ROPEC", [T, 512], F32, "ExternalInput")
    dram("QROPEC", [T, 512], F32, "ExternalInput")
    dram("QROPES", [T, 512], F32, "ExternalInput")
    dram("ROPES", [T, 512], F32, "ExternalInput")
    dram("TRI", [128, 1024], F32, "ExternalInput")
    dram("ONESBF", [128, 128], BF16, "ExternalInput")
    dram("ONESF", [128, 128], F32, "ExternalInput")
    dram("IDENTBF", [128, 128], BF16, "ExternalInput")
    dram("IDENTF", [128, 128], F32, "ExternalInput")
    dram("SEL2", [NE, 128], BF16, "ExternalInput")
    dram("SEL2F", [NE, 128], F32, "ExternalInput")
    dram("s_p1bf", [HFF, D], BF16, "Internal")
    dram("s_p2bf", [D, HFF], BF16, "Internal")
    dram("s_brebf", [NST, D], BF16, "Internal")
    dram("s_bimbf", [NST, D], BF16, "Internal")
    dram("s_crebf", [D, NST], BF16, "Internal")
    dram("s_cimbf", [D, NST], BF16, "Internal")
    dram("s_bwbf", [D, D], BF16, "Internal")
    dram("s_rwbf", [16, D], BF16, "Internal")
    dram("s_wqbf", [D, NH * DQK], BF16, "Internal")
    dram("s_wkvbbf", [KVL, NH * (DNOPE + DV)], BF16, "Internal")
    dram("s_xn0", [NTOK, D], BF16, "Internal")
    dram("s_g0", [NTOK, D], F32, "Internal")
    dram("s_x1", [NTOK, D], F32, "Internal")
    dram("s_xn1", [NTOK, D], BF16, "Internal")
    dram("s_qnope", [NH * DNOPE, NTOK], BF16, "Internal")
    dram("s_qrope", [NTOK, NH * 128], BF16, "Internal")
    dram("s_krope", [NTOK, 128], BF16, "Internal")
    dram("s_ckvn", [NTOK, KVL], BF16, "Internal")
    dram("s_knope", [SEQS * NH * DNOPE, T], BF16, "Internal")
    dram("s_v", [NTOK, NH * DV], BF16, "Internal")
    dram("s_o", [SEQS * NH * DV, T], BF16, "Internal")
    dram("s_s1", [NTOK, D], F32, "Internal")
    dram("s_g1", [NTOK, D], F32, "Internal")
    dram("out", [NTOK, D], F32, "ExternalOutput")

    c["slab"] = [nc.alloc_sbuf_tensor(f"slab{i}", [128, NTOK], F32) for i in range(4)]
    c["Hsb"] = [nc.alloc_sbuf_tensor(f"hsb{i}", [128, NTOK], BF16) for i in range(4)]
    for nm, shape, dt in [("st_m0", [128, NSLICE], F32), ("st_r0", [128, NSLICE], F32),
                          ("st_m1", [128, NSLICE], F32), ("st_r1", [128, NSLICE], F32),
                          ("v_r", [128, 2], F32), ("v_gam", [128, 2], F32),
                          ("v_th", [128, 2], F32),
                          ("b1_sb", [128, HFF // 128], F32),
                          ("b2row", [1, D], BF16), ("bbrow", [1, D], BF16),
                          ("rbrow", [1, NE], BF16), ("drow", [1, D], F32),
                          ("kvnw_sb", [128, KVL // 128], F32),
                          ("k_ones_bf", [128, 128], BF16), ("k_ones_f", [128, 128], F32),
                          ("k_id_bf", [128, 128], BF16), ("k_id_f", [128, 128], F32),
                          ("k_sel2", [NE, 128], BF16),
                          ("k_sel2f", [NE, 128], F32)]:
        c[nm] = nc.alloc_sbuf_tensor(nm, shape, dt)
    c["psA"] = nc.alloc_psum_tensor("psA", [128, 2048], F32)
    c["psB"] = nc.alloc_psum_tensor("psB", [128, 2048], F32)
    c["arena"] = Arena(nc, 23700)

    for s in ["sem_ld", "sem_ldB", "sem_ld2", "sem_gp", "sem_gpB", "sem_st",
              "sem_stB", "sem_pe", "sem_dve", "sem_dve2", "sem_act", "sem_pool",
              "sem_vself", "sem_sself", "sem_p1a", "sem_p1b", "sem_p2a", "sem_p2b",
              "sem_st2a", "sem_st2b"]:
        c[s] = Sem(nc, s)

    dump_map = {"xn0": "s_xn0", "g0": "s_g0", "x1": "s_x1", "xn1": "s_xn1",
                "qnope": "s_qnope", "qrope": "s_qrope", "g1": "s_g1",
                "out": "out", "krope": "s_krope", "ckvn": "s_ckvn",
                "knope": "s_knope", "v": "s_v", "o": "s_o", "s1": "s_s1",
                "p1bf": "s_p1bf", "brebf": "s_brebf"}

    phases = [ph_prologue, ph_wkvbfold, ph_ln0, ph_bu, ph_scan, ph_ffn0,
              ph_cm, ph_ln1, ph_q, ph_kv, ph_kvb, ph_attn, ph_wo, ph_ffn1,
              ph_out]
    for ph in phases[:n_phases]:
        c["arena"].reset()
        with nc.Block() as block:
            ph(c, block)

    if dump is not None:
        sem_st = c["sem_st"]
        if dump in ("bu", "hsb"):
            tl = c["slab"] if dump == "bu" else c["Hsb"]
            dt_ = F32 if dump == "bu" else BF16
            ddbg = nc.dram_tensor("dbg", [4 * 128, NTOK], dt_, kind="ExternalOutput")
            with nc.Block() as block:
                @block.sync
                def _(sync):
                    for k in range(4):
                        sem_st.dinc(sync.dma_start(
                            ddbg.ap()[k * 128:(k + 1) * 128, :], tl[k][:]))
                    sync.wait_ge(sem_st.h, sem_st.n)
        else:
            src = c[dump_map[dump]]
            ddbg = nc.dram_tensor("dbg", list(src.shape), src.dtype, kind="ExternalOutput")
            with nc.Block() as block:
                @block.sync
                def _(sync):
                    sem_st.dinc(sync.dma_start(ddbg.ap(), src.ap()))
                    sync.wait_ge(sem_st.h, sem_st.n)
    return nc


def ph_prologue(c, block):
    arena = c["arena"]
    sem_ld2 = c["sem_ld2"]
    sem_dve, sem_act, sem_pe = c["sem_dve"], c["sem_act"], c["sem_pe"]
    gpP = PSem(c, "sem_gp", "sem_gpB")
    stP = PSem(c, "sem_st", "sem_stB")
    psA = c["psA"]

    casts = [(c["Wq"], c["s_wqbf"], D, NH * DQK),
             (c["proj1"], c["s_p1bf"], HFF, D),
             (c["proj2"], c["s_p2bf"], D, HFF),
             (c["base_w"], c["s_bwbf"], D, D),
             (c["B_re"], c["s_brebf"], NST, D),
             (c["B_im"], c["s_bimbf"], NST, D),
             (c["C_re"], c["s_crebf"], D, NST),
             (c["C_im"], c["s_cimbf"], D, NST),
             (c["router_w"], c["s_rwbf"], NE, D)]
    jobs = []
    for (src_, dst, rows, cols) in casts:
        for r0 in range(0, rows, 128):
            rr = min(128, rows - r0)
            jobs.append((src_, dst, r0, rr, cols))
    bufs = [arena.alloc(128, 8192, BF16) for _ in range(2)]
    gpP.pre(["z"] + list(range(len(jobs))))
    stP.pre(range(len(jobs)))

    zpad = arena.alloc(8, D, BF16)

    @block.gpsimd
    def _(gp):
        n_ = c["sem_pool"].einc(gp.memset(zpad[:], 0.0))
        gp.wait_ge(c["sem_pool"].h, n_)
        gpP.issue("z", gp.dma_start(c["s_rwbf"].ap()[8:16, :], zpad[:]))
        for i, (src_, dst, r0, rr, cols) in enumerate(jobs):
            if i >= 2:
                stP.wait(gp, i - 2)
            gpP.issue(i, gp.dma_start(bufs[i % 2][0:rr, 0:cols],
                                      src_.ap()[r0:r0 + rr, :]))

    @block.sync
    def _(sync):
        for nm, src_ in [("k_ones_bf", "ONESBF"), ("k_ones_f", "ONESF"),
                         ("k_id_bf", "IDENTBF"), ("k_id_f", "IDENTF"),
                         ("k_sel2", "SEL2"), ("k_sel2f", "SEL2F")]:
            sem_ld2.dinc(sync.dma_start(c[nm][:], c[src_].ap()))
        for i, (src_, dst, r0, rr, cols) in enumerate(jobs):
            gpP.wait(sync, i)
            stP.issue(i, sync.dma_start(dst.ap()[r0:r0 + rr, :],
                                        bufs[i % 2][0:rr, 0:cols]))
        sync.wait_ge(stP.s[0].h, stP.s[0].n)
        sync.wait_ge(stP.s[1].h, stP.s[1].n)

    nu2 = arena.alloc(2, 128)
    th2 = arena.alloc(2, 128)
    b1n = arena.alloc(64, 128)
    kvn4 = arena.alloc(4, 128)
    b2f = arena.alloc(1, D)
    bbf_ = arena.alloc(1, D)
    rbf_ = arena.alloc(1, NE)
    tmp = arena.alloc(128, 2)
    tmp2 = arena.alloc(128, 2)
    tmp3 = arena.alloc(128, 2)
    dv0, a0, pe0 = sem_dve.n, sem_act.n, sem_pe.n

    @block.scalar
    def _(sc):
        sem_ld2.dinc(sc.dma_start(nu2[:], c["nu_log"].ap().rearrange("(a b) -> a b", b=128)))
        sem_ld2.dinc(sc.dma_start(th2[:], c["theta_log"].ap().rearrange("(a b) -> a b", b=128)))
        sem_ld2.dinc(sc.dma_start(b1n[:], c["proj1_bias"].ap().rearrange("(a b) -> a b", b=128)))
        sem_ld2.dinc(sc.dma_start(kvn4[:], c["kv_norm_w"].ap().rearrange("(a b) -> a b", b=128)))
        sem_ld2.dinc(sc.dma_start(b2f[:], c["proj2_bias"].ap().rearrange("(o a) -> o a", o=1)))
        sem_ld2.dinc(sc.dma_start(bbf_[:], c["base_b"].ap().rearrange("(o a) -> o a", o=1)))
        sem_ld2.dinc(sc.dma_start(rbf_[:], c["router_b"].ap().rearrange("(o a) -> o a", o=1)))
        sem_ld2.dinc(sc.dma_start(c["drow"][:], c["Dvec"].ap().rearrange("(o a) -> o a", o=1)))

    LD2_ALL = sem_ld2.n

    @block.tensor
    def _(te):
        te.wait_ge(sem_ld2.h, LD2_ALL)
        sem_pe.einc(te.transpose(psA[:, 0:2], nu2[:], c["k_id_f"][0:2, 0:2]))
        sem_pe.einc(te.transpose(psA[:, 8:10], th2[:], c["k_id_f"][0:2, 0:2]))
        sem_pe.einc(te.transpose(psA[:, 16:80], b1n[:], c["k_id_f"][0:64, 0:64]))
        sem_pe.einc(te.transpose(psA[:, 96:100], kvn4[:], c["k_id_f"][0:4, 0:4]))

    @block.vector
    def _(v):
        v.wait_ge(sem_pe.h, pe0 + 4)
        v.tensor_copy(c["b1_sb"][:], psA[:, 16:80])
        v.tensor_copy(c["kvnw_sb"][:], psA[:, 96:100])
        v.tensor_copy(c["b2row"][:], b2f[:])
        v.tensor_copy(c["bbrow"][:], bbf_[:])
        v.tensor_copy(c["rbrow"][:], rbf_[:])
        v.tensor_copy(tmp[:], psA[:, 0:2])
        sem_dve.einc(v.tensor_copy(tmp3[:], psA[:, 8:10]))

    sem_ss = c["sem_sself"]

    def ssw(sc, inst):
        n = sem_ss.einc(inst)
        sc.wait_ge(sem_ss.h, n)

    @block.scalar
    def _(sc):
        sc.wait_ge(sem_dve.h, dv0 + 1)
        ssw(sc, sc.activation(tmp2[:], tmp[:], AF.Abs))
        ssw(sc, sc.activation(tmp2[:], tmp2[:], AF.Exp, scale=np.float32(-1.0)))
        sem_act.einc(sc.activation(tmp2[:], tmp2[:], AF.Ln, bias=np.float32(1.0)))

    @block.vector
    def _(v):
        v.wait_ge(sem_act.h, a0 + 1)
        n_ = c["sem_vself"].einc(v.tensor_scalar(tmp[:], tmp[:], 0.0, None,
                                                 OP.max, OP.bypass))
        v.wait_ge(c["sem_vself"].h, n_)
        sem_dve.einc(v.tensor_tensor(tmp[:], tmp[:], tmp2[:], OP.add))

    @block.scalar
    def _(sc):
        sc.wait_ge(sem_dve.h, dv0 + 2)
        ssw(sc, sc.activation(tmp[:], tmp[:], AF.Exp))
        ssw(sc, sc.activation(c["v_r"][:], tmp[:], AF.Exp, scale=np.float32(-1.0)))
        ssw(sc, sc.activation(c["v_th"][:], tmp3[:], AF.Exp))
        sem_act.einc(sc.activation(tmp2[:], c["v_r"][:], AF.Square))

    @block.vector
    def _(v):
        v.wait_ge(sem_act.h, a0 + 2)
        n_ = c["sem_vself"].einc(v.tensor_scalar(tmp2[:], tmp2[:], -1.0, 1.0,
                                                 OP.mult, OP.add))
        v.wait_ge(c["sem_vself"].h, n_)
        sem_dve.einc(v.tensor_scalar(tmp2[:], tmp2[:], 1e-30, None, OP.max, OP.bypass))

    @block.scalar
    def _(sc):
        sc.wait_ge(sem_dve.h, dv0 + 3)
        sem_act.einc(sc.activation(c["v_gam"][:], tmp2[:], AF.Sqrt))


def _ln_sweep(c, block, src, dst, st_m, st_r):
    arena = c["arena"]
    sem_dve, sem_dve2 = c["sem_dve"], c["sem_dve2"]
    sem_act, sem_pool = c["sem_act"], c["sem_pool"]
    sem_vself = c["sem_vself"]
    ldP = PSem(c, "sem_ld", "sem_ldB")
    stP = PSem(c, "sem_st", "sem_stB")
    xb = [arena.alloc(128, D) for _ in range(2)]
    xnb = [arena.alloc(128, D, BF16) for _ in range(2)]
    sqscr = [arena.alloc(128, D) for _ in range(2)]
    sums = arena.alloc(128, 1)
    sqs = arena.alloc(128, 1)
    mm_ = arena.alloc(128, 1)
    vtmp = arena.alloc(128, NSLICE)
    dv0, dv20, a0, p0 = sem_dve.n, sem_dve2.n, sem_act.n, sem_pool.n
    ldP.pre(range(NSLICE))
    stP.pre(range(NSLICE))

    @block.sync
    def _(sync):
        for i in range(NSLICE + 1):
            if i < NSLICE:
                if i >= 2:
                    sync.wait_ge(sem_dve2.h, dv20 + (i - 1))
                    sync.wait_ge(sem_act.h, a0 + (i - 1))
                ldP.issue(i, sync.dma_start(xb[i % 2][:],
                                            src.ap()[i * TS:(i + 1) * TS, :]))
            if i >= 1:
                sync.wait_ge(sem_dve2.h, dv20 + i)
                stP.issue(i - 1, sync.dma_start(dst.ap()[(i - 1) * TS:i * TS, :],
                                                xnb[(i - 1) % 2][:]))
        sync.wait_ge(stP.s[0].h, stP.s[0].n)
        sync.wait_ge(stP.s[1].h, stP.s[1].n)

    @block.scalar
    def _(sc):
        for i in range(NSLICE):
            ldP.wait(sc, i)
            if i >= 2:
                sc.wait_ge(sem_dve.h, dv0 + (i - 1))
            sem_act.einc(sc.activation(sqscr[i % 2][:], xb[i % 2][:], AF.Square))
            if i >= 1:
                sc.wait_ge(sem_dve.h, dv0 + i)
                n_ = c["sem_sself"].einc(sc.activation(vtmp[:, i - 1:i],
                                                       vtmp[:, i - 1:i], AF.Ln))
                sc.wait_ge(c["sem_sself"].h, n_)
                sem_pool.einc(sc.activation(st_r[:, i - 1:i], vtmp[:, i - 1:i],
                                            AF.Exp, scale=np.float32(-0.5)))
        sc.wait_ge(sem_dve.h, dv0 + NSLICE)
        n_ = c["sem_sself"].einc(sc.activation(vtmp[:, NSLICE - 1:NSLICE],
                                               vtmp[:, NSLICE - 1:NSLICE], AF.Ln))
        sc.wait_ge(c["sem_sself"].h, n_)
        sem_pool.einc(sc.activation(st_r[:, NSLICE - 1:NSLICE],
                                    vtmp[:, NSLICE - 1:NSLICE],
                                    AF.Exp, scale=np.float32(-0.5)))

    def vsw(v, inst):
        n = sem_vself.einc(inst)
        v.wait_ge(sem_vself.h, n)

    def xn_emit(v, i):
        v.wait_ge(sem_pool.h, p0 + i + 1)
        if i >= 2:
            stP.wait(v, i - 2)
        sem_dve2.einc(v.tensor_scalar(xnb[i % 2][:], xb[i % 2][:],
                                      st_m[:, i:i + 1], st_r[:, i:i + 1],
                                      OP.subtract, OP.mult))

    @block.vector
    def _(v):
        for i in range(NSLICE):
            ldP.wait(v, i)
            vsw(v, v.tensor_reduce(sums[:], xb[i % 2][:], AX.X, OP.add))
            if i >= 1:
                xn_emit(v, i - 1)
            v.wait_ge(sem_act.h, a0 + i + 1)
            vsw(v, v.tensor_reduce(sqs[:], sqscr[i % 2][:], AX.X, OP.add))
            vsw(v, v.tensor_scalar(st_m[:, i:i + 1], sums[:], 1.0 / D, None,
                                   OP.mult, OP.bypass))
            vsw(v, v.tensor_tensor(mm_[:], st_m[:, i:i + 1], st_m[:, i:i + 1], OP.mult))
            vsw(v, v.tensor_scalar(vtmp[:, i:i + 1], sqs[:], 1.0 / D, None,
                                   OP.mult, OP.bypass))
            sem_dve.einc(v.tensor_scalar(vtmp[:, i:i + 1],
                                         vtmp[:, i:i + 1], mm_[:], -float(EPS_LN),
                                         OP.subtract, OP.subtract))
        xn_emit(v, NSLICE - 1)


def ph_ln0(c, block):
    _ln_sweep(c, block, c["x"], c["s_xn0"], c["st_m0"], c["st_r0"])


def ph_bu(c, block):
    arena = c["arena"]
    sem_ld2, sem_pe, sem_dve = c["sem_ld2"], c["sem_pe"], c["sem_dve"]
    ldP = PSem(c, "sem_ld", "sem_ldB")
    psq = [c["psA"], c["psB"]]
    btre = arena.alloc(128, 16 * NST, BF16)
    btim = arena.alloc(128, 16 * NST, BF16)
    xf = [arena.alloc(128, 16 * 512, BF16) for _ in range(2)]
    pe0, dv0 = sem_pe.n, sem_dve.n
    cnt = [ldP.s[0].n, ldP.s[1].n]
    for s in range(NSUP):
        cnt[s % 2] += 16 * 16
        ldP.tgt[s] = (s % 2, cnt[s % 2])

    @block.sync
    def _(sync):
        for ci in range(16):
            sem_ld2.dinc(sync.dma_start(btre[:, ci * NST:(ci + 1) * NST],
                                        c["s_brebf"].ap()[:, ci * 128:(ci + 1) * 128],
                                        transpose=True))
            sem_ld2.dinc(sync.dma_start(btim[:, ci * NST:(ci + 1) * NST],
                                        c["s_bimbf"].ap()[:, ci * 128:(ci + 1) * 128],
                                        transpose=True))
        for s in range(NSUP):
            if s >= 2:
                sync.wait_ge(sem_pe.h, pe0 + 4 * (s - 1))
            for ci in range(16):
                ldP.issue(s, sync.dma_start(
                    xf[s % 2][:, ci * 512:(ci + 1) * 512],
                    c["s_xn0"].ap()[s * SUP:(s + 1) * SUP, ci * 128:(ci + 1) * 128],
                    transpose=True))

    LD2_ALL = sem_ld2.n

    @block.tensor
    def _(te):
        te.wait_ge(sem_ld2.h, LD2_ALL)
        for s in range(NSUP):
            ldP.wait(te, s)
            if s >= 2:
                te.wait_ge(sem_dve.h, dv0 + (s - 1))
            ps = psq[s % 2]
            for q, (nch, bt) in enumerate([(0, btre), (0, btim), (1, btre), (1, btim)]):
                for ci in range(16):
                    inst = te.matmul(ps[:, q * 512:(q + 1) * 512],
                                     bt[:, ci * NST + nch * 128: ci * NST + nch * 128 + 128],
                                     xf[s % 2][:, ci * 512:(ci + 1) * 512],
                                     start=(ci == 0), stop=(ci == 15))
                sem_pe.einc(inst)

    @block.vector
    def _(v):
        slab = c["slab"]
        for s in range(NSUP):
            v.wait_ge(sem_pe.h, pe0 + 4 * (s + 1))
            ps = psq[s % 2]
            for q, (nch, comp) in enumerate([(0, 0), (0, 1), (1, 0), (1, 1)]):
                k = 2 * nch + comp
                inst = v.tensor_scalar(slab[k][:, s * SUP:(s + 1) * SUP],
                                       ps[:, q * 512:(q + 1) * 512],
                                       c["v_gam"][:, nch:nch + 1], None,
                                       OP.mult, OP.bypass)
            sem_dve.einc(inst)


def ph_scan(c, block):
    arena = c["arena"]
    sem_ld2, sem_dve, sem_act = c["sem_ld2"], c["sem_dve"], c["sem_act"]
    trow = arena.alloc(128, T)
    COS = arena.alloc(128, T)
    SIN = arena.alloc(128, T)
    PH = arena.alloc(128, T)
    PHR = arena.alloc(128, T)
    RT = arena.alloc(128, T)
    CRE = arena.alloc(128, T)
    CIM = arena.alloc(128, T)
    TMP = arena.alloc(128, T)
    dv0, a0 = sem_dve.n, sem_act.n
    slab, Hsb = c["slab"], c["Hsb"]

    @block.sync
    def _(sync):
        sem_ld2.dinc(sync.dma_start(trow[:], c["TROW"].ap()))
    LD2_ALL = sem_ld2.n

    def table(v, V, base_shift):
        V(v.tensor_scalar(PHR[:], PH[:], float(base_shift), None, OP.add, OP.bypass))
        V(v.tensor_scalar(TMP[:], PHR[:], 1.0 / TWO_PI, MAGIC, OP.mult, OP.add))
        V(v.tensor_scalar(TMP[:], TMP[:], MAGIC, None, OP.subtract, OP.bypass))
        V(v.scalar_tensor_tensor(PHR[:], TMP[:], -TWO_PI, PHR[:], OP.mult, OP.add))
        V(v.tensor_scalar(PHR[:], PHR[:], float(math.pi), None, OP.min, OP.bypass))
        return v.tensor_scalar(PHR[:], PHR[:], -float(math.pi), None, OP.max, OP.bypass)

    for nch in range(2):
        @block.vector
        def _(v, nch=nch):
            V = mk_chain(v, c["sem_vself"])
            if nch == 0:
                v.wait_ge(sem_ld2.h, LD2_ALL)
            else:
                v.wait_ge(sem_act.h, a0 + 2)
            V(v.tensor_scalar(PH[:], trow[:], c["v_th"][:, nch:nch + 1], None,
                              OP.mult, OP.bypass))
            sem_dve.einc(table(v, V, 0.0))
            v.wait_ge(sem_act.h, a0 + 2 * nch + 1)
            sem_dve.einc(table(v, V, math.pi / 2.0))
            V(v.tensor_scalar(RT[:], trow[:], 0.0, c["v_r"][:, nch:nch + 1],
                              OP.mult, OP.add))
            v.wait_ge(sem_act.h, a0 + 2 * nch + 2)
            for seq in range(SEQS):
                cols = slice(seq * T, (seq + 1) * T)
                bu_re = slab[2 * nch + 0][:, cols]
                bu_im = slab[2 * nch + 1][:, cols]
                V(v.tensor_tensor(CRE[:], COS[:], bu_re, OP.mult))
                V(v.tensor_tensor(TMP[:], SIN[:], bu_im, OP.mult))
                V(v.tensor_tensor(CRE[:], CRE[:], TMP[:], OP.add))
                V(v.tensor_tensor(CIM[:], COS[:], bu_im, OP.mult))
                V(v.tensor_tensor(TMP[:], SIN[:], bu_re, OP.mult))
                V(v.tensor_tensor(CIM[:], CIM[:], TMP[:], OP.subtract))
                V(v.tensor_tensor_scan(bu_re, RT[:], CRE[:], 0.0, OP.mult, OP.add))
                V(v.tensor_tensor_scan(bu_im, RT[:], CIM[:], 0.0, OP.mult, OP.add))
                g_re, g_im = bu_re, bu_im
                V(v.tensor_tensor(CRE[:], COS[:], g_re, OP.mult))
                V(v.tensor_tensor(TMP[:], SIN[:], g_im, OP.mult))
                V(v.tensor_tensor(Hsb[2 * nch + 0][:, cols], CRE[:], TMP[:], OP.subtract))
                V(v.tensor_tensor(CIM[:], COS[:], g_im, OP.mult))
                V(v.tensor_tensor(TMP[:], SIN[:], g_re, OP.mult))
                V(v.scalar_tensor_tensor(Hsb[2 * nch + 1][:, cols], CIM[:], -1.0,
                                         TMP[:], OP.mult, OP.subtract))

        @block.scalar
        def _(sc, nch=nch):
            sc.wait_ge(sem_dve.h, dv0 + 2 * nch + 1)
            sem_act.einc(sc.activation(SIN[:], PHR[:], AF.Sin))
            sc.wait_ge(sem_dve.h, dv0 + 2 * nch + 2)
            sem_act.einc(sc.activation(COS[:], PHR[:], AF.Sin))


def ph_ffn0(c, block):
    arena = c["arena"]
    sem_pe, sem_act = c["sem_pe"], c["sem_act"]
    ldP = PSem(c, "sem_ld", "sem_ldB")
    p1P = PSem(c, "sem_p1a", "sem_p1b")
    p2P = PSem(c, "sem_p2a", "sem_p2b")
    stP = PSem(c, "sem_st", "sem_stB")
    slab = c["slab"]
    hsl = [t[:].bitcast(BF16) for t in slab]

    def hch(hc):
        return hsl[hc // 16][:, (hc % 16) * 512:(hc % 16 + 1) * 512]

    xnb = [arena.alloc(128, 16 * 512, BF16) for _ in range(2)]
    p1b = [arena.alloc(128, 16 * 512, BF16) for _ in range(2)]
    p2b = [arena.alloc(128, 2048, BF16) for _ in range(2)]
    gb = [arena.alloc(128, 2048) for _ in range(2)]
    psA, psB = c["psA"], c["psB"]
    p0, a0 = sem_pe.n, sem_act.n
    PE_S = 194
    ACT_S = 68
    ldP.pre([s for s in range(NSUP) for _ in range(16)])
    p1P.pre([g for g in range(NSUP * 16) for _ in range(16)])
    p2P.pre(range(NSUP * 2 * 64))
    stP.pre(range(NSUP * 4))

    def pe_t(s, k):
        return p0 + s * PE_S + k

    def act_t(s, k):
        return a0 + s * ACT_S + k

    @block.sync
    def _(sync):
        for s in range(NSUP):
            if s >= 2:
                sync.wait_ge(sem_pe.h, pe_t(s - 2, 64))
            for dc in range(16):
                ldP.issue(s, sync.dma_start(
                    xnb[s % 2][:, dc * 512:(dc + 1) * 512],
                    c["s_xn0"].ap()[s * SUP:(s + 1) * SUP, dc * 128:(dc + 1) * 128],
                    transpose=True))
            for hg in range(16):
                g = s * 16 + hg
                if g >= 2:
                    sync.wait_ge(sem_pe.h, pe_t((g - 2) // 16, 4 * ((g - 2) % 16) + 4))
                for dc in range(16):
                    p1P.issue(g, sync.dma_start(
                        p1b[g % 2][:, dc * 512:(dc + 1) * 512],
                        c["s_p1bf"].ap()[hg * 512:(hg + 1) * 512, dc * 128:(dc + 1) * 128],
                        transpose=True))
            for pair in range(2):
                for hc in range(64):
                    g2 = (s * 2 + pair) * 64 + hc
                    if g2 >= 2:
                        gp = g2 - 2
                        s_, r_ = gp // 128, gp % 128
                        pair_, hc_ = r_ // 64, r_ % 64
                        sync.wait_ge(sem_pe.h,
                                     pe_t(s_, 64 + pair_ * 65 + hc_ + 1))
                    p2P.issue(g2, sync.dma_start(
                        p2b[g2 % 2][:],
                        c["s_p2bf"].ap()[:, hc * 128:(hc + 1) * 128],
                        transpose=True))
            for j in range(4):
                sj = s * 4 + j
                sync.wait_ge(sem_act.h, act_t(s, 64 + j + 1))
                stP.issue(sj, sync.dma_start(
                    c["s_g0"].ap()[s * SUP + j * TS: s * SUP + (j + 1) * TS, :],
                    gb[j % 2][:]))
        sync.wait_ge(stP.s[0].h, stP.s[0].n)
        sync.wait_ge(stP.s[1].h, stP.s[1].n)

    @block.tensor
    def _(te):
        for s in range(NSUP):
            ldP.wait(te, s)
            if s >= 1:
                te.wait_ge(sem_act.h, act_t(s - 1, 68))
            for hg in range(16):
                g = s * 16 + hg
                p1P.wait(te, g)
                for k in range(4):
                    hc = hg * 4 + k
                    if hc >= 2:
                        te.wait_ge(sem_act.h, act_t(s, hc - 1))
                    for dc in range(16):
                        inst = te.matmul(psA[:, (hc % 2) * 512:(hc % 2 + 1) * 512],
                                         p1b[g % 2][:, dc * 512 + k * 128: dc * 512 + k * 128 + 128],
                                         xnb[s % 2][:, dc * 512:(dc + 1) * 512],
                                         start=(dc == 0), stop=(dc == 15))
                    sem_pe.einc(inst)
            te.wait_ge(sem_act.h, act_t(s, 64))
            for pair in range(2):
                ps_j = [psB, psA]
                if pair >= 1:
                    te.wait_ge(sem_act.h, act_t(s, 64 + 2 * pair))
                for hc in range(64):
                    g2 = (s * 2 + pair) * 64 + hc
                    p2P.wait(te, g2)
                    for jj in range(2):
                        j = pair * 2 + jj
                        for bs in range(4):
                            inst = te.matmul(
                                ps_j[jj][:, bs * 512:(bs + 1) * 512],
                                hch(hc)[:, j * 128:(j + 1) * 128],
                                p2b[g2 % 2][:, bs * 512:(bs + 1) * 512],
                                start=(hc == 0), stop=False)
                    sem_pe.einc(inst)
                for jj in range(2):
                    for bs in range(4):
                        inst = te.matmul(ps_j[jj][:, bs * 512:(bs + 1) * 512],
                                         c["k_ones_bf"][0:1, 0:128],
                                         c["b2row"][0:1, bs * 512:(bs + 1) * 512],
                                         start=False, stop=True)
                sem_pe.einc(inst)


    @block.scalar
    def _(sc):
        for s in range(NSUP):
            for hc in range(64):
                sc.wait_ge(sem_pe.h, pe_t(s, hc + 1))
                sem_act.einc(sc.activation(hch(hc), psA[:, (hc % 2) * 512:(hc % 2 + 1) * 512],
                                           AF.Gelu, bias=c["b1_sb"][:, hc:hc + 1]))
            for pair in range(2):
                sc.wait_ge(sem_pe.h, pe_t(s, 64 + (pair + 1) * 65))
                for jj in range(2):
                    j = pair * 2 + jj
                    sj = s * 4 + j
                    if sj >= 2:
                        stP.wait(sc, sj - 2)
                    sem_act.einc(sc.activation(gb[j % 2][:],
                                               [psB, psA][jj][:],
                                               AF.Silu))


def ph_cm(c, block):
    arena = c["arena"]
    sem_ld2, sem_pe, sem_dve = c["sem_ld2"], c["sem_pe"], c["sem_dve"]
    sem_vself = c["sem_vself"]
    ldP = PSem(c, "sem_ld", "sem_ldB")
    stP = PSem(c, "sem_st", "sem_stB")
    Hsb, psA, psB = c["Hsb"], c["psA"], c["psB"]
    ct = arena.alloc(128, 4 * D, BF16)
    xb = [arena.alloc(128, D) for _ in range(2)]
    gbuf = [arena.alloc(128, D) for _ in range(2)]
    x1b = [arena.alloc(128, D) for _ in range(2)]
    XN = arena.alloc(128, D)
    T1 = arena.alloc(128, D)
    dbc = arena.alloc(128, D)
    pe0, dv0 = sem_pe.n, sem_dve.n
    ldP.pre([i for i in range(NSLICE) for _ in range(2)])
    stP.pre(range(NSLICE))

    @block.sync
    def _(sync):
        for nch in range(2):
            for comp, srcn in enumerate(["s_crebf", "s_cimbf"]):
                k = 2 * nch + comp
                sem_ld2.dinc(sync.dma_start(
                    ct[:, k * D:(k + 1) * D],
                    c[srcn].ap()[:, nch * 128:(nch + 1) * 128],
                    transpose=True))
        for i in range(NSLICE):
            if i >= 2:
                sync.wait_ge(sem_dve.h, dv0 + i)
            ldP.issue(i, sync.dma_start(xb[i % 2][:], c["x"].ap()[i * TS:(i + 1) * TS, :]))
            ldP.issue(i, sync.dma_start(gbuf[i % 2][:], c["s_g0"].ap()[i * TS:(i + 1) * TS, :]))
            if i >= 1:
                sync.wait_ge(sem_dve.h, dv0 + i + 1)
                stP.issue(i - 1, sync.dma_start(
                    c["s_x1"].ap()[(i - 1) * TS:i * TS, :], x1b[(i - 1) % 2][:]))
        sync.wait_ge(sem_dve.h, dv0 + NSLICE + 1)
        stP.issue(NSLICE - 1, sync.dma_start(
            c["s_x1"].ap()[(NSLICE - 1) * TS:NSLICE * TS, :], x1b[(NSLICE - 1) % 2][:]))
        sync.wait_ge(stP.s[0].h, stP.s[0].n)
        sync.wait_ge(stP.s[1].h, stP.s[1].n)

    LD2_ALL = sem_ld2.n

    @block.tensor
    def _(te):
        te.wait_ge(sem_ld2.h, LD2_ALL)
        for bs in range(4):
            inst = te.matmul(psA[:, bs * 512:(bs + 1) * 512],
                             c["k_ones_f"][0:1, :],
                             c["drow"][0:1, bs * 512:(bs + 1) * 512],
                             start=True, stop=True)
        sem_pe.einc(inst)
        for i in range(NSLICE):
            if i == 0:
                te.wait_ge(sem_dve.h, dv0 + 1)
            if i >= 2:
                te.wait_ge(sem_dve.h, dv0 + i)
            ps = [psA, psB][i % 2]
            for k in range(4):
                for bs in range(4):
                    inst = te.matmul(ps[:, bs * 512:(bs + 1) * 512],
                                     Hsb[k][:, i * TS:(i + 1) * TS],
                                     ct[:, k * D + bs * 512: k * D + (bs + 1) * 512],
                                     start=(k == 0), stop=(k == 3))
            sem_pe.einc(inst)

    @block.vector
    def _(v):
        V = mk_chain(v, sem_vself)
        v.wait_ge(sem_pe.h, pe0 + 1)
        sem_dve.einc(v.tensor_copy(dbc[:], psA[:]))
        for i in range(NSLICE):
            ldP.wait(v, i)
            v.wait_ge(sem_pe.h, pe0 + i + 2)
            if i >= 2:
                stP.wait(v, i - 2)
            ps = [psA, psB][i % 2]
            V(v.tensor_scalar(XN[:], xb[i % 2][:], c["st_m0"][:, i:i + 1],
                              c["st_r0"][:, i:i + 1], OP.subtract, OP.mult))
            V(v.tensor_tensor(T1[:], XN[:], dbc[:], OP.mult))
            V(v.tensor_tensor(T1[:], T1[:], ps[:], OP.add))
            V(v.tensor_tensor(T1[:], T1[:], gbuf[i % 2][:], OP.mult))
            sem_dve.einc(v.tensor_tensor(x1b[i % 2][:], T1[:], xb[i % 2][:], OP.add))


def ph_ln1(c, block):
    _ln_sweep(c, block, c["s_x1"], c["s_xn1"], c["st_m1"], c["st_r1"])


def ph_wkvbfold(c, block):
    arena = c["arena"]
    sem_dve = c["sem_dve"]
    ldP = PSem(c, "sem_ld", "sem_ldB")
    stP = PSem(c, "sem_st", "sem_stB")
    wb = [arena.alloc(128, NH * (DNOPE + DV)) for _ in range(2)]
    ob = [arena.alloc(128, NH * (DNOPE + DV), BF16) for _ in range(2)]
    dv0 = sem_dve.n
    ldP.pre(range(4))
    stP.pre(range(4))

    @block.sync
    def _(sync):
        for lc in range(4):
            if lc >= 2:
                sync.wait_ge(sem_dve.h, dv0 + lc - 1)
            ldP.issue(lc, sync.dma_start(wb[lc % 2][:],
                                         c["Wkv_b"].ap()[lc * 128:(lc + 1) * 128, :]))
            if lc >= 1:
                sync.wait_ge(sem_dve.h, dv0 + lc)
                stP.issue(lc - 1, sync.dma_start(
                    c["s_wkvbbf"].ap()[(lc - 1) * 128:lc * 128, :], ob[(lc - 1) % 2][:]))
        sync.wait_ge(sem_dve.h, dv0 + 4)
        stP.issue(3, sync.dma_start(c["s_wkvbbf"].ap()[3 * 128:4 * 128, :], ob[1][:]))
        sync.wait_ge(stP.s[0].h, stP.s[0].n)
        sync.wait_ge(stP.s[1].h, stP.s[1].n)

    @block.vector
    def _(v):
        for lc in range(4):
            ldP.wait(v, lc)
            if lc >= 2:
                stP.wait(v, lc - 2)
            sem_dve.einc(v.tensor_scalar(ob[lc % 2][:], wb[lc % 2][:],
                                         c["kvnw_sb"][:, lc:lc + 1], None,
                                         OP.mult, OP.bypass))


def ph_q(c, block):
    arena = c["arena"]
    sem_pe, sem_act, sem_dve2 = c["sem_pe"], c["sem_act"], c["sem_dve2"]
    sem_vself = c["sem_vself"]
    ldP = PSem(c, "sem_ld", "sem_ldB")
    wqP = PSem(c, "sem_p1a", "sem_p1b")
    tbP = PSem(c, "sem_p2a", "sem_p2b")
    stP = PSem(c, "sem_st", "sem_stB")
    st2P = PSem(c, "sem_st2a", "sem_st2b")
    psA, psB = c["psA"], c["psB"]
    xf = [arena.alloc(128, 16 * 512, BF16) for _ in range(2)]
    wqn = [arena.alloc(128, 16 * 128, BF16) for _ in range(2)]
    hb16 = [t[:].bitcast(BF16) for t in c["Hsb"]]

    def wqr_dc(dc):
        return hb16[dc // 4][:, (dc % 4) * 1024:(dc % 4 + 1) * 1024]
    qnb = [arena.alloc(128, 512, BF16) for _ in range(2)]
    qrb = [arena.alloc(128, 2048, BF16) for _ in range(2)]
    tabc = [arena.alloc(128, 512) for _ in range(2)]
    tabs = [arena.alloc(128, 512) for _ in range(2)]
    rt1 = arena.alloc(128, 512)
    rt2 = arena.alloc(128, 512)
    pe0, a0, dv20 = sem_pe.n, sem_act.n, sem_dve2.n
    ldP.pre([s for s in range(NSUP) for _ in range(16)])
    wqP.pre([g for g in range(NSUP * 16) for _ in range(16)])
    tbP.pre([i for i in range(NSLICE) for _ in range(2)])
    stP.pre(range(NSUP * 16))
    st2P.pre(range(NSLICE))
    PE_S = 20
    ACT_S = 16

    @block.sync
    def _(sync):
        for dc in range(16):
            for h in range(16):
                sem_ld2 = c["sem_ld2"]
                sem_ld2.dinc(sync.dma_start(
                    wqr_dc(dc)[:, h * 64:(h + 1) * 64],
                    c["s_wqbf"].ap()[dc * 128:(dc + 1) * 128,
                                     h * DQK + DNOPE: h * DQK + DQK]))
        for s in range(NSUP):
            if s >= 2:
                sync.wait_ge(sem_pe.h, pe0 + PE_S * (s - 1))
            for dc in range(16):
                ldP.issue(s, sync.dma_start(
                    xf[s % 2][:, dc * 512:(dc + 1) * 512],
                    c["s_xn1"].ap()[s * SUP:(s + 1) * SUP, dc * 128:(dc + 1) * 128],
                    transpose=True))
            def qn_store(h):
                k = s * 16 + h
                sync.wait_ge(sem_act.h, a0 + ACT_S * s + h + 1)
                stP.issue(k, sync.dma_start(
                    c["s_qnope"].ap()[h * 128:(h + 1) * 128, s * SUP:(s + 1) * SUP],
                    qnb[h % 2][:]))

            for h in range(16):
                g = s * 16 + h
                if h >= 2:
                    qn_store(h - 2)
                if g >= 2:
                    sync.wait_ge(sem_pe.h, pe0 + PE_S * ((g - 2) // 16) + ((g - 2) % 16) + 1)
                for dc in range(16):
                    wqP.issue(g, sync.dma_start(
                        wqn[g % 2][:, dc * 128:(dc + 1) * 128],
                        c["s_wqbf"].ap()[dc * 128:(dc + 1) * 128,
                                         h * DQK: h * DQK + DNOPE]))
            qn_store(14)
            qn_store(15)
            for j in range(4):
                i = s * 4 + j
                if i >= 2:
                    sync.wait_ge(sem_dve2.h, dv20 + (i - 1))
                pos0 = (i % 16) * 128
                tbP.issue(i, sync.dma_start(tabc[i % 2][:],
                                            c["QROPEC"].ap()[pos0:pos0 + 128, :]))
                tbP.issue(i, sync.dma_start(tabs[i % 2][:],
                                            c["QROPES"].ap()[pos0:pos0 + 128, :]))
            for j in range(4):
                i = s * 4 + j
                sync.wait_ge(sem_dve2.h, dv20 + i + 1)
                st2P.issue(i, sync.dma_start(
                    c["s_qrope"].ap()[i * TS:(i + 1) * TS, :], qrb[j % 2][:]))
        sync.wait_ge(stP.s[0].h, stP.s[0].n)
        sync.wait_ge(stP.s[1].h, stP.s[1].n)
        sync.wait_ge(st2P.s[0].h, st2P.s[0].n)
        sync.wait_ge(st2P.s[1].h, st2P.s[1].n)

    LD2_ALL = c["sem_ld2"].n

    @block.tensor
    def _(te):
        te.wait_ge(c["sem_ld2"].h, LD2_ALL)
        for s in range(NSUP):
            ldP.wait(te, s)
            for h in range(16):
                g = s * 16 + h
                wqP.wait(te, g)
                reg = [psA[:, 0:512], psA[:, 512:1024]][h % 2]
                if g >= 2:
                    te.wait_ge(sem_act.h, a0 + ACT_S * ((g - 2) // 16) + ((g - 2) % 16) + 1)
                for dc in range(16):
                    inst = te.matmul(reg,
                                     wqn[g % 2][:, dc * 128:(dc + 1) * 128],
                                     xf[s % 2][:, dc * 512:(dc + 1) * 512],
                                     start=(dc == 0), stop=(dc == 15))
                sem_pe.einc(inst)
            for j in range(4):
                i = s * 4 + j
                reg = [psB[:, 0:1024], psB[:, 1024:2048]][j % 2]
                if i >= 2:
                    te.wait_ge(sem_dve2.h, dv20 + (i - 1))
                for half in range(2):
                    for dc in range(16):
                        inst = te.matmul(
                            reg[:, half * 512:(half + 1) * 512],
                            xf[s % 2][:, dc * 512 + j * 128: dc * 512 + j * 128 + 128],
                            wqr_dc(dc)[:, half * 512:(half + 1) * 512],
                            start=(dc == 0), stop=(dc == 15))
                sem_pe.einc(inst)

    @block.scalar
    def _(sc):
        for s in range(NSUP):
            for h in range(16):
                g = s * 16 + h
                k = s * 16 + h
                sc.wait_ge(sem_pe.h, pe0 + PE_S * s + h + 1)
                if g >= 2:
                    kp = k - 2 if h >= 2 else (s - 1) * 16 + 14 + h
                    stP.wait(sc, kp)
                reg = [psA[:, 0:512], psA[:, 512:1024]][h % 2]
                sem_act.einc(sc.activation(qnb[h % 2][:], reg, AF.Copy,
                                           bias=0.0, scale=np.float32(SCALE_QK)))

    @block.vector
    def _(v):
        V = mk_chain(v, sem_vself)
        V(v.memset(qrb[0][:], 0.0))
        V(v.memset(qrb[1][:], 0.0))
        for s in range(NSUP):
            for j in range(4):
                i = s * 4 + j
                v.wait_ge(sem_pe.h, pe0 + PE_S * s + 16 + j + 1)
                tbP.wait(v, i)
                if i >= 2:
                    st2P.wait(v, i - 2)
                reg = [psB[:, 0:1024], psB[:, 1024:2048]][j % 2]
                rr = reg.rearrange("p (h q) -> p h q", q=64)
                x1 = rr.rearrange("p h (j t) -> p h j t", t=2)[:, :, :, 0:1].squeeze()
                x2 = rr.rearrange("p h (j t) -> p h j t", t=2)[:, :, :, 1:2].squeeze()
                ctab = tabc[i % 2][:].rearrange("p (h j) -> p h j", j=32)
                stab = tabs[i % 2][:].rearrange("p (h j) -> p h j", j=32)
                r1v = rt1[:].rearrange("p (h j) -> p h j", j=32)
                r2v = rt2[:].rearrange("p (h j) -> p h j", j=32)
                ov = qrb[j % 2][:].rearrange("p (h q) -> p h q", q=128)[:, :, 0:64]
                o1 = ov.rearrange("p h (j t) -> p h j t", t=2)[:, :, :, 0:1].squeeze()
                o2 = ov.rearrange("p h (j t) -> p h j t", t=2)[:, :, :, 1:2].squeeze()
                V(v.tensor_tensor(r1v, x1, ctab, OP.mult))
                V(v.tensor_tensor(r2v, x2, stab, OP.mult))
                V(v.tensor_tensor(o1, r1v, r2v, OP.subtract))
                V(v.tensor_tensor(r1v, x1, stab, OP.mult))
                V(v.tensor_tensor(r2v, x2, ctab, OP.mult))
                n_ = sem_dve2.einc(v.tensor_tensor(o2, r1v, r2v, OP.add))
                v.wait_ge(sem_dve2.h, n_)


def ph_kv(c, block):
    arena = c["arena"]
    sem_pe, sem_act, sem_dve, sem_dve2 = (c["sem_pe"], c["sem_act"],
                                          c["sem_dve"], c["sem_dve2"])
    sem_vself, sem_sself = c["sem_vself"], c["sem_sself"]
    ldP = PSem(c, "sem_ld", "sem_ldB")
    tbP = PSem(c, "sem_p2a", "sem_p2b")
    stP = PSem(c, "sem_st", "sem_stB")
    st2P = PSem(c, "sem_st2a", "sem_st2b")
    sem_ld2 = c["sem_ld2"]
    psA, psB = c["psA"], c["psB"]
    xf = [arena.alloc(128, 16 * 512, BF16) for _ in range(2)]
    wkva = arena.alloc(128, 16 * (KVL + DROPE), BF16)
    ckb = [arena.alloc(128, KVL, BF16) for _ in range(2)]
    krb = [arena.alloc(128, 128, BF16) for _ in range(2)]
    tabc = [arena.alloc(128, 32) for _ in range(2)]
    tabs = [arena.alloc(128, 32) for _ in range(2)]
    scr = arena.alloc(128, KVL)
    ssq = arena.alloc(128, 1)
    rstd = arena.alloc(128, 1)
    rt1 = arena.alloc(128, 32)
    rt2 = arena.alloc(128, 32)
    pe0, a0, dv0, dv20 = sem_pe.n, sem_act.n, sem_dve.n, sem_dve2.n
    ldP.pre([s for s in range(NSUP) for _ in range(16)])
    tbP.pre([i for i in range(NSLICE) for _ in range(2)])
    stP.pre(range(NSLICE))
    st2P.pre(range(NSLICE))

    @block.gpsimd
    def _(gp):
        for dc in range(16):
            c["sem_gp"].dinc(gp.dma_start(
                wkva[:, dc * 576:(dc + 1) * 576],
                c["Wkv_a"].ap()[dc * 128:(dc + 1) * 128, :]))
    GP_ALL = c["sem_gp"].n

    @block.sync
    def _(sync):
        for s in range(NSUP):
            if s >= 2:
                sync.wait_ge(sem_pe.h, pe0 + 4 * (s - 1))
            for dc in range(16):
                ldP.issue(s, sync.dma_start(
                    xf[s % 2][:, dc * 512:(dc + 1) * 512],
                    c["s_xn1"].ap()[s * SUP:(s + 1) * SUP, dc * 128:(dc + 1) * 128],
                    transpose=True))
            for j in range(4):
                i = s * 4 + j
                if i >= 2:
                    sync.wait_ge(sem_dve2.h, dv20 + (i - 1))
                pos0 = (i % 16) * 128
                tbP.issue(i, sync.dma_start(tabc[i % 2][:],
                                            c["ROPEC"].ap()[pos0:pos0 + 128, 0:32]))
                tbP.issue(i, sync.dma_start(tabs[i % 2][:],
                                            c["ROPES"].ap()[pos0:pos0 + 128, 0:32]))
            for j in range(4):
                i = s * 4 + j
                sync.wait_ge(sem_dve.h, dv0 + 2 * (i + 1))
                stP.issue(i, sync.dma_start(
                    c["s_ckvn"].ap()[i * TS:(i + 1) * TS, :], ckb[j % 2][:]))
                sync.wait_ge(sem_dve2.h, dv20 + i + 1)
                st2P.issue(i, sync.dma_start(
                    c["s_krope"].ap()[i * TS:(i + 1) * TS, :], krb[j % 2][:]))
        sync.wait_ge(stP.s[0].h, stP.s[0].n)
        sync.wait_ge(stP.s[1].h, stP.s[1].n)
        sync.wait_ge(st2P.s[0].h, st2P.s[0].n)
        sync.wait_ge(st2P.s[1].h, st2P.s[1].n)

    @block.tensor
    def _(te):
        te.wait_ge(c["sem_gp"].h, GP_ALL)
        for s in range(NSUP):
            ldP.wait(te, s)
            for j in range(4):
                i = s * 4 + j
                ps = [psA, psB][j % 2]
                if i >= 2:
                    te.wait_ge(sem_dve.h, dv0 + 2 * (i - 1))
                    te.wait_ge(sem_dve2.h, dv20 + i - 1)
                for grp, (c0, cw) in enumerate([(0, 512), (512, 64)]):
                    for dc in range(16):
                        inst = te.matmul(
                            ps[:, c0:c0 + cw],
                            xf[s % 2][:, dc * 512 + j * 128: dc * 512 + j * 128 + 128],
                            wkva[:, dc * 576 + c0: dc * 576 + c0 + cw],
                            start=(dc == 0), stop=(dc == 15))
                sem_pe.einc(inst)

    @block.scalar
    def _(sc):
        SS = mk_chain(sc, sem_sself)
        for s in range(NSUP):
            for j in range(4):
                i = s * 4 + j
                ps = [psA, psB][j % 2]
                sc.wait_ge(sem_pe.h, pe0 + i + 1)
                if i >= 1:
                    sc.wait_ge(sem_dve.h, dv0 + 2 * (i - 1) + 1)
                sem_act.einc(sc.activation(scr[:], ps[:, 0:KVL], AF.Square))
                sc.wait_ge(sem_dve.h, dv0 + 2 * i + 1)
                SS(sc.activation(rstd[:], rstd[:], AF.Ln))
                sem_act.einc(sc.activation(rstd[:], rstd[:], AF.Exp,
                                           scale=np.float32(-0.5)))

    @block.vector
    def _(v):
        V = mk_chain(v, sem_vself)
        V(v.memset(krb[0][:], 0.0))
        V(v.memset(krb[1][:], 0.0))
        for s in range(NSUP):
            for j in range(4):
                i = s * 4 + j
                ps = [psA, psB][j % 2]
                v.wait_ge(sem_act.h, a0 + 2 * i + 1)
                V(v.tensor_reduce(ssq[:], scr[:], AX.X, OP.add))
                n_ = sem_dve.einc(v.tensor_scalar(rstd[:], ssq[:], 1.0 / KVL,
                                                  float(EPS_RMS), OP.mult, OP.add))
                v.wait_ge(sem_dve.h, n_)
                v.wait_ge(sem_act.h, a0 + 2 * i + 2)
                if i >= 2:
                    stP.wait(v, i - 2)
                sem_dve.einc(v.tensor_scalar(ckb[j % 2][:], ps[:, 0:KVL],
                                             rstd[:], None, OP.mult, OP.bypass))
                tbP.wait(v, i)
                if i >= 2:
                    st2P.wait(v, i - 2)
                rr = ps[:, KVL:KVL + DROPE]
                x1 = rr.rearrange("p (j t) -> p j t", t=2)[:, :, 0:1].squeeze()
                x2 = rr.rearrange("p (j t) -> p j t", t=2)[:, :, 1:2].squeeze()
                ov = krb[j % 2][:, 0:DROPE]
                o1 = ov.rearrange("p (j t) -> p j t", t=2)[:, :, 0:1].squeeze()
                o2 = ov.rearrange("p (j t) -> p j t", t=2)[:, :, 1:2].squeeze()
                V(v.tensor_tensor(rt1[:], x1, tabc[i % 2][:], OP.mult))
                V(v.tensor_tensor(rt2[:], x2, tabs[i % 2][:], OP.mult))
                V(v.tensor_tensor(o1, rt1[:], rt2[:], OP.subtract))
                V(v.tensor_tensor(rt1[:], x1, tabs[i % 2][:], OP.mult))
                V(v.tensor_tensor(rt2[:], x2, tabc[i % 2][:], OP.mult))
                n_ = sem_dve2.einc(v.tensor_tensor(o2, rt1[:], rt2[:], OP.add))
                v.wait_ge(sem_dve2.h, n_)


def ph_kvb(c, block):
    arena = c["arena"]
    sem_pe, sem_act, sem_ld2 = c["sem_pe"], c["sem_act"], c["sem_ld2"]
    ldP = PSem(c, "sem_ld", "sem_ldB")
    stP = PSem(c, "sem_st", "sem_stB")
    st2P = PSem(c, "sem_st2a", "sem_st2b")
    psA, psB = c["psA"], c["psB"]
    wkvb = arena.alloc(128, 4 * 4096, BF16)
    ckf = [arena.alloc(128, 4 * 512, BF16) for _ in range(2)]
    knb = [arena.alloc(128, 512, BF16) for _ in range(2)]
    vb = [arena.alloc(128, 2048, BF16) for _ in range(2)]
    pe0, a0 = sem_pe.n, sem_act.n
    ldP.pre([s for s in range(NSUP) for _ in range(4)])
    stP.pre(range(NSUP * 16))
    st2P.pre(range(NSUP * 4))
    PE_S = 20
    ACT_S = 20

    @block.sync
    def _(sync):
        for lc in range(4):
            sem_ld2.dinc(sync.dma_start(wkvb[:, lc * 4096:(lc + 1) * 4096],
                                        c["s_wkvbbf"].ap()[lc * 128:(lc + 1) * 128, :]))
        for s in range(NSUP):
            if s >= 2:
                sync.wait_ge(sem_pe.h, pe0 + PE_S * (s - 1))
            for lc in range(4):
                ldP.issue(s, sync.dma_start(
                    ckf[s % 2][:, lc * 512:(lc + 1) * 512],
                    c["s_ckvn"].ap()[s * SUP:(s + 1) * SUP, lc * 128:(lc + 1) * 128],
                    transpose=True))
            seq, pos = s // 4, (s % 4) * SUP
            for h in range(16):
                k = s * 16 + h
                sync.wait_ge(sem_act.h, a0 + ACT_S * s + h + 1)
                stP.issue(k, sync.dma_start(
                    c["s_knope"].ap()[(seq * NH + h) * DNOPE:(seq * NH + h + 1) * DNOPE,
                                      pos:pos + SUP],
                    knb[h % 2][:]))
            for j in range(4):
                k = s * 4 + j
                sync.wait_ge(sem_act.h, a0 + ACT_S * s + 16 + j + 1)
                st2P.issue(k, sync.dma_start(
                    c["s_v"].ap()[s * SUP + j * TS: s * SUP + (j + 1) * TS, :],
                    vb[j % 2][:]))
        sync.wait_ge(stP.s[0].h, stP.s[0].n)
        sync.wait_ge(stP.s[1].h, stP.s[1].n)
        sync.wait_ge(st2P.s[0].h, st2P.s[0].n)
        sync.wait_ge(st2P.s[1].h, st2P.s[1].n)

    LD2_ALL = sem_ld2.n

    @block.tensor
    def _(te):
        te.wait_ge(sem_ld2.h, LD2_ALL)
        for s in range(NSUP):
            ldP.wait(te, s)
            for h in range(16):
                if h >= 4 or s >= 1:
                    kk = h - 4 if h >= 4 else 12 + h
                    ss = s if h >= 4 else s - 1
                    te.wait_ge(sem_act.h, a0 + ACT_S * ss + kk + 1)
                for lc in range(4):
                    inst = te.matmul(
                        psA[:, (h % 4) * 512:(h % 4 + 1) * 512],
                        wkvb[:, lc * 4096 + h * 256: lc * 4096 + h * 256 + 128],
                        ckf[s % 2][:, lc * 512:(lc + 1) * 512],
                        start=(lc == 0), stop=(lc == 3))
                sem_pe.einc(inst)
            for j in range(4):
                if j >= 1 or s >= 1:
                    jj = j - 1 if j >= 1 else 3
                    ss = s if j >= 1 else s - 1
                    te.wait_ge(sem_act.h, a0 + ACT_S * ss + 16 + jj + 1)
                for bs in range(4):
                    for lc in range(4):
                        vview = wkvb[:, lc * 4096: (lc + 1) * 4096].rearrange(
                            "p (h z) -> p h z", z=256)[:, 4 * bs:4 * bs + 4, 128:256]
                        inst = te.matmul(
                            psB[:, bs * 512:(bs + 1) * 512],
                            ckf[s % 2][:, lc * 512 + j * 128: lc * 512 + j * 128 + 128],
                            vview,
                            start=(lc == 0), stop=(lc == 3))
                sem_pe.einc(inst)

    @block.scalar
    def _(sc):
        for s in range(NSUP):
            for h in range(16):
                k = s * 16 + h
                sc.wait_ge(sem_pe.h, pe0 + PE_S * s + h + 1)
                if k >= 2:
                    kp = k - 2 if h >= 2 else (s - 1) * 16 + 14 + h
                    stP.wait(sc, kp)
                sem_act.einc(sc.activation(knb[h % 2][:],
                                           psA[:, (h % 4) * 512:(h % 4 + 1) * 512],
                                           AF.Copy, bias=0.0))
            for j in range(4):
                k = s * 4 + j
                sc.wait_ge(sem_pe.h, pe0 + PE_S * s + 16 + j + 1)
                if k >= 2:
                    st2P.wait(sc, k - 2)
                sem_act.einc(sc.activation(vb[j % 2][:], psB[:], AF.Copy, bias=0.0))


def _attn_sched():
    seq = []
    nd = d = n = 0
    for qs in range(4):
        for kc in range(4 * qs + 4):
            diag = kc >= 4 * qs
            seq.append((qs, kc, diag, nd, d, n))
            if diag:
                d += 1
            else:
                nd += 1
            n += 1
    return seq

ATTN_SEQ = _attn_sched()
N_EXP = len(ATTN_SEQ)
ACT_G = 3 + N_EXP
POOL_G = 2 + 16


def _pe_vals():
    vals = {}
    v = 0
    for name in ("qs2", "ks2", "bcast"):
        v += 1
        vals[name] = v
    for (qs, kc, diag, nd, d, n) in ATTN_SEQ:
        v += 1
        vals[("S", n)] = v
        last = kc == 4 * qs + 3
        v += 2 if last else 1
        vals[("RP", n)] = v
        if last:
            vals[("qsdone", qs)] = v
    return vals, v

PE_VALS, PE_G = _pe_vals()


def ph_attn(c, block):
    arena = c["arena"]
    sem_pe, sem_act, sem_dve = c["sem_pe"], c["sem_act"], c["sem_dve"]
    sem_dve2, sem_pool = c["sem_dve2"], c["sem_pool"]
    sem_vself = c["sem_vself"]
    sem_ld2 = c["sem_ld2"]
    ldP = PSem(c, "sem_ld", "sem_ldB")
    stP = PSem(c, "sem_st", "sem_stB")
    psA, psB = c["psA"], c["psB"]
    qn = [arena.alloc(128, T, BF16) for _ in range(2)]
    qr = [arena.alloc(128, T, BF16) for _ in range(2)]
    kn = [arena.alloc(128, T, BF16) for _ in range(2)]
    vv = [arena.alloc(128, 16 * DV, BF16) for _ in range(2)]
    krf = [arena.alloc(128, T, BF16) for _ in range(2)]
    tri = arena.alloc(128, 1024)
    qsq = arena.alloc(128, T, BF16)
    qsq2 = arena.alloc(64, T, BF16)
    attnb = [arena.alloc(128, 512, BF16) for _ in range(2)]
    smsk = [arena.alloc(128, 512) for _ in range(2)]
    rb = arena.alloc(128, 512)
    osb = [arena.alloc(128, 512, BF16) for _ in range(2)]
    mq = arena.alloc(1, 1)
    mk = arena.alloc(1, 1)
    nb = arena.alloc(1, 1)
    nbB = arena.alloc(128, 1)
    pe0, a0, dv0, dv20, p0 = (sem_pe.n, sem_act.n, sem_dve.n, sem_dve2.n,
                              sem_pool.n)
    ldP.pre([g for g in range(32) for _ in range(7)])
    stP.pre(range(32 * 4))

    def actT(g, name, n=0):
        base = a0 + ACT_G * g
        return base + {"sqq": 1, "sqk": 2, "sqrt": 3}.get(name, 4 + n)

    def peT(g, name, n=0, qs=0):
        base = pe0 + PE_G * g
        if name in ("qs2", "ks2", "bcast"):
            return base + PE_VALS[name]
        if name == "qsdone":
            return base + PE_VALS[("qsdone", qs)]
        return base + PE_VALS[(name, n)]

    def poolT(g, name, d=0):
        base = p0 + POOL_G * g
        return base + {"b1": 1, "b2": 2}.get(name, 3 + d)

    def qsdone_before(n):
        cnt = 0
        ncum = 0
        for q in range(4):
            ncum += 4 * q + 4
            if n >= ncum:
                cnt += 1
        return cnt

    @block.sync
    def _(sync):
        sem_ld2.dinc(sync.dma_start(tri[:], c["TRI"].ap()))
        for sq in range(2):
            sem_ld2.dinc(sync.dma_start(
                krf[sq][:], c["s_krope"].ap()[sq * T:(sq + 1) * T, :],
                transpose=True))
        for g in range(32):
            sq, h = g // 16, g % 16
            if g >= 2:
                sync.wait_ge(sem_dve.h, dv0 + 4 * (g - 2) + 4)
            ldP.issue(g, sync.dma_start(
                qn[g % 2][:], c["s_qnope"].ap()[h * 128:(h + 1) * 128,
                                                sq * T:(sq + 1) * T]))
            ldP.issue(g, sync.dma_start(
                qr[g % 2][:], c["s_qrope"].ap()[sq * T:(sq + 1) * T,
                                                h * 128:(h + 1) * 128],
                transpose=True))
            ldP.issue(g, sync.dma_start(
                kn[g % 2][:],
                c["s_knope"].ap()[(sq * NH + h) * DNOPE:(sq * NH + h + 1) * DNOPE, :]))
            for part in range(4):
                ldP.issue(g, sync.dma_start(
                    vv[g % 2][:].rearrange("p (kc z) -> p kc z", z=DV)[
                        :, part * 4:(part + 1) * 4, :],
                    c["s_v"].ap()[sq * T + part * 512: sq * T + (part + 1) * 512,
                                  h * DV:(h + 1) * DV].rearrange(
                        "(kc p) z -> p kc z", p=128)))
            for qs in range(4):
                sync.wait_ge(sem_dve.h, dv0 + 4 * g + qs + 1)
                stP.issue(4 * g + qs, sync.dma_start(
                    c["s_o"].ap()[(sq * NH + h) * DV:(sq * NH + h + 1) * DV,
                                  qs * 512:(qs + 1) * 512],
                    osb[qs % 2][:]))
        sync.wait_ge(stP.s[0].h, stP.s[0].n)
        sync.wait_ge(stP.s[1].h, stP.s[1].n)

    LD2_ALL = sem_ld2.n

    @block.tensor
    def _(te):
        te.wait_ge(sem_ld2.h, LD2_ALL)
        for g in range(32):
            sq = g // 16
            ldP.wait(te, g)
            te.wait_ge(sem_act.h, actT(g, "sqq"))
            for sl in range(4):
                te.matmul(psA[0:1, sl * 512:(sl + 1) * 512],
                          c["k_ones_bf"][:, 0:1],
                          qsq[:, sl * 512:(sl + 1) * 512], start=True, stop=False)
                inst = te.matmul(psA[0:1, sl * 512:(sl + 1) * 512],
                                 c["k_ones_bf"][0:64, 0:1],
                                 qsq2[:, sl * 512:(sl + 1) * 512],
                                 start=False, stop=True)
            sem_pe.einc(inst)
            te.wait_ge(sem_act.h, actT(g, "sqk"))
            for sl in range(4):
                te.matmul(psB[0:1, sl * 512:(sl + 1) * 512],
                          c["k_ones_bf"][:, 0:1],
                          qsq[:, sl * 512:(sl + 1) * 512], start=True, stop=False)
                inst = te.matmul(psB[0:1, sl * 512:(sl + 1) * 512],
                                 c["k_ones_bf"][0:64, 0:1],
                                 qsq2[:, sl * 512:(sl + 1) * 512],
                                 start=False, stop=True)
            sem_pe.einc(inst)
            te.wait_ge(sem_pool.h, poolT(g, "b2"))
            sem_pe.einc(te.matmul(psB[:, 0:1], c["k_ones_f"][0:1, :],
                                  nb[:], start=True, stop=True))
            for (qs, kc, diag, nd, d, n) in ATTN_SEQ:
                qsm = qsdone_before(n)
                if diag:
                    reg = psA[:, 1024 + (d % 2) * 512: 1024 + (d % 2 + 1) * 512]
                    if d >= 2:
                        te.wait_ge(sem_pool.h, poolT(g, "m", d - 2))
                else:
                    reg = psA[:, (nd % 2) * 512: (nd % 2 + 1) * 512]
                    if nd >= 2:
                        prev_n = [e[5] for e in ATTN_SEQ
                                  if not e[2] and e[3] == nd - 2][0]
                        te.wait_ge(sem_act.h, actT(g, "exp", prev_n))
                te.matmul(reg, kn[g % 2][:, kc * 128:(kc + 1) * 128],
                          qn[g % 2][:, qs * 512:(qs + 1) * 512],
                          start=True, stop=False)
                sem_pe.einc(te.matmul(reg, krf[sq][0:64, kc * 128:(kc + 1) * 128],
                                      qr[g % 2][0:64, qs * 512:(qs + 1) * 512],
                                      start=False, stop=True))
                te.wait_ge(sem_act.h, actT(g, "exp", n))
                sumb = psB[:, (qs % 2) * 1024: (qs % 2) * 1024 + 512]
                ot = psB[:, (qs % 2) * 1024 + 512: (qs % 2) * 1024 + 1024]
                if qs >= 2 and kc == 0:
                    te.wait_ge(sem_dve.h, dv0 + 4 * g + (qs - 2) + 1)
                last = kc == 4 * qs + 3
                te.matmul(sumb, c["k_ones_bf"][:, 0:128], attnb[n % 2][:],
                          start=(kc == 0), stop=last)
                inst = te.matmul(
                    ot,
                    vv[g % 2][:, kc * DV:(kc + 1) * DV],
                    attnb[n % 2][:],
                    start=(kc == 0), stop=last)
                inst.then_inc(sem_pe.h, 2 if last else 1)
                sem_pe.n += 2 if last else 1

    @block.scalar
    def _(sc):
        SS = mk_chain(sc, c["sem_sself"])
        for g in range(32):
            ldP.wait(sc, g)
            sc.activation(qsq[:], qn[g % 2][:], AF.Square)
            sem_act.einc(sc.activation(qsq2[:], qr[g % 2][0:64, :], AF.Square))
            te_dummy = None
            sc.wait_ge(sem_pe.h, peT(g, "qs2"))
            sc.activation(qsq[:], kn[g % 2][:], AF.Square)
            sem_act.einc(sc.activation(qsq2[:], krf[g // 16][0:64, :], AF.Square))
            sc.wait_ge(sem_pool.h, poolT(g, "b1"))
            sem_act.einc(sc.activation(nb[:], nb[:], AF.Sqrt))
            for (qs, kc, diag, nd, d, n) in ATTN_SEQ:
                sc.wait_ge(sem_pe.h, peT(g, "S", n))
                sc.wait_ge(sem_dve2.h, dv20 + g + 1)
                if n >= 2:
                    sc.wait_ge(sem_pe.h, peT(g, "RP", n - 2))
                if diag:
                    sc.wait_ge(sem_pool.h, poolT(g, "m", d))
                    src_ap = smsk[d % 2][:]
                else:
                    src_ap = psA[:, (nd % 2) * 512: (nd % 2 + 1) * 512]
                sem_act.einc(sc.activation(attnb[n % 2][:], src_ap, AF.Exp,
                                           bias=nbB[:]))

    @block.vector
    def _(v):
        V = mk_chain(v, sem_vself)
        for g in range(32):
            v.wait_ge(sem_pe.h, peT(g, "qs2"))
            V(v.tensor_reduce(mq[:], psA[0:1, 0:2048], AX.X, OP.max))
            v.wait_ge(sem_pe.h, peT(g, "ks2"))
            V(v.tensor_reduce(mk[:], psB[0:1, 0:2048], AX.X, OP.max))
            V(v.tensor_tensor(nb[:], mq[:], mk[:], OP.mult))
            n_ = sem_pool.einc(v.tensor_scalar(nb[:], nb[:], 1.02, None,
                                               OP.mult, OP.bypass))
            v.wait_ge(sem_pool.h, n_)
            v.wait_ge(sem_act.h, actT(g, "sqrt"))
            n_ = sem_pool.einc(v.tensor_scalar(nb[:], nb[:], -1.0, None,
                                               OP.mult, OP.bypass))
            v.wait_ge(sem_pool.h, n_)
            v.wait_ge(sem_pe.h, peT(g, "bcast"))
            n_ = sem_dve2.einc(v.tensor_copy(nbB[:], psB[:, 0:1]))
            v.wait_ge(sem_dve2.h, n_)
            for (qs, kc, diag, nd, d, n) in ATTN_SEQ:
                if diag:
                    v.wait_ge(sem_pe.h, peT(g, "S", n))
                    if d >= 2:
                        prev_n = [e[5] for e in ATTN_SEQ if e[2] and e[4] == d - 2][0]
                        v.wait_ge(sem_act.h, actT(g, "exp", prev_n))
                    delta = kc * 128 - qs * 512
                    n_ = sem_pool.einc(v.tensor_tensor(
                        smsk[d % 2][:],
                        psA[:, 1024 + (d % 2) * 512: 1024 + (d % 2 + 1) * 512],
                        tri[:, 384 - delta: 384 - delta + 512], OP.add))
                    v.wait_ge(sem_pool.h, n_)
                if kc == 4 * qs + 3:
                    v.wait_ge(sem_pe.h, peT(g, "qsdone", qs=qs))
                    if 4 * g + qs >= 2:
                        stP.wait(v, 4 * g + qs - 2)
                    sumb = psB[:, (qs % 2) * 1024: (qs % 2) * 1024 + 512]
                    ot = psB[:, (qs % 2) * 1024 + 512: (qs % 2) * 1024 + 1024]
                    V(v.reciprocal(rb[:], sumb))
                    sem_dve.einc(v.tensor_tensor(osb[qs % 2][:], ot, rb[:], OP.mult))


def ph_wo(c, block):
    arena = c["arena"]
    sem_pe, sem_act, sem_gp = c["sem_pe"], c["sem_act"], c["sem_gp"]
    ldP = PSem(c, "sem_ld", "sem_ldB")
    stP = PSem(c, "sem_st", "sem_stB")
    psA, psB = c["psA"], c["psB"]
    wo = arena.alloc(128, 16 * 2048, BF16)
    ob = [arena.alloc(128, 16 * 128, BF16) for _ in range(2)]
    s1b = [arena.alloc(128, D) for _ in range(2)]
    pe0, a0 = sem_pe.n, sem_act.n
    ldP.pre([i for i in range(NSLICE) for _ in range(16)])
    stP.pre(range(NSLICE))

    @block.gpsimd
    def _(gp):
        for hc in range(16):
            sem_gp.dinc(gp.dma_start(wo[:, hc * 2048:(hc + 1) * 2048],
                                     c["Wo"].ap()[hc * 128:(hc + 1) * 128, :]))
    GP_ALL = sem_gp.n

    @block.sync
    def _(sync):
        for i in range(NSLICE):
            sq, pos = i // 16, (i % 16) * TS
            if i >= 2:
                sync.wait_ge(sem_pe.h, pe0 + (i - 1))
            for h in range(16):
                ldP.issue(i, sync.dma_start(
                    ob[i % 2][:, h * 128:(h + 1) * 128],
                    c["s_o"].ap()[(sq * NH + h) * DV:(sq * NH + h + 1) * DV,
                                  pos:pos + TS]))
            if i >= 1:
                sync.wait_ge(sem_act.h, a0 + i)
                stP.issue(i - 1, sync.dma_start(
                    c["s_s1"].ap()[(i - 1) * TS:i * TS, :], s1b[(i - 1) % 2][:]))
        sync.wait_ge(sem_act.h, a0 + NSLICE)
        stP.issue(NSLICE - 1, sync.dma_start(
            c["s_s1"].ap()[(NSLICE - 1) * TS:NSLICE * TS, :], s1b[(NSLICE - 1) % 2][:]))
        sync.wait_ge(stP.s[0].h, stP.s[0].n)
        sync.wait_ge(stP.s[1].h, stP.s[1].n)

    @block.tensor
    def _(te):
        te.wait_ge(sem_gp.h, GP_ALL)
        for i in range(NSLICE):
            ldP.wait(te, i)
            if i >= 2:
                te.wait_ge(sem_act.h, a0 + (i - 1))
            ps = [psA, psB][i % 2]
            for hc in range(16):
                for bs in range(4):
                    inst = te.matmul(ps[:, bs * 512:(bs + 1) * 512],
                                     ob[i % 2][:, hc * 128:(hc + 1) * 128],
                                     wo[:, hc * 2048 + bs * 512: hc * 2048 + (bs + 1) * 512],
                                     start=(hc == 0), stop=(hc == 15))
            sem_pe.einc(inst)

    @block.scalar
    def _(sc):
        for i in range(NSLICE):
            sc.wait_ge(sem_pe.h, pe0 + i + 1)
            if i >= 2:
                stP.wait(sc, i - 2)
            ps = [psA, psB][i % 2]
            sem_act.einc(sc.activation(s1b[i % 2][:], ps[:], AF.Copy, bias=0.0))


def ph_ffn1(c, block):
    arena = c["arena"]
    sem_pe, sem_act, sem_dve, sem_dve2 = (c["sem_pe"], c["sem_act"],
                                          c["sem_dve"], c["sem_dve2"])
    sem_vself, sem_sself = c["sem_vself"], c["sem_sself"]
    sem_ld2, sem_gp = c["sem_ld2"], c["sem_gp"]
    ldP = PSem(c, "sem_ld", "sem_ldB")
    stP = PSem(c, "sem_st", "sem_stB")
    psA, psB = c["psA"], c["psB"]
    slabbf = [t[:].bitcast(BF16) for t in c["slab"]]
    hbf = [t[:].bitcast(BF16) for t in c["Hsb"]]

    def bwT_dc(dc):
        return slabbf[dc // 4][:, (dc % 4) * 2048:(dc % 4 + 1) * 2048]

    laA = hbf[0][:, 0:2048]
    laB = hbf[1][:, 0:2048]
    rwT = arena.alloc(128, 16 * 16, BF16)
    xf = [arena.alloc(128, 16 * 512, BF16) for _ in range(2)]
    g1b = [arena.alloc(128, D) for _ in range(2)]
    gmax = arena.alloc(128, 1)
    nmax = arena.alloc(128, 1)
    m2_ = arena.alloc(128, 1)
    den = arena.alloc(128, 1)
    gsum = arena.alloc(128, 1)
    grec = arena.alloc(128, 1)
    gexp = arena.alloc(128, NE)
    gprob = arena.alloc(128, NE)
    gmask = arena.alloc(128, NE)
    ge2 = arena.alloc(128, NE)
    gate = arena.alloc(128, NE)
    gateT = arena.alloc(NE, 128)
    gateB = arena.alloc(128, 128)
    mids = arena.alloc(128, 128, BF16)
    pe0, a0, dv0, dv20 = sem_pe.n, sem_act.n, sem_dve.n, sem_dve2.n
    ldP.pre([s for s in range(NSUP) for _ in range(16)])
    stP.pre(range(NSLICE))
    PE_I, DVE_I, ACT_I = 5, 4, 2

    @block.gpsimd
    def _(gp):
        for dc in range(16):
            sem_gp.dinc(gp.dma_start(
                laA[:, dc * 128:(dc + 1) * 128].rearrange("p (e r) -> p e r", r=RL),
                c["lora_A"].ap().transpose([1, 0, 2])[dc * 128:(dc + 1) * 128, :, :]))
        sem_gp.dinc(gp.dma_start(
            laB[:], c["lora_B"].ap().rearrange("e r d -> (e r) d")))
    GP_ALL = sem_gp.n

    @block.sync
    def _(sync):
        for dc in range(16):
            sem_ld2.dinc(sync.dma_start(bwT_dc(dc),
                                        c["s_bwbf"].ap()[:, dc * 128:(dc + 1) * 128],
                                        transpose=True))
            sem_ld2.dinc(sync.dma_start(rwT[:, dc * 16:(dc + 1) * 16],
                                        c["s_rwbf"].ap()[:, dc * 128:(dc + 1) * 128],
                                        transpose=True))
        for s in range(NSUP):
            if s >= 2:
                sync.wait_ge(sem_pe.h, pe0 + PE_I * 4 * (s - 1))
            for dc in range(16):
                ldP.issue(s, sync.dma_start(
                    xf[s % 2][:, dc * 512:(dc + 1) * 512],
                    c["s_xn1"].ap()[s * SUP:(s + 1) * SUP, dc * 128:(dc + 1) * 128],
                    transpose=True))
            for j in range(4):
                i = s * 4 + j
                sync.wait_ge(sem_act.h, a0 + ACT_I * i + 2)
                stP.issue(i, sync.dma_start(
                    c["s_g1"].ap()[i * TS:(i + 1) * TS, :], g1b[i % 2][:]))
        sync.wait_ge(stP.s[0].h, stP.s[0].n)
        sync.wait_ge(stP.s[1].h, stP.s[1].n)

    LD2_ALL = sem_ld2.n

    @block.tensor
    def _(te):
        te.wait_ge(sem_ld2.h, LD2_ALL)
        te.wait_ge(sem_gp.h, GP_ALL)
        for s in range(NSUP):
            ldP.wait(te, s)
            for j in range(4):
                i = s * 4 + j
                xsl = xf[s % 2]
                if i >= 1:
                    te.wait_ge(sem_act.h, a0 + ACT_I * (i - 1) + 1)
                for dc in range(16):
                    te.matmul(psB[:, 0:8],
                              xsl[:, dc * 512 + j * 128: dc * 512 + j * 128 + 128],
                              rwT[:, dc * 16: dc * 16 + 8],
                              start=(dc == 0), stop=False)
                sem_pe.einc(te.matmul(psB[:, 0:8], c["k_ones_bf"][0:1, 0:128],
                                      c["rbrow"][0:1, :], start=False, stop=True))
                te.wait_ge(sem_dve.h, dv0 + DVE_I * i + 1)
                sem_pe.einc(te.transpose(psB[0:8, 512:640], gate[:], c["k_id_f"][:]))
                te.wait_ge(sem_dve.h, dv0 + DVE_I * i + 2)
                sem_pe.einc(te.matmul(psB[:, 640:768], c["k_sel2f"][:],
                                      gateT[:], start=True, stop=True))
                if i >= 1:
                    te.wait_ge(sem_act.h, a0 + ACT_I * (i - 1) + 2)
                for dc in range(16):
                    for bs in range(4):
                        te.matmul(psA[:, bs * 512:(bs + 1) * 512],
                                  xsl[:, dc * 512 + j * 128: dc * 512 + j * 128 + 128],
                                  bwT_dc(dc)[:, bs * 512:(bs + 1) * 512],
                                  start=(dc == 0), stop=False)
                te.wait_ge(sem_dve.h, dv0 + DVE_I * i + 3)
                for dc in range(16):
                    inst = te.matmul(psB[:, 768:896], laA[:, dc * 128:(dc + 1) * 128],
                                     xsl[:, dc * 512 + j * 128: dc * 512 + j * 128 + 128],
                                     start=(dc == 0), stop=(dc == 15))
                sem_pe.einc(inst)
                te.wait_ge(sem_dve.h, dv0 + DVE_I * i + 4)
                for bs in range(4):
                    te.matmul(psA[:, bs * 512:(bs + 1) * 512], mids[:],
                              laB[:, bs * 512:(bs + 1) * 512], start=False, stop=False)
                for bs in range(4):
                    inst = te.matmul(psA[:, bs * 512:(bs + 1) * 512],
                                     c["k_ones_bf"][0:1, 0:128],
                                     c["bbrow"][0:1, bs * 512:(bs + 1) * 512],
                                     start=False, stop=True)
                sem_pe.einc(inst)

    @block.scalar
    def _(sc):
        for i in range(NSLICE):
            sc.wait_ge(sem_dve2.h, dv20 + i + 1)
            sem_act.einc(sc.activation(gexp[:], psB[:, 0:8], AF.Exp, bias=nmax[:]))
            sc.wait_ge(sem_pe.h, pe0 + PE_I * i + 5)
            if i >= 2:
                stP.wait(sc, i - 2)
            sem_act.einc(sc.activation(g1b[i % 2][:], psA[:], AF.Silu))

    @block.vector
    def _(v):
        V = mk_chain(v, sem_vself)
        for i in range(NSLICE):
            v.wait_ge(sem_pe.h, pe0 + PE_I * i + 1)
            V(v.tensor_reduce(gmax[:], psB[:, 0:8], AX.X, OP.max))
            n_ = sem_dve2.einc(v.tensor_scalar(nmax[:], gmax[:], -1.0, None,
                                               OP.mult, OP.bypass))
            v.wait_ge(sem_dve2.h, n_)
            v.wait_ge(sem_act.h, a0 + ACT_I * i + 1)
            V(v.tensor_reduce(gsum[:], gexp[:], AX.X, OP.add))
            V(v.reciprocal(grec[:], gsum[:]))
            V(v.tensor_scalar(gprob[:], gexp[:], grec[:], None, OP.mult, OP.bypass))
            V(v.tensor_reduce(gmax[:], gprob[:], AX.X, OP.max))
            V(v.tensor_scalar(gmask[:], gprob[:], gmax[:], None, OP.is_ge, OP.bypass))
            V(v.tensor_tensor(gmask[:], gprob[:], gmask[:], OP.subtract))
            V(v.tensor_reduce(m2_[:], gmask[:], AX.X, OP.max))
            V(v.tensor_tensor(den[:], gmax[:], m2_[:], OP.add))
            V(v.reciprocal(den[:], den[:]))
            V(v.tensor_scalar(ge2[:], gprob[:], m2_[:], None, OP.is_ge, OP.bypass))
            V(v.tensor_tensor(gate[:], gprob[:], ge2[:], OP.mult))
            n_ = sem_dve.einc(v.tensor_scalar(gate[:], gate[:], den[:], None,
                                              OP.mult, OP.bypass))
            v.wait_ge(sem_dve.h, n_)
            v.wait_ge(sem_pe.h, pe0 + PE_I * i + 2)
            n_ = sem_dve.einc(v.tensor_copy(gateT[:], psB[0:8, 512:640]))
            v.wait_ge(sem_dve.h, n_)
            v.wait_ge(sem_pe.h, pe0 + PE_I * i + 3)
            n_ = sem_dve.einc(v.tensor_copy(gateB[:], psB[:, 640:768]))
            v.wait_ge(sem_dve.h, n_)
            v.wait_ge(sem_pe.h, pe0 + PE_I * i + 4)
            n_ = sem_dve.einc(v.tensor_tensor(mids[:], psB[:, 768:896], gateB[:],
                                              OP.mult))
            v.wait_ge(sem_dve.h, n_)


def ph_out(c, block):
    arena = c["arena"]
    sem_dve = c["sem_dve"]
    sem_vself = c["sem_vself"]
    ldP = PSem(c, "sem_ld", "sem_ldB")
    stP = PSem(c, "sem_st", "sem_stB")
    x1b = [arena.alloc(128, D) for _ in range(2)]
    s1b = [arena.alloc(128, D) for _ in range(2)]
    g1b = [arena.alloc(128, D) for _ in range(2)]
    outb = [arena.alloc(128, D) for _ in range(2)]
    dv0 = sem_dve.n
    ldP.pre([i for i in range(NSLICE) for _ in range(3)])
    stP.pre(range(NSLICE))

    @block.sync
    def _(sync):
        for i in range(NSLICE):
            if i >= 2:
                sync.wait_ge(sem_dve.h, dv0 + i - 1)
            ldP.issue(i, sync.dma_start(x1b[i % 2][:], c["s_x1"].ap()[i * TS:(i + 1) * TS, :]))
            ldP.issue(i, sync.dma_start(s1b[i % 2][:], c["s_s1"].ap()[i * TS:(i + 1) * TS, :]))
            ldP.issue(i, sync.dma_start(g1b[i % 2][:], c["s_g1"].ap()[i * TS:(i + 1) * TS, :]))
            if i >= 1:
                sync.wait_ge(sem_dve.h, dv0 + i)
                stP.issue(i - 1, sync.dma_start(
                    c["out"].ap()[(i - 1) * TS:i * TS, :], outb[(i - 1) % 2][:]))
        sync.wait_ge(sem_dve.h, dv0 + NSLICE)
        stP.issue(NSLICE - 1, sync.dma_start(
            c["out"].ap()[(NSLICE - 1) * TS:NSLICE * TS, :], outb[(NSLICE - 1) % 2][:]))
        sync.wait_ge(stP.s[0].h, stP.s[0].n)
        sync.wait_ge(stP.s[1].h, stP.s[1].n)

    @block.vector
    def _(v):
        V = mk_chain(v, sem_vself)
        for i in range(NSLICE):
            ldP.wait(v, i)
            if i >= 2:
                stP.wait(v, i - 2)
            V(v.tensor_tensor(outb[i % 2][:], s1b[i % 2][:], g1b[i % 2][:], OP.mult))
            sem_dve.einc(v.tensor_tensor(outb[i % 2][:], outb[i % 2][:],
                                         x1b[i % 2][:], OP.add))


def kernel(**inputs):
    consts = host_constants()
    x = np.ascontiguousarray(inputs["L0_attn_x"])
    nc = build_program(n_phases=PHASES_LIMIT, dump=DBG_DUMP)
    base = {
        "nu_log": inputs["L0_attn_nu_log"], "theta_log": inputs["L0_attn_theta_log"],
        "B_re": inputs["L0_attn_B_re"], "B_im": inputs["L0_attn_B_im"],
        "C_re": inputs["L0_attn_C_re"], "C_im": inputs["L0_attn_C_im"],
        "Dvec": inputs["L0_attn_D"],
        "proj1": inputs["L0_ffn_proj1"], "proj1_bias": inputs["L0_ffn_proj1_bias"],
        "proj2": inputs["L0_ffn_proj2"], "proj2_bias": inputs["L0_ffn_proj2_bias"],
        "Wq": inputs["L1_attn_Wq"], "Wkv_a": inputs["L1_attn_Wkv_a"],
        "kv_norm_w": inputs["L1_attn_kv_norm_w"], "Wkv_b": inputs["L1_attn_Wkv_b"],
        "Wo": inputs["L1_attn_Wo"],
        "base_w": inputs["L1_ffn_base_w"], "base_b": inputs["L1_ffn_base_b"],
        "router_w": inputs["L1_ffn_router_w"], "router_b": inputs["L1_ffn_router_b"],
        "lora_A": inputs["L1_ffn_lora_A"], "lora_B": inputs["L1_ffn_lora_B"],
    }
    base.update(consts)
    in_maps = []
    for cid in range(NCORE):
        m = dict(base)
        m["x"] = np.ascontiguousarray(x[cid * SEQS:(cid + 1) * SEQS].reshape(NTOK, D))
        in_maps.append(m)
    res = run_bass_kernel_spmd(nc, in_maps, core_ids=list(range(NCORE)))
    out = np.empty((BATCH, T, D), dtype=np.float32)
    for cid in range(NCORE):
        out[cid * SEQS:(cid + 1) * SEQS] = res.results[cid]["out"].reshape(SEQS, T, D)
    return out


# version 02
# speedup vs baseline: 1.2722x; 1.2722x over previous
import math
import numpy as np
import ml_dtypes
import concourse.bass as bass
import concourse.mybir as mybir
from concourse.bass_utils import run_bass_kernel_spmd
from concourse.alu_op_type import AluOpType as OP

F32 = mybir.dt.float32
BF16 = mybir.dt.bfloat16
AF = mybir.ActivationFunctionType
AX = mybir.AxisListType
f32 = np.float32
bf16 = ml_dtypes.bfloat16

NCORE = 8
BATCH = 16
SEQS = 2
T = 2048
NTOK = SEQS * T
D = 2048
TS = 128
NSLICE = NTOK // TS
SUP = 512
NSUP = NTOK // SUP
NST = 256
HFF = 8192
NH = 16
DQK = 192
DROPE = 64
DNOPE = 128
KVL = 512
DV = 128
NE = 8
RL = 16
SCALE_QK = 1.0 / math.sqrt(DQK)
EPS_LN = 1e-5
EPS_RMS = 1.1920928955078125e-07
NEG_BIG = -1e30
TWO_PI = 2.0 * math.pi
MAGIC = 12582912.0

DBG_DUMP = None
PHASES_LIMIT = 99


_const_cache = {}

def host_constants():
    if _const_cache:
        return _const_cache
    trow = np.broadcast_to(np.arange(T, dtype=np.float64), (128, T)).astype(f32).copy()
    freqs = 1.0 / (10000.0 ** (np.arange(0, DROPE, 2, dtype=np.float64) / DROPE))
    tpos = np.arange(T, dtype=np.float64)[:, None] * freqs[None, :]
    ropec = np.tile(np.cos(tpos), (1, NH)).astype(f32)
    ropes = np.tile(np.sin(tpos), (1, NH)).astype(f32)
    m = np.arange(1024) - 384
    tri = np.where(np.arange(128)[:, None] > m[None, :], f32(NEG_BIG), f32(0.0)).astype(f32)
    onesbf = np.ones((128, 128), dtype=bf16)
    onesf = np.ones((128, 128), dtype=f32)
    identbf = np.eye(128, dtype=bf16)
    identf = np.eye(128, dtype=f32)
    sel2 = np.zeros((NE, 128), dtype=bf16)
    for e in range(NE):
        sel2[e, e * RL:(e + 1) * RL] = 2.0
    sel2f = sel2.astype(f32)
    qropec = (ropec.astype(np.float64) * SCALE_QK).astype(f32)
    qropes = (ropes.astype(np.float64) * SCALE_QK).astype(f32)
    _const_cache.update(dict(QROPEC=qropec, QROPES=qropes,
                             TROW=trow, ROPEC=ropec, ROPES=ropes, TRI=tri,
                             ONESBF=onesbf, ONESF=onesf, IDENTBF=identbf,
                             IDENTF=identf, SEL2=sel2, SEL2F=sel2f))
    return _const_cache


class Sem:
    def __init__(self, nc, name):
        self.h = nc.alloc_semaphore(name)
        self.n = 0

    def dinc(self, inst):
        inst.then_inc(self.h, 16)
        self.n += 16
        return self.n

    def einc(self, inst):
        inst.then_inc(self.h, 1)
        self.n += 1
        return self.n


class PSem:

    def __init__(self, c, a, b):
        self.s = [c[a], c[b]]
        self.tgt = {}

    @staticmethod
    def _par(key):
        if isinstance(key, int):
            return key % 2
        if isinstance(key, str):
            return 0
        return key[0] % 2

    def pre(self, keys):
        cnt = [self.s[0].n, self.s[1].n]
        for key in keys:
            p = self._par(key)
            cnt[p] += 16
            self.tgt[key] = (p, cnt[p])

    def issue(self, key, inst):
        p = self._par(key)
        n = self.s[p].dinc(inst)
        if key in self.tgt:
            assert self.tgt[key] == (p, n) or self.tgt[key][1] >= n, (key, self.tgt[key], n)
        self.tgt[key] = (p, max(n, self.tgt.get(key, (p, 0))[1]))
        return n

    def wait(self, eng, key):
        p, n = self.tgt[key]
        eng.wait_ge(self.s[p].h, n)


def mk_chain(eng, sem):
    def _c(inst):
        n = sem.einc(inst)
        eng.wait_ge(sem.h, n)
        return inst
    return _c


class Arena:
    def __init__(self, nc, f32_cols):
        self.t = nc.alloc_sbuf_tensor("arena", [128, f32_cols], F32)
        self.cols = f32_cols
        self.off = 0

    def reset(self):
        self.off = 0

    def alloc(self, parts, cols, dt=F32):
        per = 4 if dt == F32 else 2
        ncol_f32 = (cols * per + 3) // 4
        self.off = (self.off + 7) & ~7
        assert self.off + ncol_f32 <= self.cols, \
            f"arena overflow {self.off}+{ncol_f32}>{self.cols}"
        base = self.off
        self.off += ncol_f32
        if dt != F32:
            return self.t[:].bitcast(dt)[0:parts, base * 2: base * 2 + cols]
        return self.t[0:parts, base: base + cols]


def build_program(n_phases=99, dump=None):
    nc = bass.Bass("TRN2", target_bir_lowering=False)
    c = {}
    c["nc"] = nc

    def dram(name, shape, dt, kind):
        h = nc.dram_tensor(name, shape, dt, kind=kind)
        c[name] = h
        return h

    dram("x", [NTOK, D], F32, "ExternalInput")
    dram("nu_log", [NST], F32, "ExternalInput")
    dram("theta_log", [NST], F32, "ExternalInput")
    dram("B_re", [NST, D], F32, "ExternalInput")
    dram("B_im", [NST, D], F32, "ExternalInput")
    dram("C_re", [D, NST], F32, "ExternalInput")
    dram("C_im", [D, NST], F32, "ExternalInput")
    dram("Dvec", [D], F32, "ExternalInput")
    dram("proj1", [HFF, D], F32, "ExternalInput")
    dram("proj1_bias", [HFF], F32, "ExternalInput")
    dram("proj2", [D, HFF], F32, "ExternalInput")
    dram("proj2_bias", [D], F32, "ExternalInput")
    dram("Wq", [D, NH * DQK], F32, "ExternalInput")
    dram("Wkv_a", [D, KVL + DROPE], F32, "ExternalInput")
    dram("kv_norm_w", [KVL], F32, "ExternalInput")
    dram("Wkv_b", [KVL, NH * (DNOPE + DV)], F32, "ExternalInput")
    dram("Wo", [NH * DV, D], F32, "ExternalInput")
    dram("base_w", [D, D], F32, "ExternalInput")
    dram("base_b", [D], F32, "ExternalInput")
    dram("router_w", [NE, D], F32, "ExternalInput")
    dram("router_b", [NE], F32, "ExternalInput")
    dram("lora_A", [NE, D, RL], F32, "ExternalInput")
    dram("lora_B", [NE, RL, D], F32, "ExternalInput")
    dram("TROW", [128, T], F32, "ExternalInput")
    dram("ROPEC", [T, 512], F32, "ExternalInput")
    dram("QROPEC", [T, 512], F32, "ExternalInput")
    dram("QROPES", [T, 512], F32, "ExternalInput")
    dram("ROPES", [T, 512], F32, "ExternalInput")
    dram("TRI", [128, 1024], F32, "ExternalInput")
    dram("ONESBF", [128, 128], BF16, "ExternalInput")
    dram("ONESF", [128, 128], F32, "ExternalInput")
    dram("IDENTBF", [128, 128], BF16, "ExternalInput")
    dram("IDENTF", [128, 128], F32, "ExternalInput")
    dram("SEL2", [NE, 128], BF16, "ExternalInput")
    dram("SEL2F", [NE, 128], F32, "ExternalInput")
    dram("s_p1bf", [HFF, D], BF16, "Internal")
    dram("s_p2bf", [D, HFF], BF16, "Internal")
    dram("s_brebf", [NST, D], BF16, "Internal")
    dram("s_bimbf", [NST, D], BF16, "Internal")
    dram("s_crebf", [D, NST], BF16, "Internal")
    dram("s_cimbf", [D, NST], BF16, "Internal")
    dram("s_bwbf", [D, D], BF16, "Internal")
    dram("s_rwbf", [16, D], BF16, "Internal")
    dram("s_wqbf", [D, NH * DQK], BF16, "Internal")
    dram("s_wkvbbf", [KVL, NH * (DNOPE + DV)], BF16, "Internal")
    dram("s_xn0", [NTOK, D], BF16, "Internal")
    dram("s_g0", [NTOK, D], F32, "Internal")
    dram("s_x1", [NTOK, D], F32, "Internal")
    dram("s_xn1", [NTOK, D], BF16, "Internal")
    dram("s_qnope", [NH * DNOPE, NTOK], BF16, "Internal")
    dram("s_qrope", [NTOK, NH * 128], BF16, "Internal")
    dram("s_krope", [NTOK, 128], BF16, "Internal")
    dram("s_ckvn", [NTOK, KVL], BF16, "Internal")
    dram("s_knope", [SEQS * NH * DNOPE, T], BF16, "Internal")
    dram("s_v", [NTOK, NH * DV], BF16, "Internal")
    dram("s_o", [SEQS * NH * DV, T], BF16, "Internal")
    dram("s_s1", [NTOK, D], F32, "Internal")
    dram("s_g1", [NTOK, D], F32, "Internal")
    dram("out", [NTOK, D], F32, "ExternalOutput")

    c["slab"] = [nc.alloc_sbuf_tensor(f"slab{i}", [128, NTOK], F32) for i in range(4)]
    c["Hsb"] = [nc.alloc_sbuf_tensor(f"hsb{i}", [128, NTOK], BF16) for i in range(4)]
    for nm, shape, dt in [("st_m0", [128, NSLICE], F32), ("st_r0", [128, NSLICE], F32),
                          ("st_m1", [128, NSLICE], F32), ("st_r1", [128, NSLICE], F32),
                          ("v_r", [128, 2], F32), ("v_gam", [128, 2], F32),
                          ("v_th", [128, 2], F32),
                          ("b1_sb", [128, HFF // 128], F32),
                          ("b2row", [1, D], BF16), ("bbrow", [1, D], BF16),
                          ("rbrow", [1, NE], BF16), ("drow", [1, D], F32),
                          ("kvnw_sb", [128, KVL // 128], F32),
                          ("k_ones_bf", [128, 128], BF16), ("k_ones_f", [128, 128], F32),
                          ("k_id_bf", [128, 128], BF16), ("k_id_f", [128, 128], F32),
                          ("k_sel2", [NE, 128], BF16),
                          ("k_sel2f", [NE, 128], F32)]:
        c[nm] = nc.alloc_sbuf_tensor(nm, shape, dt)
    c["psA"] = nc.alloc_psum_tensor("psA", [128, 2048], F32)
    c["psB"] = nc.alloc_psum_tensor("psB", [128, 2048], F32)
    c["arena"] = Arena(nc, 23700)

    for s in ["sem_ld", "sem_ldB", "sem_ld2", "sem_gp", "sem_gpB", "sem_st",
              "sem_stB", "sem_pe", "sem_dve", "sem_dve2", "sem_act", "sem_pool",
              "sem_vself", "sem_sself", "sem_p1a", "sem_p1b", "sem_p2a", "sem_p2b",
              "sem_st2a", "sem_st2b", "sem_p3a", "sem_p3b"]:
        c[s] = Sem(nc, s)

    dump_map = {"xn0": "s_xn0", "g0": "s_g0", "x1": "s_x1", "xn1": "s_xn1",
                "qnope": "s_qnope", "qrope": "s_qrope", "g1": "s_g1",
                "out": "out", "krope": "s_krope", "ckvn": "s_ckvn",
                "knope": "s_knope", "v": "s_v", "o": "s_o", "s1": "s_s1",
                "p1bf": "s_p1bf", "brebf": "s_brebf"}

    phases = [ph_prologue, ph_wkvbfold, ph_ln0, ph_bu, ph_scan, ph_ffn0,
              ph_cm, ph_ln1, ph_q, ph_kv, ph_kvb, ph_attn, ph_wo, ph_ffn1,
              ph_out]
    for ph in phases[:n_phases]:
        c["arena"].reset()
        with nc.named_scope(ph.__name__):
            with nc.Block() as block:
                ph(c, block)

    if dump is not None:
        sem_st = c["sem_st"]
        if dump in ("bu", "hsb"):
            tl = c["slab"] if dump == "bu" else c["Hsb"]
            dt_ = F32 if dump == "bu" else BF16
            ddbg = nc.dram_tensor("dbg", [4 * 128, NTOK], dt_, kind="ExternalOutput")
            with nc.Block() as block:
                @block.sync
                def _(sync):
                    for k in range(4):
                        sem_st.dinc(sync.dma_start(
                            ddbg.ap()[k * 128:(k + 1) * 128, :], tl[k][:]))
                    sync.wait_ge(sem_st.h, sem_st.n)
        else:
            src = c[dump_map[dump]]
            ddbg = nc.dram_tensor("dbg", list(src.shape), src.dtype, kind="ExternalOutput")
            with nc.Block() as block:
                @block.sync
                def _(sync):
                    sem_st.dinc(sync.dma_start(ddbg.ap(), src.ap()))
                    sync.wait_ge(sem_st.h, sem_st.n)
    return nc


def ph_prologue(c, block):
    arena = c["arena"]
    sem_ld2 = c["sem_ld2"]
    sem_dve, sem_act, sem_pe = c["sem_dve"], c["sem_act"], c["sem_pe"]
    gpP = PSem(c, "sem_gp", "sem_gpB")
    stP = PSem(c, "sem_st", "sem_stB")
    psA = c["psA"]

    casts = [(c["Wq"], c["s_wqbf"], D, NH * DQK),
             (c["proj1"], c["s_p1bf"], HFF, D),
             (c["proj2"], c["s_p2bf"], D, HFF),
             (c["base_w"], c["s_bwbf"], D, D),
             (c["B_re"], c["s_brebf"], NST, D),
             (c["B_im"], c["s_bimbf"], NST, D),
             (c["C_re"], c["s_crebf"], D, NST),
             (c["C_im"], c["s_cimbf"], D, NST),
             (c["router_w"], c["s_rwbf"], NE, D)]
    jobs = []
    for (src_, dst, rows, cols) in casts:
        for r0 in range(0, rows, 128):
            rr = min(128, rows - r0)
            jobs.append((src_, dst, r0, rr, cols))
    bufs = [arena.alloc(128, 8192, BF16) for _ in range(2)]
    gpP.pre(["z"] + list(range(len(jobs))))
    stP.pre(range(len(jobs)))

    zpad = arena.alloc(8, D, BF16)

    @block.gpsimd
    def _(gp):
        n_ = c["sem_pool"].einc(gp.memset(zpad[:], 0.0))
        gp.wait_ge(c["sem_pool"].h, n_)
        gpP.issue("z", gp.dma_start(c["s_rwbf"].ap()[8:16, :], zpad[:]))
        for i, (src_, dst, r0, rr, cols) in enumerate(jobs):
            if i >= 2:
                stP.wait(gp, i - 2)
            gpP.issue(i, gp.dma_start(bufs[i % 2][0:rr, 0:cols],
                                      src_.ap()[r0:r0 + rr, :]))

    @block.sync
    def _(sync):
        for nm, src_ in [("k_ones_bf", "ONESBF"), ("k_ones_f", "ONESF"),
                         ("k_id_bf", "IDENTBF"), ("k_id_f", "IDENTF"),
                         ("k_sel2", "SEL2"), ("k_sel2f", "SEL2F")]:
            sem_ld2.dinc(sync.dma_start(c[nm][:], c[src_].ap()))
        for i, (src_, dst, r0, rr, cols) in enumerate(jobs):
            gpP.wait(sync, i)
            stP.issue(i, sync.dma_start(dst.ap()[r0:r0 + rr, :],
                                        bufs[i % 2][0:rr, 0:cols]))
        sync.wait_ge(stP.s[0].h, stP.s[0].n)
        sync.wait_ge(stP.s[1].h, stP.s[1].n)

    nu2 = arena.alloc(2, 128)
    th2 = arena.alloc(2, 128)
    b1n = arena.alloc(64, 128)
    kvn4 = arena.alloc(4, 128)
    b2f = arena.alloc(1, D)
    bbf_ = arena.alloc(1, D)
    rbf_ = arena.alloc(1, NE)
    tmp = arena.alloc(128, 2)
    tmp2 = arena.alloc(128, 2)
    tmp3 = arena.alloc(128, 2)
    dv0, a0, pe0 = sem_dve.n, sem_act.n, sem_pe.n

    @block.scalar
    def _(sc):
        sem_ld2.dinc(sc.dma_start(nu2[:], c["nu_log"].ap().rearrange("(a b) -> a b", b=128)))
        sem_ld2.dinc(sc.dma_start(th2[:], c["theta_log"].ap().rearrange("(a b) -> a b", b=128)))
        sem_ld2.dinc(sc.dma_start(b1n[:], c["proj1_bias"].ap().rearrange("(a b) -> a b", b=128)))
        sem_ld2.dinc(sc.dma_start(kvn4[:], c["kv_norm_w"].ap().rearrange("(a b) -> a b", b=128)))
        sem_ld2.dinc(sc.dma_start(b2f[:], c["proj2_bias"].ap().rearrange("(o a) -> o a", o=1)))
        sem_ld2.dinc(sc.dma_start(bbf_[:], c["base_b"].ap().rearrange("(o a) -> o a", o=1)))
        sem_ld2.dinc(sc.dma_start(rbf_[:], c["router_b"].ap().rearrange("(o a) -> o a", o=1)))
        sem_ld2.dinc(sc.dma_start(c["drow"][:], c["Dvec"].ap().rearrange("(o a) -> o a", o=1)))

    LD2_ALL = sem_ld2.n

    @block.tensor
    def _(te):
        te.wait_ge(sem_ld2.h, LD2_ALL)
        sem_pe.einc(te.transpose(psA[:, 0:2], nu2[:], c["k_id_f"][0:2, 0:2]))
        sem_pe.einc(te.transpose(psA[:, 8:10], th2[:], c["k_id_f"][0:2, 0:2]))
        sem_pe.einc(te.transpose(psA[:, 16:80], b1n[:], c["k_id_f"][0:64, 0:64]))
        sem_pe.einc(te.transpose(psA[:, 96:100], kvn4[:], c["k_id_f"][0:4, 0:4]))

    @block.vector
    def _(v):
        v.wait_ge(sem_pe.h, pe0 + 4)
        v.tensor_copy(c["b1_sb"][:], psA[:, 16:80])
        v.tensor_copy(c["kvnw_sb"][:], psA[:, 96:100])
        v.tensor_copy(c["b2row"][:], b2f[:])
        v.tensor_copy(c["bbrow"][:], bbf_[:])
        v.tensor_copy(c["rbrow"][:], rbf_[:])
        v.tensor_copy(tmp[:], psA[:, 0:2])
        sem_dve.einc(v.tensor_copy(tmp3[:], psA[:, 8:10]))

    sem_ss = c["sem_sself"]

    def ssw(sc, inst):
        n = sem_ss.einc(inst)
        sc.wait_ge(sem_ss.h, n)

    @block.scalar
    def _(sc):
        sc.wait_ge(sem_dve.h, dv0 + 1)
        ssw(sc, sc.activation(tmp2[:], tmp[:], AF.Abs))
        ssw(sc, sc.activation(tmp2[:], tmp2[:], AF.Exp, scale=np.float32(-1.0)))
        sem_act.einc(sc.activation(tmp2[:], tmp2[:], AF.Ln, bias=np.float32(1.0)))

    @block.vector
    def _(v):
        v.wait_ge(sem_act.h, a0 + 1)
        n_ = c["sem_vself"].einc(v.tensor_scalar(tmp[:], tmp[:], 0.0, None,
                                                 OP.max, OP.bypass))
        v.wait_ge(c["sem_vself"].h, n_)
        sem_dve.einc(v.tensor_tensor(tmp[:], tmp[:], tmp2[:], OP.add))

    @block.scalar
    def _(sc):
        sc.wait_ge(sem_dve.h, dv0 + 2)
        ssw(sc, sc.activation(tmp[:], tmp[:], AF.Exp))
        ssw(sc, sc.activation(c["v_r"][:], tmp[:], AF.Exp, scale=np.float32(-1.0)))
        ssw(sc, sc.activation(c["v_th"][:], tmp3[:], AF.Exp))
        sem_act.einc(sc.activation(tmp2[:], c["v_r"][:], AF.Square))

    @block.vector
    def _(v):
        v.wait_ge(sem_act.h, a0 + 2)
        n_ = c["sem_vself"].einc(v.tensor_scalar(tmp2[:], tmp2[:], -1.0, 1.0,
                                                 OP.mult, OP.add))
        v.wait_ge(c["sem_vself"].h, n_)
        sem_dve.einc(v.tensor_scalar(tmp2[:], tmp2[:], 1e-30, None, OP.max, OP.bypass))

    @block.scalar
    def _(sc):
        sc.wait_ge(sem_dve.h, dv0 + 3)
        sem_act.einc(sc.activation(c["v_gam"][:], tmp2[:], AF.Sqrt))


def _ln_sweep(c, block, src, dst, st_m, st_r):
    arena = c["arena"]
    sem_dve, sem_dve2 = c["sem_dve"], c["sem_dve2"]
    sem_act, sem_pool = c["sem_act"], c["sem_pool"]
    sem_vself = c["sem_vself"]
    ldP = PSem(c, "sem_ld", "sem_ldB")
    stP = PSem(c, "sem_st", "sem_stB")
    xb = [arena.alloc(128, D) for _ in range(2)]
    xnb = [arena.alloc(128, D, BF16) for _ in range(2)]
    sqscr = [arena.alloc(128, D) for _ in range(2)]
    sums = arena.alloc(128, 1)
    sqs = arena.alloc(128, 1)
    mm_ = arena.alloc(128, 1)
    vtmp = arena.alloc(128, NSLICE)
    dv0, dv20, a0, p0 = sem_dve.n, sem_dve2.n, sem_act.n, sem_pool.n
    ldP.pre(range(NSLICE))
    stP.pre(range(NSLICE))

    @block.sync
    def _(sync):
        for i in range(NSLICE + 1):
            if i < NSLICE:
                if i >= 2:
                    sync.wait_ge(sem_dve2.h, dv20 + (i - 1))
                    sync.wait_ge(sem_act.h, a0 + (i - 1))
                ldP.issue(i, sync.dma_start(xb[i % 2][:],
                                            src.ap()[i * TS:(i + 1) * TS, :]))
            if i >= 1:
                sync.wait_ge(sem_dve2.h, dv20 + i)
                stP.issue(i - 1, sync.dma_start(dst.ap()[(i - 1) * TS:i * TS, :],
                                                xnb[(i - 1) % 2][:]))
        sync.wait_ge(stP.s[0].h, stP.s[0].n)
        sync.wait_ge(stP.s[1].h, stP.s[1].n)

    @block.scalar
    def _(sc):
        for i in range(NSLICE):
            ldP.wait(sc, i)
            if i >= 2:
                sc.wait_ge(sem_dve.h, dv0 + (i - 1))
            sem_act.einc(sc.activation(sqscr[i % 2][:], xb[i % 2][:], AF.Square))
            if i >= 1:
                sc.wait_ge(sem_dve.h, dv0 + i)
                n_ = c["sem_sself"].einc(sc.activation(vtmp[:, i - 1:i],
                                                       vtmp[:, i - 1:i], AF.Ln))
                sc.wait_ge(c["sem_sself"].h, n_)
                sem_pool.einc(sc.activation(st_r[:, i - 1:i], vtmp[:, i - 1:i],
                                            AF.Exp, scale=np.float32(-0.5)))
        sc.wait_ge(sem_dve.h, dv0 + NSLICE)
        n_ = c["sem_sself"].einc(sc.activation(vtmp[:, NSLICE - 1:NSLICE],
                                               vtmp[:, NSLICE - 1:NSLICE], AF.Ln))
        sc.wait_ge(c["sem_sself"].h, n_)
        sem_pool.einc(sc.activation(st_r[:, NSLICE - 1:NSLICE],
                                    vtmp[:, NSLICE - 1:NSLICE],
                                    AF.Exp, scale=np.float32(-0.5)))

    def vsw(v, inst):
        n = sem_vself.einc(inst)
        v.wait_ge(sem_vself.h, n)

    def xn_emit(v, i):
        v.wait_ge(sem_pool.h, p0 + i + 1)
        if i >= 2:
            stP.wait(v, i - 2)
        sem_dve2.einc(v.tensor_scalar(xnb[i % 2][:], xb[i % 2][:],
                                      st_m[:, i:i + 1], st_r[:, i:i + 1],
                                      OP.subtract, OP.mult))

    @block.vector
    def _(v):
        for i in range(NSLICE):
            ldP.wait(v, i)
            vsw(v, v.tensor_reduce(sums[:], xb[i % 2][:], AX.X, OP.add))
            if i >= 1:
                xn_emit(v, i - 1)
            v.wait_ge(sem_act.h, a0 + i + 1)
            vsw(v, v.tensor_reduce(sqs[:], sqscr[i % 2][:], AX.X, OP.add))
            vsw(v, v.tensor_scalar(st_m[:, i:i + 1], sums[:], 1.0 / D, None,
                                   OP.mult, OP.bypass))
            vsw(v, v.tensor_tensor(mm_[:], st_m[:, i:i + 1], st_m[:, i:i + 1], OP.mult))
            vsw(v, v.tensor_scalar(vtmp[:, i:i + 1], sqs[:], 1.0 / D, None,
                                   OP.mult, OP.bypass))
            sem_dve.einc(v.tensor_scalar(vtmp[:, i:i + 1],
                                         vtmp[:, i:i + 1], mm_[:], -float(EPS_LN),
                                         OP.subtract, OP.subtract))
        xn_emit(v, NSLICE - 1)


def ph_ln0(c, block):
    _ln_sweep(c, block, c["x"], c["s_xn0"], c["st_m0"], c["st_r0"])


def ph_bu(c, block):
    arena = c["arena"]
    sem_ld2, sem_pe, sem_dve = c["sem_ld2"], c["sem_pe"], c["sem_dve"]
    ldP = PSem(c, "sem_ld", "sem_ldB")
    psq = [c["psA"], c["psB"]]
    btre = arena.alloc(128, 16 * NST, BF16)
    btim = arena.alloc(128, 16 * NST, BF16)
    xf = [arena.alloc(128, 16 * 512, BF16) for _ in range(2)]
    pe0, dv0 = sem_pe.n, sem_dve.n
    cnt = [ldP.s[0].n, ldP.s[1].n]
    for s in range(NSUP):
        cnt[s % 2] += 16 * 16
        ldP.tgt[s] = (s % 2, cnt[s % 2])

    @block.sync
    def _(sync):
        for ci in range(16):
            sem_ld2.dinc(sync.dma_start(btre[:, ci * NST:(ci + 1) * NST],
                                        c["s_brebf"].ap()[:, ci * 128:(ci + 1) * 128],
                                        transpose=True))
            sem_ld2.dinc(sync.dma_start(btim[:, ci * NST:(ci + 1) * NST],
                                        c["s_bimbf"].ap()[:, ci * 128:(ci + 1) * 128],
                                        transpose=True))
        for s in range(NSUP):
            if s >= 2:
                sync.wait_ge(sem_pe.h, pe0 + 4 * (s - 1))
            for ci in range(16):
                ldP.issue(s, sync.dma_start(
                    xf[s % 2][:, ci * 512:(ci + 1) * 512],
                    c["s_xn0"].ap()[s * SUP:(s + 1) * SUP, ci * 128:(ci + 1) * 128],
                    transpose=True))

    LD2_ALL = sem_ld2.n

    @block.tensor
    def _(te):
        te.wait_ge(sem_ld2.h, LD2_ALL)
        for s in range(NSUP):
            ldP.wait(te, s)
            if s >= 2:
                te.wait_ge(sem_dve.h, dv0 + (s - 1))
            ps = psq[s % 2]
            for q, (nch, bt) in enumerate([(0, btre), (0, btim), (1, btre), (1, btim)]):
                for ci in range(16):
                    inst = te.matmul(ps[:, q * 512:(q + 1) * 512],
                                     bt[:, ci * NST + nch * 128: ci * NST + nch * 128 + 128],
                                     xf[s % 2][:, ci * 512:(ci + 1) * 512],
                                     start=(ci == 0), stop=(ci == 15))
                sem_pe.einc(inst)

    @block.vector
    def _(v):
        slab = c["slab"]
        for s in range(NSUP):
            v.wait_ge(sem_pe.h, pe0 + 4 * (s + 1))
            ps = psq[s % 2]
            for q, (nch, comp) in enumerate([(0, 0), (0, 1), (1, 0), (1, 1)]):
                k = 2 * nch + comp
                inst = v.tensor_scalar(slab[k][:, s * SUP:(s + 1) * SUP],
                                       ps[:, q * 512:(q + 1) * 512],
                                       c["v_gam"][:, nch:nch + 1], None,
                                       OP.mult, OP.bypass)
            sem_dve.einc(inst)


def ph_scan(c, block):
    arena = c["arena"]
    sem_ld2, sem_dve, sem_act = c["sem_ld2"], c["sem_dve"], c["sem_act"]
    trow = arena.alloc(128, T)
    COS = arena.alloc(128, T)
    SIN = arena.alloc(128, T)
    PH = arena.alloc(128, T)
    PHR = arena.alloc(128, T)
    RT = arena.alloc(128, T)
    CRE = arena.alloc(128, T)
    CIM = arena.alloc(128, T)
    TMP = arena.alloc(128, T)
    dv0, a0 = sem_dve.n, sem_act.n
    slab, Hsb = c["slab"], c["Hsb"]

    @block.sync
    def _(sync):
        sem_ld2.dinc(sync.dma_start(trow[:], c["TROW"].ap()))
    LD2_ALL = sem_ld2.n

    def table(v, V, base_shift):
        V(v.tensor_scalar(PHR[:], PH[:], float(base_shift), None, OP.add, OP.bypass))
        V(v.tensor_scalar(TMP[:], PHR[:], 1.0 / TWO_PI, MAGIC, OP.mult, OP.add))
        V(v.tensor_scalar(TMP[:], TMP[:], MAGIC, None, OP.subtract, OP.bypass))
        V(v.scalar_tensor_tensor(PHR[:], TMP[:], -TWO_PI, PHR[:], OP.mult, OP.add))
        V(v.tensor_scalar(PHR[:], PHR[:], float(math.pi), None, OP.min, OP.bypass))
        return v.tensor_scalar(PHR[:], PHR[:], -float(math.pi), None, OP.max, OP.bypass)

    for nch in range(2):
        @block.vector
        def _(v, nch=nch):
            V = mk_chain(v, c["sem_vself"])
            if nch == 0:
                v.wait_ge(sem_ld2.h, LD2_ALL)
            else:
                v.wait_ge(sem_act.h, a0 + 2)
            V(v.tensor_scalar(PH[:], trow[:], c["v_th"][:, nch:nch + 1], None,
                              OP.mult, OP.bypass))
            sem_dve.einc(table(v, V, 0.0))
            v.wait_ge(sem_act.h, a0 + 2 * nch + 1)
            sem_dve.einc(table(v, V, math.pi / 2.0))
            V(v.tensor_scalar(RT[:], trow[:], 0.0, c["v_r"][:, nch:nch + 1],
                              OP.mult, OP.add))
            v.wait_ge(sem_act.h, a0 + 2 * nch + 2)
            for seq in range(SEQS):
                cols = slice(seq * T, (seq + 1) * T)
                bu_re = slab[2 * nch + 0][:, cols]
                bu_im = slab[2 * nch + 1][:, cols]
                V(v.tensor_tensor(CRE[:], COS[:], bu_re, OP.mult))
                V(v.tensor_tensor(TMP[:], SIN[:], bu_im, OP.mult))
                V(v.tensor_tensor(CRE[:], CRE[:], TMP[:], OP.add))
                V(v.tensor_tensor(CIM[:], COS[:], bu_im, OP.mult))
                V(v.tensor_tensor(TMP[:], SIN[:], bu_re, OP.mult))
                V(v.tensor_tensor(CIM[:], CIM[:], TMP[:], OP.subtract))
                V(v.tensor_tensor_scan(bu_re, RT[:], CRE[:], 0.0, OP.mult, OP.add))
                V(v.tensor_tensor_scan(bu_im, RT[:], CIM[:], 0.0, OP.mult, OP.add))
                g_re, g_im = bu_re, bu_im
                V(v.tensor_tensor(CRE[:], COS[:], g_re, OP.mult))
                V(v.tensor_tensor(TMP[:], SIN[:], g_im, OP.mult))
                V(v.tensor_tensor(Hsb[2 * nch + 0][:, cols], CRE[:], TMP[:], OP.subtract))
                V(v.tensor_tensor(CIM[:], COS[:], g_im, OP.mult))
                V(v.tensor_tensor(TMP[:], SIN[:], g_re, OP.mult))
                V(v.scalar_tensor_tensor(Hsb[2 * nch + 1][:, cols], CIM[:], -1.0,
                                         TMP[:], OP.mult, OP.subtract))

        @block.scalar
        def _(sc, nch=nch):
            sc.wait_ge(sem_dve.h, dv0 + 2 * nch + 1)
            sem_act.einc(sc.activation(SIN[:], PHR[:], AF.Sin))
            sc.wait_ge(sem_dve.h, dv0 + 2 * nch + 2)
            sem_act.einc(sc.activation(COS[:], PHR[:], AF.Sin))


def ph_ffn0(c, block):
    arena = c["arena"]
    sem_pe, sem_act = c["sem_pe"], c["sem_act"]
    ldP = PSem(c, "sem_ld", "sem_ldB")
    p1P = PSem(c, "sem_p1a", "sem_p1b")
    p2P = PSem(c, "sem_p2a", "sem_p2b")
    p3P = PSem(c, "sem_p3a", "sem_p3b")
    stP = PSem(c, "sem_st", "sem_stB")
    slab = c["slab"]
    hsl = [t[:].bitcast(BF16) for t in slab]

    def hch(hc):
        return hsl[hc // 16][:, (hc % 16) * 512:(hc % 16 + 1) * 512]

    xnb = [arena.alloc(128, 16 * 512, BF16) for _ in range(2)]
    p1b = [arena.alloc(128, 16 * 512, BF16) for _ in range(2)]
    p2b = [arena.alloc(128, 2048, BF16) for _ in range(4)]
    gb = [arena.alloc(128, 2048)]
    psA, psB = c["psA"], c["psB"]
    p0, a0 = sem_pe.n, sem_act.n
    PE_S = 194
    ACT_S = 68
    ldP.pre([s for s in range(NSUP) for _ in range(16)])
    p1P.pre([g for g in range(NSUP * 16) for _ in range(16)])
    p2P.pre([g for g in range(NSUP * 2 * 64) if (g // 2) % 2 == 0])
    p3P.pre([g for g in range(NSUP * 2 * 64) if (g // 2) % 2 == 1])
    stP.pre(range(NSUP * 4))

    def pe_t(s, k):
        return p0 + s * PE_S + k

    def act_t(s, k):
        return a0 + s * ACT_S + k

    @block.sync
    def _(sync):
        for s in range(NSUP):
            if s >= 2:
                sync.wait_ge(sem_pe.h, pe_t(s - 2, 64))
            for dc in range(16):
                ldP.issue(s, sync.dma_start(
                    xnb[s % 2][:, dc * 512:(dc + 1) * 512],
                    c["s_xn0"].ap()[s * SUP:(s + 1) * SUP, dc * 128:(dc + 1) * 128],
                    transpose=True))
            for hg in range(16):
                g = s * 16 + hg
                if g >= 2:
                    sync.wait_ge(sem_pe.h, pe_t((g - 2) // 16, 4 * ((g - 2) % 16) + 4))
                for dc in range(16):
                    p1P.issue(g, sync.dma_start(
                        p1b[g % 2][:, dc * 512:(dc + 1) * 512],
                        c["s_p1bf"].ap()[hg * 512:(hg + 1) * 512, dc * 128:(dc + 1) * 128],
                        transpose=True))
            for pair in range(2):
                for hc in range(64):
                    g2 = (s * 2 + pair) * 64 + hc
                    if g2 >= 4:
                        gp = g2 - 4
                        s_, r_ = gp // 128, gp % 128
                        pair_, hc_ = r_ // 64, r_ % 64
                        sync.wait_ge(sem_pe.h,
                                     pe_t(s_, 64 + pair_ * 65 + hc_ + 1))
                    (p2P if (g2 // 2) % 2 == 0 else p3P).issue(g2, sync.dma_start(
                        p2b[g2 % 4][:],
                        c["s_p2bf"].ap()[:, hc * 128:(hc + 1) * 128],
                        transpose=True))
                for jj in range(2):
                    j = pair * 2 + jj
                    sj = s * 4 + j
                    sync.wait_ge(sem_act.h, act_t(s, 64 + j + 1))
                    stP.issue(sj, sync.dma_start(
                        c["s_g0"].ap()[s * SUP + j * TS: s * SUP + (j + 1) * TS, :],
                        gb[0][:]))
        sync.wait_ge(stP.s[0].h, stP.s[0].n)
        sync.wait_ge(stP.s[1].h, stP.s[1].n)

    @block.tensor
    def _(te):
        for s in range(NSUP):
            ldP.wait(te, s)
            if s >= 1:
                te.wait_ge(sem_act.h, act_t(s - 1, 68))
            for hg in range(16):
                g = s * 16 + hg
                p1P.wait(te, g)
                for k in range(4):
                    hc = hg * 4 + k
                    if hc >= 2:
                        te.wait_ge(sem_act.h, act_t(s, hc - 1))
                    for dc in range(16):
                        inst = te.matmul(psA[:, (hc % 2) * 512:(hc % 2 + 1) * 512],
                                         p1b[g % 2][:, dc * 512 + k * 128: dc * 512 + k * 128 + 128],
                                         xnb[s % 2][:, dc * 512:(dc + 1) * 512],
                                         start=(dc == 0), stop=(dc == 15))
                    sem_pe.einc(inst)
            te.wait_ge(sem_act.h, act_t(s, 64))
            for pair in range(2):
                ps_j = [psB, psA]
                if pair >= 1:
                    te.wait_ge(sem_act.h, act_t(s, 64 + 2 * pair))
                for hc in range(64):
                    g2 = (s * 2 + pair) * 64 + hc
                    (p2P if (g2 // 2) % 2 == 0 else p3P).wait(te, g2)
                    for jj in range(2):
                        j = pair * 2 + jj
                        for bs in range(4):
                            inst = te.matmul(
                                ps_j[jj][:, bs * 512:(bs + 1) * 512],
                                hch(hc)[:, j * 128:(j + 1) * 128],
                                p2b[g2 % 4][:, bs * 512:(bs + 1) * 512],
                                start=(hc == 0), stop=False)
                    sem_pe.einc(inst)
                for jj in range(2):
                    for bs in range(4):
                        inst = te.matmul(ps_j[jj][:, bs * 512:(bs + 1) * 512],
                                         c["k_ones_bf"][0:1, 0:128],
                                         c["b2row"][0:1, bs * 512:(bs + 1) * 512],
                                         start=False, stop=True)
                sem_pe.einc(inst)


    @block.scalar
    def _(sc):
        for s in range(NSUP):
            for hc in range(64):
                sc.wait_ge(sem_pe.h, pe_t(s, hc + 1))
                sem_act.einc(sc.activation(hch(hc), psA[:, (hc % 2) * 512:(hc % 2 + 1) * 512],
                                           AF.Gelu, bias=c["b1_sb"][:, hc:hc + 1]))
            for pair in range(2):
                sc.wait_ge(sem_pe.h, pe_t(s, 64 + (pair + 1) * 65))
                for jj in range(2):
                    j = pair * 2 + jj
                    sj = s * 4 + j
                    if sj >= 1:
                        stP.wait(sc, sj - 1)
                    sem_act.einc(sc.activation(gb[0][:],
                                               [psB, psA][jj][:],
                                               AF.Silu))


def ph_cm(c, block):
    arena = c["arena"]
    sem_ld2, sem_pe, sem_dve = c["sem_ld2"], c["sem_pe"], c["sem_dve"]
    sem_vself = c["sem_vself"]
    ldP = PSem(c, "sem_ld", "sem_ldB")
    stP = PSem(c, "sem_st", "sem_stB")
    Hsb, psA, psB = c["Hsb"], c["psA"], c["psB"]
    ct = arena.alloc(128, 4 * D, BF16)
    xb = [arena.alloc(128, D) for _ in range(2)]
    gbuf = [arena.alloc(128, D) for _ in range(2)]
    x1b = [arena.alloc(128, D) for _ in range(2)]
    XN = arena.alloc(128, D)
    T1 = arena.alloc(128, D)
    dbc = arena.alloc(128, D)
    pe0, dv0 = sem_pe.n, sem_dve.n
    ldP.pre([i for i in range(NSLICE) for _ in range(2)])
    stP.pre(range(NSLICE))

    @block.sync
    def _(sync):
        for nch in range(2):
            for comp, srcn in enumerate(["s_crebf", "s_cimbf"]):
                k = 2 * nch + comp
                sem_ld2.dinc(sync.dma_start(
                    ct[:, k * D:(k + 1) * D],
                    c[srcn].ap()[:, nch * 128:(nch + 1) * 128],
                    transpose=True))
        for i in range(NSLICE):
            if i >= 2:
                sync.wait_ge(sem_dve.h, dv0 + i)
            ldP.issue(i, sync.dma_start(xb[i % 2][:], c["x"].ap()[i * TS:(i + 1) * TS, :]))
            ldP.issue(i, sync.dma_start(gbuf[i % 2][:], c["s_g0"].ap()[i * TS:(i + 1) * TS, :]))
            if i >= 1:
                sync.wait_ge(sem_dve.h, dv0 + i + 1)
                stP.issue(i - 1, sync.dma_start(
                    c["s_x1"].ap()[(i - 1) * TS:i * TS, :], x1b[(i - 1) % 2][:]))
        sync.wait_ge(sem_dve.h, dv0 + NSLICE + 1)
        stP.issue(NSLICE - 1, sync.dma_start(
            c["s_x1"].ap()[(NSLICE - 1) * TS:NSLICE * TS, :], x1b[(NSLICE - 1) % 2][:]))
        sync.wait_ge(stP.s[0].h, stP.s[0].n)
        sync.wait_ge(stP.s[1].h, stP.s[1].n)

    LD2_ALL = sem_ld2.n

    @block.tensor
    def _(te):
        te.wait_ge(sem_ld2.h, LD2_ALL)
        for bs in range(4):
            inst = te.matmul(psA[:, bs * 512:(bs + 1) * 512],
                             c["k_ones_f"][0:1, :],
                             c["drow"][0:1, bs * 512:(bs + 1) * 512],
                             start=True, stop=True)
        sem_pe.einc(inst)
        for i in range(NSLICE):
            if i == 0:
                te.wait_ge(sem_dve.h, dv0 + 1)
            if i >= 2:
                te.wait_ge(sem_dve.h, dv0 + i)
            ps = [psA, psB][i % 2]
            for k in range(4):
                for bs in range(4):
                    inst = te.matmul(ps[:, bs * 512:(bs + 1) * 512],
                                     Hsb[k][:, i * TS:(i + 1) * TS],
                                     ct[:, k * D + bs * 512: k * D + (bs + 1) * 512],
                                     start=(k == 0), stop=(k == 3))
            sem_pe.einc(inst)

    @block.vector
    def _(v):
        V = mk_chain(v, sem_vself)
        v.wait_ge(sem_pe.h, pe0 + 1)
        sem_dve.einc(v.tensor_copy(dbc[:], psA[:]))
        for i in range(NSLICE):
            ldP.wait(v, i)
            v.wait_ge(sem_pe.h, pe0 + i + 2)
            if i >= 2:
                stP.wait(v, i - 2)
            ps = [psA, psB][i % 2]
            V(v.tensor_scalar(XN[:], xb[i % 2][:], c["st_m0"][:, i:i + 1],
                              c["st_r0"][:, i:i + 1], OP.subtract, OP.mult))
            V(v.tensor_tensor(T1[:], XN[:], dbc[:], OP.mult))
            V(v.tensor_tensor(T1[:], T1[:], ps[:], OP.add))
            V(v.tensor_tensor(T1[:], T1[:], gbuf[i % 2][:], OP.mult))
            sem_dve.einc(v.tensor_tensor(x1b[i % 2][:], T1[:], xb[i % 2][:], OP.add))


def ph_ln1(c, block):
    _ln_sweep(c, block, c["s_x1"], c["s_xn1"], c["st_m1"], c["st_r1"])


def ph_wkvbfold(c, block):
    arena = c["arena"]
    sem_dve = c["sem_dve"]
    ldP = PSem(c, "sem_ld", "sem_ldB")
    stP = PSem(c, "sem_st", "sem_stB")
    wb = [arena.alloc(128, NH * (DNOPE + DV)) for _ in range(2)]
    ob = [arena.alloc(128, NH * (DNOPE + DV), BF16) for _ in range(2)]
    dv0 = sem_dve.n
    ldP.pre(range(4))
    stP.pre(range(4))

    @block.sync
    def _(sync):
        for lc in range(4):
            if lc >= 2:
                sync.wait_ge(sem_dve.h, dv0 + lc - 1)
            ldP.issue(lc, sync.dma_start(wb[lc % 2][:],
                                         c["Wkv_b"].ap()[lc * 128:(lc + 1) * 128, :]))
            if lc >= 1:
                sync.wait_ge(sem_dve.h, dv0 + lc)
                stP.issue(lc - 1, sync.dma_start(
                    c["s_wkvbbf"].ap()[(lc - 1) * 128:lc * 128, :], ob[(lc - 1) % 2][:]))
        sync.wait_ge(sem_dve.h, dv0 + 4)
        stP.issue(3, sync.dma_start(c["s_wkvbbf"].ap()[3 * 128:4 * 128, :], ob[1][:]))
        sync.wait_ge(stP.s[0].h, stP.s[0].n)
        sync.wait_ge(stP.s[1].h, stP.s[1].n)

    @block.vector
    def _(v):
        for lc in range(4):
            ldP.wait(v, lc)
            if lc >= 2:
                stP.wait(v, lc - 2)
            sem_dve.einc(v.tensor_scalar(ob[lc % 2][:], wb[lc % 2][:],
                                         c["kvnw_sb"][:, lc:lc + 1], None,
                                         OP.mult, OP.bypass))


def ph_q(c, block):
    arena = c["arena"]
    sem_pe, sem_act, sem_dve2 = c["sem_pe"], c["sem_act"], c["sem_dve2"]
    sem_vself = c["sem_vself"]
    ldP = PSem(c, "sem_ld", "sem_ldB")
    wqP = PSem(c, "sem_p1a", "sem_p1b")
    tbP = PSem(c, "sem_p2a", "sem_p2b")
    stP = PSem(c, "sem_st", "sem_stB")
    st2P = PSem(c, "sem_st2a", "sem_st2b")
    psA, psB = c["psA"], c["psB"]
    xf = [arena.alloc(128, 16 * 512, BF16) for _ in range(2)]
    wqn = [arena.alloc(128, 16 * 128, BF16) for _ in range(2)]
    hb16 = [t[:].bitcast(BF16) for t in c["Hsb"]]

    def wqr_dc(dc):
        return hb16[dc // 4][:, (dc % 4) * 1024:(dc % 4 + 1) * 1024]
    qnb = [arena.alloc(128, 512, BF16) for _ in range(2)]
    qrb = [arena.alloc(128, 2048, BF16) for _ in range(2)]
    tabc = [arena.alloc(128, 512) for _ in range(2)]
    tabs = [arena.alloc(128, 512) for _ in range(2)]
    rt1 = arena.alloc(128, 512)
    rt2 = arena.alloc(128, 512)
    pe0, a0, dv20 = sem_pe.n, sem_act.n, sem_dve2.n
    ldP.pre([s for s in range(NSUP) for _ in range(16)])
    wqP.pre([g for g in range(NSUP * 16) for _ in range(2)])
    tbP.pre([i for i in range(NSLICE) for _ in range(2)])
    stP.pre(range(NSUP * 16))
    st2P.pre(range(NSLICE))
    PE_S = 20
    ACT_S = 16

    @block.sync
    def _(sync):
        for dc in range(16):
            for h in range(16):
                sem_ld2 = c["sem_ld2"]
                sem_ld2.dinc(sync.dma_start(
                    wqr_dc(dc)[:, h * 64:(h + 1) * 64],
                    c["s_wqbf"].ap()[dc * 128:(dc + 1) * 128,
                                     h * DQK + DNOPE: h * DQK + DQK]))
        for s in range(NSUP):
            if s >= 2:
                sync.wait_ge(sem_pe.h, pe0 + PE_S * (s - 1))
            for dc in range(16):
                ldP.issue(s, sync.dma_start(
                    xf[s % 2][:, dc * 512:(dc + 1) * 512],
                    c["s_xn1"].ap()[s * SUP:(s + 1) * SUP, dc * 128:(dc + 1) * 128],
                    transpose=True))
            def qn_store(h):
                k = s * 16 + h
                sync.wait_ge(sem_act.h, a0 + ACT_S * s + h + 1)
                stP.issue(k, sync.dma_start(
                    c["s_qnope"].ap()[h * 128:(h + 1) * 128, s * SUP:(s + 1) * SUP],
                    qnb[h % 2][:]))

            for h in range(16):
                g = s * 16 + h
                if h >= 2:
                    qn_store(h - 2)
                if g >= 2:
                    sync.wait_ge(sem_pe.h, pe0 + PE_S * ((g - 2) // 16) + ((g - 2) % 16) + 1)
                for half2 in range(2):
                    wqP.issue(g, sync.dma_start(
                        wqn[g % 2][:].rearrange("p (dc z) -> p dc z", z=128)[
                            :, half2 * 8:(half2 + 1) * 8, :],
                        c["s_wqbf"].ap()[half2 * 1024:(half2 + 1) * 1024,
                                         h * DQK: h * DQK + DNOPE].rearrange(
                            "(dc p) z -> p dc z", p=128)))
            qn_store(14)
            qn_store(15)
            for j in range(4):
                i = s * 4 + j
                if i >= 2:
                    sync.wait_ge(sem_dve2.h, dv20 + (i - 1))
                pos0 = (i % 16) * 128
                tbP.issue(i, sync.dma_start(tabc[i % 2][:],
                                            c["QROPEC"].ap()[pos0:pos0 + 128, :]))
                tbP.issue(i, sync.dma_start(tabs[i % 2][:],
                                            c["QROPES"].ap()[pos0:pos0 + 128, :]))
            for j in range(4):
                i = s * 4 + j
                sync.wait_ge(sem_dve2.h, dv20 + i + 1)
                st2P.issue(i, sync.dma_start(
                    c["s_qrope"].ap()[i * TS:(i + 1) * TS, :], qrb[j % 2][:]))
        sync.wait_ge(stP.s[0].h, stP.s[0].n)
        sync.wait_ge(stP.s[1].h, stP.s[1].n)
        sync.wait_ge(st2P.s[0].h, st2P.s[0].n)
        sync.wait_ge(st2P.s[1].h, st2P.s[1].n)

    LD2_ALL = c["sem_ld2"].n

    @block.tensor
    def _(te):
        te.wait_ge(c["sem_ld2"].h, LD2_ALL)
        for s in range(NSUP):
            ldP.wait(te, s)
            for h in range(16):
                g = s * 16 + h
                wqP.wait(te, g)
                reg = [psA[:, 0:512], psA[:, 512:1024]][h % 2]
                if g >= 2:
                    te.wait_ge(sem_act.h, a0 + ACT_S * ((g - 2) // 16) + ((g - 2) % 16) + 1)
                for dc in range(16):
                    inst = te.matmul(reg,
                                     wqn[g % 2][:, dc * 128:(dc + 1) * 128],
                                     xf[s % 2][:, dc * 512:(dc + 1) * 512],
                                     start=(dc == 0), stop=(dc == 15))
                sem_pe.einc(inst)
            for j in range(4):
                i = s * 4 + j
                reg = [psB[:, 0:1024], psB[:, 1024:2048]][j % 2]
                if i >= 2:
                    te.wait_ge(sem_dve2.h, dv20 + (i - 1))
                for half in range(2):
                    for dc in range(16):
                        inst = te.matmul(
                            reg[:, half * 512:(half + 1) * 512],
                            xf[s % 2][:, dc * 512 + j * 128: dc * 512 + j * 128 + 128],
                            wqr_dc(dc)[:, half * 512:(half + 1) * 512],
                            start=(dc == 0), stop=(dc == 15))
                sem_pe.einc(inst)

    @block.scalar
    def _(sc):
        for s in range(NSUP):
            for h in range(16):
                g = s * 16 + h
                k = s * 16 + h
                sc.wait_ge(sem_pe.h, pe0 + PE_S * s + h + 1)
                if g >= 2:
                    kp = k - 2 if h >= 2 else (s - 1) * 16 + 14 + h
                    stP.wait(sc, kp)
                reg = [psA[:, 0:512], psA[:, 512:1024]][h % 2]
                sem_act.einc(sc.activation(qnb[h % 2][:], reg, AF.Copy,
                                           bias=0.0, scale=np.float32(SCALE_QK)))

    @block.vector
    def _(v):
        V = mk_chain(v, sem_vself)
        V(v.memset(qrb[0][:], 0.0))
        V(v.memset(qrb[1][:], 0.0))
        for s in range(NSUP):
            for j in range(4):
                i = s * 4 + j
                v.wait_ge(sem_pe.h, pe0 + PE_S * s + 16 + j + 1)
                tbP.wait(v, i)
                if i >= 2:
                    st2P.wait(v, i - 2)
                reg = [psB[:, 0:1024], psB[:, 1024:2048]][j % 2]
                rr = reg.rearrange("p (h q) -> p h q", q=64)
                x1 = rr.rearrange("p h (j t) -> p h j t", t=2)[:, :, :, 0:1].squeeze()
                x2 = rr.rearrange("p h (j t) -> p h j t", t=2)[:, :, :, 1:2].squeeze()
                ctab = tabc[i % 2][:].rearrange("p (h j) -> p h j", j=32)
                stab = tabs[i % 2][:].rearrange("p (h j) -> p h j", j=32)
                r1v = rt1[:].rearrange("p (h j) -> p h j", j=32)
                r2v = rt2[:].rearrange("p (h j) -> p h j", j=32)
                ov = qrb[j % 2][:].rearrange("p (h q) -> p h q", q=128)[:, :, 0:64]
                o1 = ov.rearrange("p h (j t) -> p h j t", t=2)[:, :, :, 0:1].squeeze()
                o2 = ov.rearrange("p h (j t) -> p h j t", t=2)[:, :, :, 1:2].squeeze()
                V(v.tensor_tensor(r1v, x1, ctab, OP.mult))
                V(v.tensor_tensor(r2v, x2, stab, OP.mult))
                V(v.tensor_tensor(o1, r1v, r2v, OP.subtract))
                V(v.tensor_tensor(r1v, x1, stab, OP.mult))
                V(v.tensor_tensor(r2v, x2, ctab, OP.mult))
                n_ = sem_dve2.einc(v.tensor_tensor(o2, r1v, r2v, OP.add))
                v.wait_ge(sem_dve2.h, n_)


def ph_kv(c, block):
    arena = c["arena"]
    sem_pe, sem_act, sem_dve, sem_dve2 = (c["sem_pe"], c["sem_act"],
                                          c["sem_dve"], c["sem_dve2"])
    sem_vself, sem_sself = c["sem_vself"], c["sem_sself"]
    ldP = PSem(c, "sem_ld", "sem_ldB")
    tbP = PSem(c, "sem_p2a", "sem_p2b")
    stP = PSem(c, "sem_st", "sem_stB")
    st2P = PSem(c, "sem_st2a", "sem_st2b")
    sem_ld2 = c["sem_ld2"]
    psA, psB = c["psA"], c["psB"]
    xf = [arena.alloc(128, 16 * 512, BF16) for _ in range(2)]
    wkva = arena.alloc(128, 16 * (KVL + DROPE), BF16)
    ckb = [arena.alloc(128, KVL, BF16) for _ in range(2)]
    krb = [arena.alloc(128, 128, BF16) for _ in range(2)]
    tabc = [arena.alloc(128, 32) for _ in range(2)]
    tabs = [arena.alloc(128, 32) for _ in range(2)]
    scr = arena.alloc(128, KVL)
    ssq = arena.alloc(128, 1)
    rstd = arena.alloc(128, 1)
    rt1 = arena.alloc(128, 32)
    rt2 = arena.alloc(128, 32)
    pe0, a0, dv0, dv20 = sem_pe.n, sem_act.n, sem_dve.n, sem_dve2.n
    ldP.pre([s for s in range(NSUP) for _ in range(16)])
    tbP.pre([i for i in range(NSLICE) for _ in range(2)])
    stP.pre(range(NSLICE))
    st2P.pre(range(NSLICE))

    @block.gpsimd
    def _(gp):
        for dc in range(16):
            c["sem_gp"].dinc(gp.dma_start(
                wkva[:, dc * 576:(dc + 1) * 576],
                c["Wkv_a"].ap()[dc * 128:(dc + 1) * 128, :]))
    GP_ALL = c["sem_gp"].n

    @block.sync
    def _(sync):
        for s in range(NSUP):
            if s >= 2:
                sync.wait_ge(sem_pe.h, pe0 + 4 * (s - 1))
            for dc in range(16):
                ldP.issue(s, sync.dma_start(
                    xf[s % 2][:, dc * 512:(dc + 1) * 512],
                    c["s_xn1"].ap()[s * SUP:(s + 1) * SUP, dc * 128:(dc + 1) * 128],
                    transpose=True))
            for j in range(4):
                i = s * 4 + j
                if i >= 2:
                    sync.wait_ge(sem_dve2.h, dv20 + (i - 1))
                pos0 = (i % 16) * 128
                tbP.issue(i, sync.dma_start(tabc[i % 2][:],
                                            c["ROPEC"].ap()[pos0:pos0 + 128, 0:32]))
                tbP.issue(i, sync.dma_start(tabs[i % 2][:],
                                            c["ROPES"].ap()[pos0:pos0 + 128, 0:32]))
            for j in range(4):
                i = s * 4 + j
                sync.wait_ge(sem_dve.h, dv0 + 2 * (i + 1))
                stP.issue(i, sync.dma_start(
                    c["s_ckvn"].ap()[i * TS:(i + 1) * TS, :], ckb[j % 2][:]))
                sync.wait_ge(sem_dve2.h, dv20 + i + 1)
                st2P.issue(i, sync.dma_start(
                    c["s_krope"].ap()[i * TS:(i + 1) * TS, :], krb[j % 2][:]))
        sync.wait_ge(stP.s[0].h, stP.s[0].n)
        sync.wait_ge(stP.s[1].h, stP.s[1].n)
        sync.wait_ge(st2P.s[0].h, st2P.s[0].n)
        sync.wait_ge(st2P.s[1].h, st2P.s[1].n)

    @block.tensor
    def _(te):
        te.wait_ge(c["sem_gp"].h, GP_ALL)
        for s in range(NSUP):
            ldP.wait(te, s)
            for j in range(4):
                i = s * 4 + j
                ps = [psA, psB][j % 2]
                if i >= 2:
                    te.wait_ge(sem_dve.h, dv0 + 2 * (i - 1))
                    te.wait_ge(sem_dve2.h, dv20 + i - 1)
                for grp, (c0, cw) in enumerate([(0, 512), (512, 64)]):
                    for dc in range(16):
                        inst = te.matmul(
                            ps[:, c0:c0 + cw],
                            xf[s % 2][:, dc * 512 + j * 128: dc * 512 + j * 128 + 128],
                            wkva[:, dc * 576 + c0: dc * 576 + c0 + cw],
                            start=(dc == 0), stop=(dc == 15))
                sem_pe.einc(inst)

    @block.scalar
    def _(sc):
        SS = mk_chain(sc, sem_sself)
        for s in range(NSUP):
            for j in range(4):
                i = s * 4 + j
                ps = [psA, psB][j % 2]
                sc.wait_ge(sem_pe.h, pe0 + i + 1)
                if i >= 1:
                    sc.wait_ge(sem_dve.h, dv0 + 2 * (i - 1) + 1)
                sem_act.einc(sc.activation(scr[:], ps[:, 0:KVL], AF.Square))
                sc.wait_ge(sem_dve.h, dv0 + 2 * i + 1)
                SS(sc.activation(rstd[:], rstd[:], AF.Ln))
                sem_act.einc(sc.activation(rstd[:], rstd[:], AF.Exp,
                                           scale=np.float32(-0.5)))

    @block.vector
    def _(v):
        V = mk_chain(v, sem_vself)
        V(v.memset(krb[0][:], 0.0))
        V(v.memset(krb[1][:], 0.0))
        for s in range(NSUP):
            for j in range(4):
                i = s * 4 + j
                ps = [psA, psB][j % 2]
                v.wait_ge(sem_act.h, a0 + 2 * i + 1)
                V(v.tensor_reduce(ssq[:], scr[:], AX.X, OP.add))
                n_ = sem_dve.einc(v.tensor_scalar(rstd[:], ssq[:], 1.0 / KVL,
                                                  float(EPS_RMS), OP.mult, OP.add))
                v.wait_ge(sem_dve.h, n_)
                v.wait_ge(sem_act.h, a0 + 2 * i + 2)
                if i >= 2:
                    stP.wait(v, i - 2)
                sem_dve.einc(v.tensor_scalar(ckb[j % 2][:], ps[:, 0:KVL],
                                             rstd[:], None, OP.mult, OP.bypass))
                tbP.wait(v, i)
                if i >= 2:
                    st2P.wait(v, i - 2)
                rr = ps[:, KVL:KVL + DROPE]
                x1 = rr.rearrange("p (j t) -> p j t", t=2)[:, :, 0:1].squeeze()
                x2 = rr.rearrange("p (j t) -> p j t", t=2)[:, :, 1:2].squeeze()
                ov = krb[j % 2][:, 0:DROPE]
                o1 = ov.rearrange("p (j t) -> p j t", t=2)[:, :, 0:1].squeeze()
                o2 = ov.rearrange("p (j t) -> p j t", t=2)[:, :, 1:2].squeeze()
                V(v.tensor_tensor(rt1[:], x1, tabc[i % 2][:], OP.mult))
                V(v.tensor_tensor(rt2[:], x2, tabs[i % 2][:], OP.mult))
                V(v.tensor_tensor(o1, rt1[:], rt2[:], OP.subtract))
                V(v.tensor_tensor(rt1[:], x1, tabs[i % 2][:], OP.mult))
                V(v.tensor_tensor(rt2[:], x2, tabc[i % 2][:], OP.mult))
                n_ = sem_dve2.einc(v.tensor_tensor(o2, rt1[:], rt2[:], OP.add))
                v.wait_ge(sem_dve2.h, n_)


def ph_kvb(c, block):
    arena = c["arena"]
    sem_pe, sem_act, sem_ld2 = c["sem_pe"], c["sem_act"], c["sem_ld2"]
    ldP = PSem(c, "sem_ld", "sem_ldB")
    stP = PSem(c, "sem_st", "sem_stB")
    st2P = PSem(c, "sem_st2a", "sem_st2b")
    psA, psB = c["psA"], c["psB"]
    wkvb = arena.alloc(128, 4 * 4096, BF16)
    ckf = [arena.alloc(128, 4 * 512, BF16) for _ in range(2)]
    knb = [arena.alloc(128, 512, BF16) for _ in range(2)]
    vb = [arena.alloc(128, 2048, BF16) for _ in range(2)]
    pe0, a0 = sem_pe.n, sem_act.n
    ldP.pre([s for s in range(NSUP) for _ in range(4)])
    stP.pre(range(NSUP * 16))
    st2P.pre(range(NSUP * 4))
    PE_S = 20
    ACT_S = 20

    @block.sync
    def _(sync):
        for lc in range(4):
            sem_ld2.dinc(sync.dma_start(wkvb[:, lc * 4096:(lc + 1) * 4096],
                                        c["s_wkvbbf"].ap()[lc * 128:(lc + 1) * 128, :]))
        for s in range(NSUP):
            if s >= 2:
                sync.wait_ge(sem_pe.h, pe0 + PE_S * (s - 1))
            for lc in range(4):
                ldP.issue(s, sync.dma_start(
                    ckf[s % 2][:, lc * 512:(lc + 1) * 512],
                    c["s_ckvn"].ap()[s * SUP:(s + 1) * SUP, lc * 128:(lc + 1) * 128],
                    transpose=True))
            seq, pos = s // 4, (s % 4) * SUP
            for h in range(16):
                k = s * 16 + h
                sync.wait_ge(sem_act.h, a0 + ACT_S * s + h + 1)
                stP.issue(k, sync.dma_start(
                    c["s_knope"].ap()[(seq * NH + h) * DNOPE:(seq * NH + h + 1) * DNOPE,
                                      pos:pos + SUP],
                    knb[h % 2][:]))
            for j in range(4):
                k = s * 4 + j
                sync.wait_ge(sem_act.h, a0 + ACT_S * s + 16 + j + 1)
                st2P.issue(k, sync.dma_start(
                    c["s_v"].ap()[s * SUP + j * TS: s * SUP + (j + 1) * TS, :],
                    vb[j % 2][:]))
        sync.wait_ge(stP.s[0].h, stP.s[0].n)
        sync.wait_ge(stP.s[1].h, stP.s[1].n)
        sync.wait_ge(st2P.s[0].h, st2P.s[0].n)
        sync.wait_ge(st2P.s[1].h, st2P.s[1].n)

    LD2_ALL = sem_ld2.n

    @block.tensor
    def _(te):
        te.wait_ge(sem_ld2.h, LD2_ALL)
        for s in range(NSUP):
            ldP.wait(te, s)
            for h in range(16):
                if h >= 4 or s >= 1:
                    kk = h - 4 if h >= 4 else 12 + h
                    ss = s if h >= 4 else s - 1
                    te.wait_ge(sem_act.h, a0 + ACT_S * ss + kk + 1)
                for lc in range(4):
                    inst = te.matmul(
                        psA[:, (h % 4) * 512:(h % 4 + 1) * 512],
                        wkvb[:, lc * 4096 + h * 256: lc * 4096 + h * 256 + 128],
                        ckf[s % 2][:, lc * 512:(lc + 1) * 512],
                        start=(lc == 0), stop=(lc == 3))
                sem_pe.einc(inst)
            for j in range(4):
                if j >= 1 or s >= 1:
                    jj = j - 1 if j >= 1 else 3
                    ss = s if j >= 1 else s - 1
                    te.wait_ge(sem_act.h, a0 + ACT_S * ss + 16 + jj + 1)
                for bs in range(4):
                    for lc in range(4):
                        vview = wkvb[:, lc * 4096: (lc + 1) * 4096].rearrange(
                            "p (h z) -> p h z", z=256)[:, 4 * bs:4 * bs + 4, 128:256]
                        inst = te.matmul(
                            psB[:, bs * 512:(bs + 1) * 512],
                            ckf[s % 2][:, lc * 512 + j * 128: lc * 512 + j * 128 + 128],
                            vview,
                            start=(lc == 0), stop=(lc == 3))
                sem_pe.einc(inst)

    @block.scalar
    def _(sc):
        for s in range(NSUP):
            for h in range(16):
                k = s * 16 + h
                sc.wait_ge(sem_pe.h, pe0 + PE_S * s + h + 1)
                if k >= 2:
                    kp = k - 2 if h >= 2 else (s - 1) * 16 + 14 + h
                    stP.wait(sc, kp)
                sem_act.einc(sc.activation(knb[h % 2][:],
                                           psA[:, (h % 4) * 512:(h % 4 + 1) * 512],
                                           AF.Copy, bias=0.0))
            for j in range(4):
                k = s * 4 + j
                sc.wait_ge(sem_pe.h, pe0 + PE_S * s + 16 + j + 1)
                if k >= 2:
                    st2P.wait(sc, k - 2)
                sem_act.einc(sc.activation(vb[j % 2][:], psB[:], AF.Copy, bias=0.0))


def _attn_sched():
    seq = []
    nd = d = n = 0
    for qs in range(4):
        for kc in range(4 * qs + 4):
            diag = kc >= 4 * qs
            seq.append((qs, kc, diag, nd, d, n))
            if diag:
                d += 1
            else:
                nd += 1
            n += 1
    return seq

ATTN_SEQ = _attn_sched()
N_EXP = len(ATTN_SEQ)
ACT_G = 3 + N_EXP
POOL_G = 2 + 16


def _pe_vals():
    vals = {}
    v = 0
    for name in ("qs2", "ks2", "bcast"):
        v += 1
        vals[name] = v
    for qs in range(4):
        K = 4 * qs + 4
        evs = [e for e in ATTN_SEQ if e[0] == qs]
        for e in evs:
            v += 1
            vals[("S", e[5])] = v
        for e in evs:
            last = e[1] == K - 1
            v += 2 if last else 1
            vals[("RP", e[5])] = v
            if last:
                vals[("qsdone", qs)] = v
    return vals, v

PE_VALS, PE_G = _pe_vals()


def ph_attn(c, block):
    arena = c["arena"]
    sem_pe, sem_act, sem_dve = c["sem_pe"], c["sem_act"], c["sem_dve"]
    sem_dve2, sem_pool = c["sem_dve2"], c["sem_pool"]
    sem_vself = c["sem_vself"]
    sem_ld2 = c["sem_ld2"]
    ldP = PSem(c, "sem_ld", "sem_ldB")
    stP = PSem(c, "sem_st", "sem_stB")
    psA, psB = c["psA"], c["psB"]
    qn = [arena.alloc(128, T, BF16) for _ in range(2)]
    qr = [arena.alloc(128, T, BF16) for _ in range(2)]
    kn = [arena.alloc(128, T, BF16) for _ in range(2)]
    vv = [arena.alloc(128, 16 * DV, BF16) for _ in range(2)]
    krf = [arena.alloc(128, T, BF16) for _ in range(2)]
    tri = arena.alloc(128, 1024)
    qsq = arena.alloc(128, T, BF16)
    qsq2 = arena.alloc(64, T, BF16)
    attnq = arena.alloc(128, 16 * 512, BF16)
    smsk = [arena.alloc(128, 512) for _ in range(4)]
    rb = arena.alloc(128, 512)
    osb = [arena.alloc(128, 512, BF16) for _ in range(2)]
    mq = arena.alloc(1, 1)
    mk = arena.alloc(1, 1)
    nb = arena.alloc(1, 1)
    nbB = arena.alloc(128, 1)
    pe0, a0, dv0, dv20, p0 = (sem_pe.n, sem_act.n, sem_dve.n, sem_dve2.n,
                              sem_pool.n)
    ldP.pre([g for g in range(32) for _ in range(7)])
    stP.pre(range(32 * 4))

    def actT(g, name, n=0):
        base = a0 + ACT_G * g
        return base + {"sqq": 1, "sqk": 2, "sqrt": 3}.get(name, 4 + n)

    def peT(g, name, n=0, qs=0):
        base = pe0 + PE_G * g
        if name in ("qs2", "ks2", "bcast"):
            return base + PE_VALS[name]
        if name == "qsdone":
            return base + PE_VALS[("qsdone", qs)]
        return base + PE_VALS[(name, n)]

    def poolT(g, name, d=0):
        base = p0 + POOL_G * g
        return base + {"b1": 1, "b2": 2}.get(name, 3 + d)

    def qsdone_before(n):
        cnt = 0
        ncum = 0
        for q in range(4):
            ncum += 4 * q + 4
            if n >= ncum:
                cnt += 1
        return cnt

    @block.sync
    def _(sync):
        sem_ld2.dinc(sync.dma_start(tri[:], c["TRI"].ap()))
        for sq in range(2):
            sem_ld2.dinc(sync.dma_start(
                krf[sq][:], c["s_krope"].ap()[sq * T:(sq + 1) * T, :],
                transpose=True))
        for g in range(32):
            sq, h = g // 16, g % 16
            if g >= 2:
                sync.wait_ge(sem_dve.h, dv0 + 4 * (g - 2) + 4)
            ldP.issue(g, sync.dma_start(
                qn[g % 2][:], c["s_qnope"].ap()[h * 128:(h + 1) * 128,
                                                sq * T:(sq + 1) * T]))
            ldP.issue(g, sync.dma_start(
                qr[g % 2][:], c["s_qrope"].ap()[sq * T:(sq + 1) * T,
                                                h * 128:(h + 1) * 128],
                transpose=True))
            ldP.issue(g, sync.dma_start(
                kn[g % 2][:],
                c["s_knope"].ap()[(sq * NH + h) * DNOPE:(sq * NH + h + 1) * DNOPE, :]))
            for part in range(4):
                ldP.issue(g, sync.dma_start(
                    vv[g % 2][:].rearrange("p (kc z) -> p kc z", z=DV)[
                        :, part * 4:(part + 1) * 4, :],
                    c["s_v"].ap()[sq * T + part * 512: sq * T + (part + 1) * 512,
                                  h * DV:(h + 1) * DV].rearrange(
                        "(kc p) z -> p kc z", p=128)))
            for qs in range(4):
                sync.wait_ge(sem_dve.h, dv0 + 4 * g + qs + 1)
                stP.issue(4 * g + qs, sync.dma_start(
                    c["s_o"].ap()[(sq * NH + h) * DV:(sq * NH + h + 1) * DV,
                                  qs * 512:(qs + 1) * 512],
                    osb[qs % 2][:]))
        sync.wait_ge(stP.s[0].h, stP.s[0].n)
        sync.wait_ge(stP.s[1].h, stP.s[1].n)

    LD2_ALL = sem_ld2.n

    @block.tensor
    def _(te):
        te.wait_ge(sem_ld2.h, LD2_ALL)
        for g in range(32):
            sq = g // 16
            ldP.wait(te, g)
            te.wait_ge(sem_act.h, actT(g, "sqq"))
            for sl in range(4):
                te.matmul(psA[0:1, sl * 512:(sl + 1) * 512],
                          c["k_ones_bf"][:, 0:1],
                          qsq[:, sl * 512:(sl + 1) * 512], start=True, stop=False)
                inst = te.matmul(psA[0:1, sl * 512:(sl + 1) * 512],
                                 c["k_ones_bf"][0:64, 0:1],
                                 qsq2[:, sl * 512:(sl + 1) * 512],
                                 start=False, stop=True)
            sem_pe.einc(inst)
            te.wait_ge(sem_act.h, actT(g, "sqk"))
            for sl in range(4):
                te.matmul(psB[0:1, sl * 512:(sl + 1) * 512],
                          c["k_ones_bf"][:, 0:1],
                          qsq[:, sl * 512:(sl + 1) * 512], start=True, stop=False)
                inst = te.matmul(psB[0:1, sl * 512:(sl + 1) * 512],
                                 c["k_ones_bf"][0:64, 0:1],
                                 qsq2[:, sl * 512:(sl + 1) * 512],
                                 start=False, stop=True)
            sem_pe.einc(inst)
            te.wait_ge(sem_pool.h, poolT(g, "b2"))
            sem_pe.einc(te.matmul(psB[:, 0:1], c["k_ones_f"][0:1, :],
                                  nb[:], start=True, stop=True))
            for qs in range(4):
                K = 4 * qs + 4
                evs = [e for e in ATTN_SEQ if e[0] == qs]
                for (qs_, kc, diag, nd, d, n) in evs:
                    reg = psA[:, (n % 4) * 512: (n % 4 + 1) * 512]
                    if n >= 4:
                        te.wait_ge(sem_act.h, actT(g, "exp", n - 4))
                    te.matmul(reg, kn[g % 2][:, kc * 128:(kc + 1) * 128],
                              qn[g % 2][:, qs * 512:(qs + 1) * 512],
                              start=True, stop=False)
                    sem_pe.einc(te.matmul(reg, krf[sq][0:64, kc * 128:(kc + 1) * 128],
                                          qr[g % 2][0:64, qs * 512:(qs + 1) * 512],
                                          start=False, stop=True))
                sumb = psB[:, (qs % 2) * 1024: (qs % 2) * 1024 + 512]
                ot = psB[:, (qs % 2) * 1024 + 512: (qs % 2) * 1024 + 1024]
                for (qs_, kc, diag, nd, d, n) in evs:
                    te.wait_ge(sem_act.h, actT(g, "exp", n))
                    if qs >= 2 and kc == 0:
                        te.wait_ge(sem_dve.h, dv0 + 4 * g + (qs - 2) + 1)
                    last = kc == K - 1
                    ab = attnq[:, kc * 512:(kc + 1) * 512]
                    te.matmul(sumb, c["k_ones_bf"][:, 0:128], ab,
                              start=(kc == 0), stop=last)
                    inst = te.matmul(
                        ot, vv[g % 2][:, kc * DV:(kc + 1) * DV], ab,
                        start=(kc == 0), stop=last)
                    inst.then_inc(sem_pe.h, 2 if last else 1)
                    sem_pe.n += 2 if last else 1

    @block.scalar
    def _(sc):
        SS = mk_chain(sc, c["sem_sself"])
        for g in range(32):
            ldP.wait(sc, g)
            sc.activation(qsq[:], qn[g % 2][:], AF.Square)
            sem_act.einc(sc.activation(qsq2[:], qr[g % 2][0:64, :], AF.Square))
            te_dummy = None
            sc.wait_ge(sem_pe.h, peT(g, "qs2"))
            sc.activation(qsq[:], kn[g % 2][:], AF.Square)
            sem_act.einc(sc.activation(qsq2[:], krf[g // 16][0:64, :], AF.Square))
            sc.wait_ge(sem_pool.h, poolT(g, "b1"))
            sem_act.einc(sc.activation(nb[:], nb[:], AF.Sqrt))
            for (qs, kc, diag, nd, d, n) in ATTN_SEQ:
                sc.wait_ge(sem_pe.h, peT(g, "S", n))
                sc.wait_ge(sem_dve2.h, dv20 + g + 1)
                if qs >= 1 and kc < 4 * qs:
                    prev_n = [e[5] for e in ATTN_SEQ
                              if e[0] == qs - 1 and e[1] == kc][0]
                    sc.wait_ge(sem_pe.h, peT(g, "RP", prev_n))
                if diag:
                    sc.wait_ge(sem_pool.h, poolT(g, "m", d))
                    src_ap = smsk[d % 4][:]
                else:
                    src_ap = psA[:, (n % 4) * 512: (n % 4 + 1) * 512]
                sem_act.einc(sc.activation(attnq[:, kc * 512:(kc + 1) * 512],
                                           src_ap, AF.Exp, bias=nbB[:]))

    @block.vector
    def _(v):
        V = mk_chain(v, sem_vself)
        for g in range(32):
            v.wait_ge(sem_pe.h, peT(g, "qs2"))
            V(v.tensor_reduce(mq[:], psA[0:1, 0:2048], AX.X, OP.max))
            v.wait_ge(sem_pe.h, peT(g, "ks2"))
            V(v.tensor_reduce(mk[:], psB[0:1, 0:2048], AX.X, OP.max))
            V(v.tensor_tensor(nb[:], mq[:], mk[:], OP.mult))
            n_ = sem_pool.einc(v.tensor_scalar(nb[:], nb[:], 1.02, None,
                                               OP.mult, OP.bypass))
            v.wait_ge(sem_pool.h, n_)
            v.wait_ge(sem_act.h, actT(g, "sqrt"))
            n_ = sem_pool.einc(v.tensor_scalar(nb[:], nb[:], -1.0, None,
                                               OP.mult, OP.bypass))
            v.wait_ge(sem_pool.h, n_)
            v.wait_ge(sem_pe.h, peT(g, "bcast"))
            n_ = sem_dve2.einc(v.tensor_copy(nbB[:], psB[:, 0:1]))
            v.wait_ge(sem_dve2.h, n_)
            for (qs, kc, diag, nd, d, n) in ATTN_SEQ:
                if diag:
                    v.wait_ge(sem_pe.h, peT(g, "S", n))
                    if qs >= 1:
                        prev_n = [e[5] for e in ATTN_SEQ
                                  if e[0] == qs - 1 and e[2] and e[4] % 4 == d % 4][0]
                        v.wait_ge(sem_act.h, actT(g, "exp", prev_n))
                    delta = kc * 128 - qs * 512
                    n_ = sem_pool.einc(v.tensor_tensor(
                        smsk[d % 4][:],
                        psA[:, (n % 4) * 512: (n % 4 + 1) * 512],
                        tri[:, 384 - delta: 384 - delta + 512], OP.add))
                    v.wait_ge(sem_pool.h, n_)
                if kc == 4 * qs + 3:
                    v.wait_ge(sem_pe.h, peT(g, "qsdone", qs=qs))
                    if 4 * g + qs >= 2:
                        stP.wait(v, 4 * g + qs - 2)
                    sumb = psB[:, (qs % 2) * 1024: (qs % 2) * 1024 + 512]
                    ot = psB[:, (qs % 2) * 1024 + 512: (qs % 2) * 1024 + 1024]
                    V(v.reciprocal(rb[:], sumb))
                    sem_dve.einc(v.tensor_tensor(osb[qs % 2][:], ot, rb[:], OP.mult))


def ph_wo(c, block):
    arena = c["arena"]
    sem_pe, sem_act, sem_gp = c["sem_pe"], c["sem_act"], c["sem_gp"]
    ldP = PSem(c, "sem_ld", "sem_ldB")
    stP = PSem(c, "sem_st", "sem_stB")
    psA, psB = c["psA"], c["psB"]
    wo = arena.alloc(128, 16 * 2048, BF16)
    ob = [arena.alloc(128, 16 * 128, BF16) for _ in range(2)]
    s1b = [arena.alloc(128, D) for _ in range(2)]
    pe0, a0 = sem_pe.n, sem_act.n
    ldP.pre([i for i in range(NSLICE) for _ in range(16)])
    stP.pre(range(NSLICE))

    @block.gpsimd
    def _(gp):
        for hc in range(16):
            sem_gp.dinc(gp.dma_start(wo[:, hc * 2048:(hc + 1) * 2048],
                                     c["Wo"].ap()[hc * 128:(hc + 1) * 128, :]))
    GP_ALL = sem_gp.n

    @block.sync
    def _(sync):
        for i in range(NSLICE):
            sq, pos = i // 16, (i % 16) * TS
            if i >= 2:
                sync.wait_ge(sem_pe.h, pe0 + (i - 1))
            for h in range(16):
                ldP.issue(i, sync.dma_start(
                    ob[i % 2][:, h * 128:(h + 1) * 128],
                    c["s_o"].ap()[(sq * NH + h) * DV:(sq * NH + h + 1) * DV,
                                  pos:pos + TS]))
            if i >= 1:
                sync.wait_ge(sem_act.h, a0 + i)
                stP.issue(i - 1, sync.dma_start(
                    c["s_s1"].ap()[(i - 1) * TS:i * TS, :], s1b[(i - 1) % 2][:]))
        sync.wait_ge(sem_act.h, a0 + NSLICE)
        stP.issue(NSLICE - 1, sync.dma_start(
            c["s_s1"].ap()[(NSLICE - 1) * TS:NSLICE * TS, :], s1b[(NSLICE - 1) % 2][:]))
        sync.wait_ge(stP.s[0].h, stP.s[0].n)
        sync.wait_ge(stP.s[1].h, stP.s[1].n)

    @block.tensor
    def _(te):
        te.wait_ge(sem_gp.h, GP_ALL)
        for i in range(NSLICE):
            ldP.wait(te, i)
            if i >= 2:
                te.wait_ge(sem_act.h, a0 + (i - 1))
            ps = [psA, psB][i % 2]
            for hc in range(16):
                for bs in range(4):
                    inst = te.matmul(ps[:, bs * 512:(bs + 1) * 512],
                                     ob[i % 2][:, hc * 128:(hc + 1) * 128],
                                     wo[:, hc * 2048 + bs * 512: hc * 2048 + (bs + 1) * 512],
                                     start=(hc == 0), stop=(hc == 15))
            sem_pe.einc(inst)

    @block.scalar
    def _(sc):
        for i in range(NSLICE):
            sc.wait_ge(sem_pe.h, pe0 + i + 1)
            if i >= 2:
                stP.wait(sc, i - 2)
            ps = [psA, psB][i % 2]
            sem_act.einc(sc.activation(s1b[i % 2][:], ps[:], AF.Copy, bias=0.0))


def ph_ffn1(c, block):
    arena = c["arena"]
    sem_pe, sem_act, sem_dve, sem_dve2 = (c["sem_pe"], c["sem_act"],
                                          c["sem_dve"], c["sem_dve2"])
    sem_vself, sem_sself = c["sem_vself"], c["sem_sself"]
    sem_ld2, sem_gp = c["sem_ld2"], c["sem_gp"]
    ldP = PSem(c, "sem_ld", "sem_ldB")
    stP = PSem(c, "sem_st", "sem_stB")
    psA, psB = c["psA"], c["psB"]
    slabbf = [t[:].bitcast(BF16) for t in c["slab"]]
    hbf = [t[:].bitcast(BF16) for t in c["Hsb"]]

    def bwT_dc(dc):
        return slabbf[dc // 4][:, (dc % 4) * 2048:(dc % 4 + 1) * 2048]

    laA = hbf[0][:, 0:2048]
    laB = hbf[1][:, 0:2048]
    rwT = arena.alloc(128, 16 * 16, BF16)
    xf = [arena.alloc(128, 16 * 512, BF16) for _ in range(2)]
    g1b = [arena.alloc(128, D) for _ in range(2)]
    gmax = arena.alloc(128, 1)
    nmax = arena.alloc(128, 1)
    m2_ = arena.alloc(128, 1)
    den = arena.alloc(128, 1)
    gsum = arena.alloc(128, 1)
    grec = arena.alloc(128, 1)
    gexp = arena.alloc(128, NE)
    gprob = arena.alloc(128, NE)
    gmask = arena.alloc(128, NE)
    ge2 = arena.alloc(128, NE)
    gate = arena.alloc(128, NE)
    gateT = arena.alloc(NE, 128)
    gateB = arena.alloc(128, 128)
    mids = arena.alloc(128, 128, BF16)
    pe0, a0, dv0, dv20 = sem_pe.n, sem_act.n, sem_dve.n, sem_dve2.n
    ldP.pre([s for s in range(NSUP) for _ in range(16)])
    stP.pre(range(NSLICE))
    PE_I, DVE_I, ACT_I = 5, 4, 2

    @block.gpsimd
    def _(gp):
        for dc in range(16):
            sem_gp.dinc(gp.dma_start(
                laA[:, dc * 128:(dc + 1) * 128].rearrange("p (e r) -> p e r", r=RL),
                c["lora_A"].ap().transpose([1, 0, 2])[dc * 128:(dc + 1) * 128, :, :]))
        sem_gp.dinc(gp.dma_start(
            laB[:], c["lora_B"].ap().rearrange("e r d -> (e r) d")))
    GP_ALL = sem_gp.n

    @block.sync
    def _(sync):
        for dc in range(16):
            sem_ld2.dinc(sync.dma_start(bwT_dc(dc),
                                        c["s_bwbf"].ap()[:, dc * 128:(dc + 1) * 128],
                                        transpose=True))
            sem_ld2.dinc(sync.dma_start(rwT[:, dc * 16:(dc + 1) * 16],
                                        c["s_rwbf"].ap()[:, dc * 128:(dc + 1) * 128],
                                        transpose=True))
        for s in range(NSUP):
            if s >= 2:
                sync.wait_ge(sem_pe.h, pe0 + PE_I * 4 * (s - 1))
            for dc in range(16):
                ldP.issue(s, sync.dma_start(
                    xf[s % 2][:, dc * 512:(dc + 1) * 512],
                    c["s_xn1"].ap()[s * SUP:(s + 1) * SUP, dc * 128:(dc + 1) * 128],
                    transpose=True))
            for j in range(4):
                i = s * 4 + j
                sync.wait_ge(sem_act.h, a0 + ACT_I * i + 2)
                stP.issue(i, sync.dma_start(
                    c["s_g1"].ap()[i * TS:(i + 1) * TS, :], g1b[i % 2][:]))
        sync.wait_ge(stP.s[0].h, stP.s[0].n)
        sync.wait_ge(stP.s[1].h, stP.s[1].n)

    LD2_ALL = sem_ld2.n

    @block.tensor
    def _(te):
        te.wait_ge(sem_ld2.h, LD2_ALL)
        te.wait_ge(sem_gp.h, GP_ALL)
        for s in range(NSUP):
            ldP.wait(te, s)
            for j in range(4):
                i = s * 4 + j
                xsl = xf[s % 2]
                if i >= 1:
                    te.wait_ge(sem_act.h, a0 + ACT_I * (i - 1) + 1)
                for dc in range(16):
                    te.matmul(psB[:, 0:8],
                              xsl[:, dc * 512 + j * 128: dc * 512 + j * 128 + 128],
                              rwT[:, dc * 16: dc * 16 + 8],
                              start=(dc == 0), stop=False)
                sem_pe.einc(te.matmul(psB[:, 0:8], c["k_ones_bf"][0:1, 0:128],
                                      c["rbrow"][0:1, :], start=False, stop=True))
                te.wait_ge(sem_dve.h, dv0 + DVE_I * i + 1)
                sem_pe.einc(te.transpose(psB[0:8, 512:640], gate[:], c["k_id_f"][:]))
                te.wait_ge(sem_dve.h, dv0 + DVE_I * i + 2)
                sem_pe.einc(te.matmul(psB[:, 640:768], c["k_sel2f"][:],
                                      gateT[:], start=True, stop=True))
                if i >= 1:
                    te.wait_ge(sem_act.h, a0 + ACT_I * (i - 1) + 2)
                for dc in range(16):
                    for bs in range(4):
                        te.matmul(psA[:, bs * 512:(bs + 1) * 512],
                                  xsl[:, dc * 512 + j * 128: dc * 512 + j * 128 + 128],
                                  bwT_dc(dc)[:, bs * 512:(bs + 1) * 512],
                                  start=(dc == 0), stop=False)
                te.wait_ge(sem_dve.h, dv0 + DVE_I * i + 3)
                for dc in range(16):
                    inst = te.matmul(psB[:, 768:896], laA[:, dc * 128:(dc + 1) * 128],
                                     xsl[:, dc * 512 + j * 128: dc * 512 + j * 128 + 128],
                                     start=(dc == 0), stop=(dc == 15))
                sem_pe.einc(inst)
                te.wait_ge(sem_dve.h, dv0 + DVE_I * i + 4)
                for bs in range(4):
                    te.matmul(psA[:, bs * 512:(bs + 1) * 512], mids[:],
                              laB[:, bs * 512:(bs + 1) * 512], start=False, stop=False)
                for bs in range(4):
                    inst = te.matmul(psA[:, bs * 512:(bs + 1) * 512],
                                     c["k_ones_bf"][0:1, 0:128],
                                     c["bbrow"][0:1, bs * 512:(bs + 1) * 512],
                                     start=False, stop=True)
                sem_pe.einc(inst)

    @block.scalar
    def _(sc):
        for i in range(NSLICE):
            sc.wait_ge(sem_dve2.h, dv20 + i + 1)
            sem_act.einc(sc.activation(gexp[:], psB[:, 0:8], AF.Exp, bias=nmax[:]))
            sc.wait_ge(sem_pe.h, pe0 + PE_I * i + 5)
            if i >= 2:
                stP.wait(sc, i - 2)
            sem_act.einc(sc.activation(g1b[i % 2][:], psA[:], AF.Silu))

    @block.vector
    def _(v):
        V = mk_chain(v, sem_vself)
        for i in range(NSLICE):
            v.wait_ge(sem_pe.h, pe0 + PE_I * i + 1)
            V(v.tensor_reduce(gmax[:], psB[:, 0:8], AX.X, OP.max))
            n_ = sem_dve2.einc(v.tensor_scalar(nmax[:], gmax[:], -1.0, None,
                                               OP.mult, OP.bypass))
            v.wait_ge(sem_dve2.h, n_)
            v.wait_ge(sem_act.h, a0 + ACT_I * i + 1)
            V(v.tensor_reduce(gsum[:], gexp[:], AX.X, OP.add))
            V(v.reciprocal(grec[:], gsum[:]))
            V(v.tensor_scalar(gprob[:], gexp[:], grec[:], None, OP.mult, OP.bypass))
            V(v.tensor_reduce(gmax[:], gprob[:], AX.X, OP.max))
            V(v.tensor_scalar(gmask[:], gprob[:], gmax[:], None, OP.is_ge, OP.bypass))
            V(v.tensor_tensor(gmask[:], gprob[:], gmask[:], OP.subtract))
            V(v.tensor_reduce(m2_[:], gmask[:], AX.X, OP.max))
            V(v.tensor_tensor(den[:], gmax[:], m2_[:], OP.add))
            V(v.reciprocal(den[:], den[:]))
            V(v.tensor_scalar(ge2[:], gprob[:], m2_[:], None, OP.is_ge, OP.bypass))
            V(v.tensor_tensor(gate[:], gprob[:], ge2[:], OP.mult))
            n_ = sem_dve.einc(v.tensor_scalar(gate[:], gate[:], den[:], None,
                                              OP.mult, OP.bypass))
            v.wait_ge(sem_dve.h, n_)
            v.wait_ge(sem_pe.h, pe0 + PE_I * i + 2)
            n_ = sem_dve.einc(v.tensor_copy(gateT[:], psB[0:8, 512:640]))
            v.wait_ge(sem_dve.h, n_)
            v.wait_ge(sem_pe.h, pe0 + PE_I * i + 3)
            n_ = sem_dve.einc(v.tensor_copy(gateB[:], psB[:, 640:768]))
            v.wait_ge(sem_dve.h, n_)
            v.wait_ge(sem_pe.h, pe0 + PE_I * i + 4)
            n_ = sem_dve.einc(v.tensor_tensor(mids[:], psB[:, 768:896], gateB[:],
                                              OP.mult))
            v.wait_ge(sem_dve.h, n_)


def ph_out(c, block):
    arena = c["arena"]
    sem_dve = c["sem_dve"]
    sem_vself = c["sem_vself"]
    ldP = PSem(c, "sem_ld", "sem_ldB")
    stP = PSem(c, "sem_st", "sem_stB")
    x1b = [arena.alloc(128, D) for _ in range(2)]
    s1b = [arena.alloc(128, D) for _ in range(2)]
    g1b = [arena.alloc(128, D) for _ in range(2)]
    outb = [arena.alloc(128, D) for _ in range(2)]
    dv0 = sem_dve.n
    ldP.pre([i for i in range(NSLICE) for _ in range(3)])
    stP.pre(range(NSLICE))

    @block.sync
    def _(sync):
        for i in range(NSLICE):
            if i >= 2:
                sync.wait_ge(sem_dve.h, dv0 + i - 1)
            ldP.issue(i, sync.dma_start(x1b[i % 2][:], c["s_x1"].ap()[i * TS:(i + 1) * TS, :]))
            ldP.issue(i, sync.dma_start(s1b[i % 2][:], c["s_s1"].ap()[i * TS:(i + 1) * TS, :]))
            ldP.issue(i, sync.dma_start(g1b[i % 2][:], c["s_g1"].ap()[i * TS:(i + 1) * TS, :]))
            if i >= 1:
                sync.wait_ge(sem_dve.h, dv0 + i)
                stP.issue(i - 1, sync.dma_start(
                    c["out"].ap()[(i - 1) * TS:i * TS, :], outb[(i - 1) % 2][:]))
        sync.wait_ge(sem_dve.h, dv0 + NSLICE)
        stP.issue(NSLICE - 1, sync.dma_start(
            c["out"].ap()[(NSLICE - 1) * TS:NSLICE * TS, :], outb[(NSLICE - 1) % 2][:]))
        sync.wait_ge(stP.s[0].h, stP.s[0].n)
        sync.wait_ge(stP.s[1].h, stP.s[1].n)

    @block.vector
    def _(v):
        V = mk_chain(v, sem_vself)
        for i in range(NSLICE):
            ldP.wait(v, i)
            if i >= 2:
                stP.wait(v, i - 2)
            V(v.tensor_tensor(outb[i % 2][:], s1b[i % 2][:], g1b[i % 2][:], OP.mult))
            sem_dve.einc(v.tensor_tensor(outb[i % 2][:], outb[i % 2][:],
                                         x1b[i % 2][:], OP.add))


def kernel(**inputs):
    consts = host_constants()
    x = np.ascontiguousarray(inputs["L0_attn_x"])
    nc = build_program(n_phases=PHASES_LIMIT, dump=DBG_DUMP)
    base = {
        "nu_log": inputs["L0_attn_nu_log"], "theta_log": inputs["L0_attn_theta_log"],
        "B_re": inputs["L0_attn_B_re"], "B_im": inputs["L0_attn_B_im"],
        "C_re": inputs["L0_attn_C_re"], "C_im": inputs["L0_attn_C_im"],
        "Dvec": inputs["L0_attn_D"],
        "proj1": inputs["L0_ffn_proj1"], "proj1_bias": inputs["L0_ffn_proj1_bias"],
        "proj2": inputs["L0_ffn_proj2"], "proj2_bias": inputs["L0_ffn_proj2_bias"],
        "Wq": inputs["L1_attn_Wq"], "Wkv_a": inputs["L1_attn_Wkv_a"],
        "kv_norm_w": inputs["L1_attn_kv_norm_w"], "Wkv_b": inputs["L1_attn_Wkv_b"],
        "Wo": inputs["L1_attn_Wo"],
        "base_w": inputs["L1_ffn_base_w"], "base_b": inputs["L1_ffn_base_b"],
        "router_w": inputs["L1_ffn_router_w"], "router_b": inputs["L1_ffn_router_b"],
        "lora_A": inputs["L1_ffn_lora_A"], "lora_B": inputs["L1_ffn_lora_B"],
    }
    base.update(consts)
    in_maps = []
    for cid in range(NCORE):
        m = dict(base)
        m["x"] = np.ascontiguousarray(x[cid * SEQS:(cid + 1) * SEQS].reshape(NTOK, D))
        in_maps.append(m)
    res = run_bass_kernel_spmd(nc, in_maps, core_ids=list(range(NCORE)))
    out = np.empty((BATCH, T, D), dtype=np.float32)
    for cid in range(NCORE):
        out[cid * SEQS:(cid + 1) * SEQS] = res.results[cid]["out"].reshape(SEQS, T, D)
    return out


# version 03
# speedup vs baseline: 1.3161x; 1.0345x over previous
import math
import numpy as np
import ml_dtypes
import concourse.bass as bass
import concourse.mybir as mybir
from concourse.bass_utils import run_bass_kernel_spmd
from concourse.alu_op_type import AluOpType as OP

F32 = mybir.dt.float32
BF16 = mybir.dt.bfloat16
AF = mybir.ActivationFunctionType
AX = mybir.AxisListType
f32 = np.float32
bf16 = ml_dtypes.bfloat16

NCORE = 8
BATCH = 16
SEQS = 2
T = 2048
NTOK = SEQS * T
D = 2048
TS = 128
NSLICE = NTOK // TS
SUP = 512
NSUP = NTOK // SUP
NST = 256
HFF = 8192
NH = 16
DQK = 192
DROPE = 64
DNOPE = 128
KVL = 512
DV = 128
NE = 8
RL = 16
SCALE_QK = 1.0 / math.sqrt(DQK)
EPS_LN = 1e-5
EPS_RMS = 1.1920928955078125e-07
NEG_BIG = -1e30
TWO_PI = 2.0 * math.pi
MAGIC = 12582912.0

DBG_DUMP = None
PHASES_LIMIT = 99


_const_cache = {}

def host_constants():
    if _const_cache:
        return _const_cache
    trow = np.broadcast_to(np.arange(T, dtype=np.float64), (128, T)).astype(f32).copy()
    freqs = 1.0 / (10000.0 ** (np.arange(0, DROPE, 2, dtype=np.float64) / DROPE))
    tpos = np.arange(T, dtype=np.float64)[:, None] * freqs[None, :]
    ropec = np.tile(np.cos(tpos), (1, NH)).astype(f32)
    ropes = np.tile(np.sin(tpos), (1, NH)).astype(f32)
    m = np.arange(1024) - 384
    tri = np.where(np.arange(128)[:, None] > m[None, :], f32(NEG_BIG), f32(0.0)).astype(f32)
    onesbf = np.ones((128, 128), dtype=bf16)
    onesf = np.ones((128, 128), dtype=f32)
    identbf = np.eye(128, dtype=bf16)
    identf = np.eye(128, dtype=f32)
    sel2 = np.zeros((NE, 128), dtype=bf16)
    for e in range(NE):
        sel2[e, e * RL:(e + 1) * RL] = 2.0
    sel2f = sel2.astype(f32)
    qropec = (ropec.astype(np.float64) * SCALE_QK).astype(f32)
    qropes = (ropes.astype(np.float64) * SCALE_QK).astype(f32)
    _const_cache.update(dict(QROPEC=qropec, QROPES=qropes,
                             TROW=trow, ROPEC=ropec, ROPES=ropes, TRI=tri,
                             ONESBF=onesbf, ONESF=onesf, IDENTBF=identbf,
                             IDENTF=identf, SEL2=sel2, SEL2F=sel2f))
    return _const_cache


class Sem:
    def __init__(self, nc, name):
        self.h = nc.alloc_semaphore(name)
        self.n = 0

    def dinc(self, inst):
        inst.then_inc(self.h, 16)
        self.n += 16
        return self.n

    def einc(self, inst):
        inst.then_inc(self.h, 1)
        self.n += 1
        return self.n


class PSem:

    def __init__(self, c, a, b):
        self.s = [c[a], c[b]]
        self.tgt = {}

    @staticmethod
    def _par(key):
        if isinstance(key, int):
            return key % 2
        if isinstance(key, str):
            return 0
        return key[0] % 2

    def pre(self, keys):
        cnt = [self.s[0].n, self.s[1].n]
        for key in keys:
            p = self._par(key)
            cnt[p] += 16
            self.tgt[key] = (p, cnt[p])

    def issue(self, key, inst):
        p = self._par(key)
        n = self.s[p].dinc(inst)
        if key in self.tgt:
            assert self.tgt[key] == (p, n) or self.tgt[key][1] >= n, (key, self.tgt[key], n)
        self.tgt[key] = (p, max(n, self.tgt.get(key, (p, 0))[1]))
        return n

    def wait(self, eng, key):
        p, n = self.tgt[key]
        eng.wait_ge(self.s[p].h, n)


def mk_chain(eng, sem):
    def _c(inst):
        n = sem.einc(inst)
        eng.wait_ge(sem.h, n)
        return inst
    return _c


class Arena:
    def __init__(self, nc, f32_cols):
        self.t = nc.alloc_sbuf_tensor("arena", [128, f32_cols], F32)
        self.cols = f32_cols
        self.off = 0

    def reset(self):
        self.off = 0

    def alloc(self, parts, cols, dt=F32):
        per = 4 if dt == F32 else 2
        ncol_f32 = (cols * per + 3) // 4
        self.off = (self.off + 7) & ~7
        assert self.off + ncol_f32 <= self.cols, \
            f"arena overflow {self.off}+{ncol_f32}>{self.cols}"
        base = self.off
        self.off += ncol_f32
        if dt != F32:
            return self.t[:].bitcast(dt)[0:parts, base * 2: base * 2 + cols]
        return self.t[0:parts, base: base + cols]


def build_program(n_phases=99, dump=None):
    nc = bass.Bass("TRN2", target_bir_lowering=False)
    c = {}
    c["nc"] = nc

    def dram(name, shape, dt, kind):
        h = nc.dram_tensor(name, shape, dt, kind=kind)
        c[name] = h
        return h

    dram("x", [NTOK, D], F32, "ExternalInput")
    dram("nu_log", [NST], F32, "ExternalInput")
    dram("theta_log", [NST], F32, "ExternalInput")
    dram("B_re", [NST, D], F32, "ExternalInput")
    dram("B_im", [NST, D], F32, "ExternalInput")
    dram("C_re", [D, NST], F32, "ExternalInput")
    dram("C_im", [D, NST], F32, "ExternalInput")
    dram("Dvec", [D], F32, "ExternalInput")
    dram("proj1", [HFF, D], F32, "ExternalInput")
    dram("proj1_bias", [HFF], F32, "ExternalInput")
    dram("proj2", [D, HFF], F32, "ExternalInput")
    dram("proj2_bias", [D], F32, "ExternalInput")
    dram("Wq", [D, NH * DQK], F32, "ExternalInput")
    dram("Wkv_a", [D, KVL + DROPE], F32, "ExternalInput")
    dram("kv_norm_w", [KVL], F32, "ExternalInput")
    dram("Wkv_b", [KVL, NH * (DNOPE + DV)], F32, "ExternalInput")
    dram("Wo", [NH * DV, D], F32, "ExternalInput")
    dram("base_w", [D, D], F32, "ExternalInput")
    dram("base_b", [D], F32, "ExternalInput")
    dram("router_w", [NE, D], F32, "ExternalInput")
    dram("router_b", [NE], F32, "ExternalInput")
    dram("lora_A", [NE, D, RL], F32, "ExternalInput")
    dram("lora_B", [NE, RL, D], F32, "ExternalInput")
    dram("TROW", [128, T], F32, "ExternalInput")
    dram("ROPEC", [T, 512], F32, "ExternalInput")
    dram("QROPEC", [T, 512], F32, "ExternalInput")
    dram("QROPES", [T, 512], F32, "ExternalInput")
    dram("ROPES", [T, 512], F32, "ExternalInput")
    dram("TRI", [128, 1024], F32, "ExternalInput")
    dram("ONESBF", [128, 128], BF16, "ExternalInput")
    dram("ONESF", [128, 128], F32, "ExternalInput")
    dram("IDENTBF", [128, 128], BF16, "ExternalInput")
    dram("IDENTF", [128, 128], F32, "ExternalInput")
    dram("SEL2", [NE, 128], BF16, "ExternalInput")
    dram("SEL2F", [NE, 128], F32, "ExternalInput")
    dram("s_p1bf", [HFF, D], BF16, "Internal")
    dram("s_p2bf", [D, HFF], BF16, "Internal")
    dram("s_p2t", [HFF, D], BF16, "Internal")
    dram("s_brebf", [NST, D], BF16, "Internal")
    dram("s_bimbf", [NST, D], BF16, "Internal")
    dram("s_crebf", [D, NST], BF16, "Internal")
    dram("s_cimbf", [D, NST], BF16, "Internal")
    dram("s_bwbf", [D, D], BF16, "Internal")
    dram("s_rwbf", [16, D], BF16, "Internal")
    dram("s_wqbf", [D, NH * DQK], BF16, "Internal")
    dram("s_wkvbbf", [KVL, NH * (DNOPE + DV)], BF16, "Internal")
    dram("s_xn0", [NTOK, D], BF16, "Internal")
    dram("s_g0", [NTOK, D], F32, "Internal")
    dram("s_x1", [NTOK, D], F32, "Internal")
    dram("s_xn1", [NTOK, D], BF16, "Internal")
    dram("s_qnope", [NH * DNOPE, NTOK], BF16, "Internal")
    dram("s_qrope", [NTOK, NH * 128], BF16, "Internal")
    dram("s_krope", [NTOK, 128], BF16, "Internal")
    dram("s_ckvn", [NTOK, KVL], BF16, "Internal")
    dram("s_knope", [SEQS * NH * DNOPE, T], BF16, "Internal")
    dram("s_v", [NTOK, NH * DV], BF16, "Internal")
    dram("s_o", [SEQS * NH * DV, T], BF16, "Internal")
    dram("s_s1", [NTOK, D], F32, "Internal")
    dram("s_g1", [NTOK, D], F32, "Internal")
    dram("out", [NTOK, D], F32, "ExternalOutput")

    c["slab"] = [nc.alloc_sbuf_tensor(f"slab{i}", [128, NTOK], F32) for i in range(4)]
    c["Hsb"] = [nc.alloc_sbuf_tensor(f"hsb{i}", [128, NTOK], BF16) for i in range(4)]
    for nm, shape, dt in [("st_m0", [128, NSLICE], F32), ("st_r0", [128, NSLICE], F32),
                          ("st_m1", [128, NSLICE], F32), ("st_r1", [128, NSLICE], F32),
                          ("v_r", [128, 2], F32), ("v_gam", [128, 2], F32),
                          ("v_th", [128, 2], F32),
                          ("b1_sb", [128, HFF // 128], F32),
                          ("b2row", [1, D], BF16), ("bbrow", [1, D], BF16),
                          ("rbrow", [1, NE], BF16), ("drow", [1, D], F32),
                          ("kvnw_sb", [128, KVL // 128], F32),
                          ("k_ones_bf", [128, 128], BF16), ("k_ones_f", [128, 128], F32),
                          ("k_id_bf", [128, 128], BF16), ("k_id_f", [128, 128], F32),
                          ("k_sel2", [NE, 128], BF16),
                          ("k_sel2f", [NE, 128], F32)]:
        c[nm] = nc.alloc_sbuf_tensor(nm, shape, dt)
    c["psA"] = nc.alloc_psum_tensor("psA", [128, 2048], F32)
    c["psB"] = nc.alloc_psum_tensor("psB", [128, 2048], F32)
    c["arena"] = Arena(nc, 23700)

    for s in ["sem_ld", "sem_ldB", "sem_ld2", "sem_gp", "sem_gpB", "sem_st",
              "sem_stB", "sem_pe", "sem_dve", "sem_dve2", "sem_act", "sem_pool",
              "sem_vself", "sem_sself", "sem_p1a", "sem_p1b", "sem_p2a", "sem_p2b",
              "sem_st2a", "sem_st2b", "sem_p3a", "sem_p3b"]:
        c[s] = Sem(nc, s)

    dump_map = {"xn0": "s_xn0", "g0": "s_g0", "x1": "s_x1", "xn1": "s_xn1",
                "qnope": "s_qnope", "qrope": "s_qrope", "g1": "s_g1",
                "out": "out", "krope": "s_krope", "ckvn": "s_ckvn",
                "knope": "s_knope", "v": "s_v", "o": "s_o", "s1": "s_s1",
                "p1bf": "s_p1bf", "brebf": "s_brebf"}

    phases = [ph_prologue, ph_wkvbfold, ph_ln0, ph_p2t, ph_bu, ph_scan, ph_ffn0,
              ph_cm, ph_ln1, ph_q, ph_kv, ph_kvb, ph_attn, ph_wo, ph_ffn1,
              ph_out]
    for ph in phases[:n_phases]:
        c["arena"].reset()
        with nc.named_scope(ph.__name__):
            with nc.Block() as block:
                ph(c, block)

    if dump is not None:
        sem_st = c["sem_st"]
        if dump in ("bu", "hsb"):
            tl = c["slab"] if dump == "bu" else c["Hsb"]
            dt_ = F32 if dump == "bu" else BF16
            ddbg = nc.dram_tensor("dbg", [4 * 128, NTOK], dt_, kind="ExternalOutput")
            with nc.Block() as block:
                @block.sync
                def _(sync):
                    for k in range(4):
                        sem_st.dinc(sync.dma_start(
                            ddbg.ap()[k * 128:(k + 1) * 128, :], tl[k][:]))
                    sync.wait_ge(sem_st.h, sem_st.n)
        else:
            src = c[dump_map[dump]]
            ddbg = nc.dram_tensor("dbg", list(src.shape), src.dtype, kind="ExternalOutput")
            with nc.Block() as block:
                @block.sync
                def _(sync):
                    sem_st.dinc(sync.dma_start(ddbg.ap(), src.ap()))
                    sync.wait_ge(sem_st.h, sem_st.n)
    return nc


def ph_prologue(c, block):
    arena = c["arena"]
    sem_ld2 = c["sem_ld2"]
    sem_dve, sem_act, sem_pe = c["sem_dve"], c["sem_act"], c["sem_pe"]
    gpP = PSem(c, "sem_gp", "sem_gpB")
    stP = PSem(c, "sem_st", "sem_stB")
    psA = c["psA"]

    casts = [(c["Wq"], c["s_wqbf"], D, NH * DQK),
             (c["proj1"], c["s_p1bf"], HFF, D),
             (c["proj2"], c["s_p2bf"], D, HFF),
             (c["base_w"], c["s_bwbf"], D, D),
             (c["B_re"], c["s_brebf"], NST, D),
             (c["B_im"], c["s_bimbf"], NST, D),
             (c["C_re"], c["s_crebf"], D, NST),
             (c["C_im"], c["s_cimbf"], D, NST),
             (c["router_w"], c["s_rwbf"], NE, D)]
    jobs = []
    for (src_, dst, rows, cols) in casts:
        for r0 in range(0, rows, 128):
            rr = min(128, rows - r0)
            jobs.append((src_, dst, r0, rr, cols))
    bufs = [arena.alloc(128, 8192, BF16) for _ in range(2)]
    gpP.pre(["z"] + list(range(len(jobs))))
    stP.pre(range(len(jobs)))

    zpad = arena.alloc(8, D, BF16)

    @block.gpsimd
    def _(gp):
        n_ = c["sem_pool"].einc(gp.memset(zpad[:], 0.0))
        gp.wait_ge(c["sem_pool"].h, n_)
        gpP.issue("z", gp.dma_start(c["s_rwbf"].ap()[8:16, :], zpad[:]))
        for i, (src_, dst, r0, rr, cols) in enumerate(jobs):
            if i >= 2:
                stP.wait(gp, i - 2)
            gpP.issue(i, gp.dma_start(bufs[i % 2][0:rr, 0:cols],
                                      src_.ap()[r0:r0 + rr, :]))

    @block.sync
    def _(sync):
        for nm, src_ in [("k_ones_bf", "ONESBF"), ("k_ones_f", "ONESF"),
                         ("k_id_bf", "IDENTBF"), ("k_id_f", "IDENTF"),
                         ("k_sel2", "SEL2"), ("k_sel2f", "SEL2F")]:
            sem_ld2.dinc(sync.dma_start(c[nm][:], c[src_].ap()))
        for i, (src_, dst, r0, rr, cols) in enumerate(jobs):
            gpP.wait(sync, i)
            stP.issue(i, sync.dma_start(dst.ap()[r0:r0 + rr, :],
                                        bufs[i % 2][0:rr, 0:cols]))
        sync.wait_ge(stP.s[0].h, stP.s[0].n)
        sync.wait_ge(stP.s[1].h, stP.s[1].n)

    nu2 = arena.alloc(2, 128)
    th2 = arena.alloc(2, 128)
    b1n = arena.alloc(64, 128)
    kvn4 = arena.alloc(4, 128)
    b2f = arena.alloc(1, D)
    bbf_ = arena.alloc(1, D)
    rbf_ = arena.alloc(1, NE)
    tmp = arena.alloc(128, 2)
    tmp2 = arena.alloc(128, 2)
    tmp3 = arena.alloc(128, 2)
    dv0, a0, pe0 = sem_dve.n, sem_act.n, sem_pe.n

    @block.scalar
    def _(sc):
        sem_ld2.dinc(sc.dma_start(nu2[:], c["nu_log"].ap().rearrange("(a b) -> a b", b=128)))
        sem_ld2.dinc(sc.dma_start(th2[:], c["theta_log"].ap().rearrange("(a b) -> a b", b=128)))
        sem_ld2.dinc(sc.dma_start(b1n[:], c["proj1_bias"].ap().rearrange("(a b) -> a b", b=128)))
        sem_ld2.dinc(sc.dma_start(kvn4[:], c["kv_norm_w"].ap().rearrange("(a b) -> a b", b=128)))
        sem_ld2.dinc(sc.dma_start(b2f[:], c["proj2_bias"].ap().rearrange("(o a) -> o a", o=1)))
        sem_ld2.dinc(sc.dma_start(bbf_[:], c["base_b"].ap().rearrange("(o a) -> o a", o=1)))
        sem_ld2.dinc(sc.dma_start(rbf_[:], c["router_b"].ap().rearrange("(o a) -> o a", o=1)))
        sem_ld2.dinc(sc.dma_start(c["drow"][:], c["Dvec"].ap().rearrange("(o a) -> o a", o=1)))

    LD2_ALL = sem_ld2.n

    @block.tensor
    def _(te):
        te.wait_ge(sem_ld2.h, LD2_ALL)
        sem_pe.einc(te.transpose(psA[:, 0:2], nu2[:], c["k_id_f"][0:2, 0:2]))
        sem_pe.einc(te.transpose(psA[:, 8:10], th2[:], c["k_id_f"][0:2, 0:2]))
        sem_pe.einc(te.transpose(psA[:, 16:80], b1n[:], c["k_id_f"][0:64, 0:64]))
        sem_pe.einc(te.transpose(psA[:, 96:100], kvn4[:], c["k_id_f"][0:4, 0:4]))

    @block.vector
    def _(v):
        v.wait_ge(sem_pe.h, pe0 + 4)
        v.tensor_copy(c["b1_sb"][:], psA[:, 16:80])
        v.tensor_copy(c["kvnw_sb"][:], psA[:, 96:100])
        v.tensor_copy(c["b2row"][:], b2f[:])
        v.tensor_copy(c["bbrow"][:], bbf_[:])
        v.tensor_copy(c["rbrow"][:], rbf_[:])
        v.tensor_copy(tmp[:], psA[:, 0:2])
        sem_dve.einc(v.tensor_copy(tmp3[:], psA[:, 8:10]))

    sem_ss = c["sem_sself"]

    def ssw(sc, inst):
        n = sem_ss.einc(inst)
        sc.wait_ge(sem_ss.h, n)

    @block.scalar
    def _(sc):
        sc.wait_ge(sem_dve.h, dv0 + 1)
        ssw(sc, sc.activation(tmp2[:], tmp[:], AF.Abs))
        ssw(sc, sc.activation(tmp2[:], tmp2[:], AF.Exp, scale=np.float32(-1.0)))
        sem_act.einc(sc.activation(tmp2[:], tmp2[:], AF.Ln, bias=np.float32(1.0)))

    @block.vector
    def _(v):
        v.wait_ge(sem_act.h, a0 + 1)
        n_ = c["sem_vself"].einc(v.tensor_scalar(tmp[:], tmp[:], 0.0, None,
                                                 OP.max, OP.bypass))
        v.wait_ge(c["sem_vself"].h, n_)
        sem_dve.einc(v.tensor_tensor(tmp[:], tmp[:], tmp2[:], OP.add))

    @block.scalar
    def _(sc):
        sc.wait_ge(sem_dve.h, dv0 + 2)
        ssw(sc, sc.activation(tmp[:], tmp[:], AF.Exp))
        ssw(sc, sc.activation(c["v_r"][:], tmp[:], AF.Exp, scale=np.float32(-1.0)))
        ssw(sc, sc.activation(c["v_th"][:], tmp3[:], AF.Exp))
        sem_act.einc(sc.activation(tmp2[:], c["v_r"][:], AF.Square))

    @block.vector
    def _(v):
        v.wait_ge(sem_act.h, a0 + 2)
        n_ = c["sem_vself"].einc(v.tensor_scalar(tmp2[:], tmp2[:], -1.0, 1.0,
                                                 OP.mult, OP.add))
        v.wait_ge(c["sem_vself"].h, n_)
        sem_dve.einc(v.tensor_scalar(tmp2[:], tmp2[:], 1e-30, None, OP.max, OP.bypass))

    @block.scalar
    def _(sc):
        sc.wait_ge(sem_dve.h, dv0 + 3)
        sem_act.einc(sc.activation(c["v_gam"][:], tmp2[:], AF.Sqrt))


def _ln_sweep(c, block, src, dst, st_m, st_r):
    arena = c["arena"]
    sem_dve, sem_dve2 = c["sem_dve"], c["sem_dve2"]
    sem_act, sem_pool = c["sem_act"], c["sem_pool"]
    sem_vself = c["sem_vself"]
    ldP = PSem(c, "sem_ld", "sem_ldB")
    stP = PSem(c, "sem_st", "sem_stB")
    xb = [arena.alloc(128, D) for _ in range(2)]
    xnb = [arena.alloc(128, D, BF16) for _ in range(2)]
    sqscr = [arena.alloc(128, D) for _ in range(2)]
    sums = arena.alloc(128, 1)
    sqs = arena.alloc(128, 1)
    mm_ = arena.alloc(128, 1)
    vtmp = arena.alloc(128, NSLICE)
    dv0, dv20, a0, p0 = sem_dve.n, sem_dve2.n, sem_act.n, sem_pool.n
    ldP.pre(range(NSLICE))
    stP.pre(range(NSLICE))

    @block.sync
    def _(sync):
        for i in range(NSLICE + 1):
            if i < NSLICE:
                if i >= 2:
                    sync.wait_ge(sem_dve2.h, dv20 + (i - 1))
                    sync.wait_ge(sem_act.h, a0 + (i - 1))
                ldP.issue(i, sync.dma_start(xb[i % 2][:],
                                            src.ap()[i * TS:(i + 1) * TS, :]))
            if i >= 1:
                sync.wait_ge(sem_dve2.h, dv20 + i)
                stP.issue(i - 1, sync.dma_start(dst.ap()[(i - 1) * TS:i * TS, :],
                                                xnb[(i - 1) % 2][:]))
        sync.wait_ge(stP.s[0].h, stP.s[0].n)
        sync.wait_ge(stP.s[1].h, stP.s[1].n)

    @block.scalar
    def _(sc):
        for i in range(NSLICE):
            ldP.wait(sc, i)
            if i >= 2:
                sc.wait_ge(sem_dve.h, dv0 + (i - 1))
            sem_act.einc(sc.activation(sqscr[i % 2][:], xb[i % 2][:], AF.Square))
            if i >= 1:
                sc.wait_ge(sem_dve.h, dv0 + i)
                n_ = c["sem_sself"].einc(sc.activation(vtmp[:, i - 1:i],
                                                       vtmp[:, i - 1:i], AF.Ln))
                sc.wait_ge(c["sem_sself"].h, n_)
                sem_pool.einc(sc.activation(st_r[:, i - 1:i], vtmp[:, i - 1:i],
                                            AF.Exp, scale=np.float32(-0.5)))
        sc.wait_ge(sem_dve.h, dv0 + NSLICE)
        n_ = c["sem_sself"].einc(sc.activation(vtmp[:, NSLICE - 1:NSLICE],
                                               vtmp[:, NSLICE - 1:NSLICE], AF.Ln))
        sc.wait_ge(c["sem_sself"].h, n_)
        sem_pool.einc(sc.activation(st_r[:, NSLICE - 1:NSLICE],
                                    vtmp[:, NSLICE - 1:NSLICE],
                                    AF.Exp, scale=np.float32(-0.5)))

    def vsw(v, inst):
        n = sem_vself.einc(inst)
        v.wait_ge(sem_vself.h, n)

    def xn_emit(v, i):
        v.wait_ge(sem_pool.h, p0 + i + 1)
        if i >= 2:
            stP.wait(v, i - 2)
        sem_dve2.einc(v.tensor_scalar(xnb[i % 2][:], xb[i % 2][:],
                                      st_m[:, i:i + 1], st_r[:, i:i + 1],
                                      OP.subtract, OP.mult))

    @block.vector
    def _(v):
        for i in range(NSLICE):
            ldP.wait(v, i)
            vsw(v, v.tensor_reduce(sums[:], xb[i % 2][:], AX.X, OP.add))
            if i >= 1:
                xn_emit(v, i - 1)
            v.wait_ge(sem_act.h, a0 + i + 1)
            vsw(v, v.tensor_reduce(sqs[:], sqscr[i % 2][:], AX.X, OP.add))
            vsw(v, v.tensor_scalar(st_m[:, i:i + 1], sums[:], 1.0 / D, None,
                                   OP.mult, OP.bypass))
            vsw(v, v.tensor_tensor(mm_[:], st_m[:, i:i + 1], st_m[:, i:i + 1], OP.mult))
            vsw(v, v.tensor_scalar(vtmp[:, i:i + 1], sqs[:], 1.0 / D, None,
                                   OP.mult, OP.bypass))
            sem_dve.einc(v.tensor_scalar(vtmp[:, i:i + 1],
                                         vtmp[:, i:i + 1], mm_[:], -float(EPS_LN),
                                         OP.subtract, OP.subtract))
        xn_emit(v, NSLICE - 1)


def ph_p2t(c, block):
    arena = c["arena"]
    ldX = PSem(c, "sem_p1a", "sem_p1b")
    stX = PSem(c, "sem_p2a", "sem_p2b")
    tb = [arena.alloc(128, 2048, BF16) for _ in range(2)]
    ldX.pre(range(64))
    stX.pre(range(64))

    @block.sync
    def _(sync):
        for hc in range(64):
            if hc >= 2:
                stX.wait(sync, hc - 2)
            ldX.issue(hc, sync.dma_start(tb[hc % 2][:],
                                         c["s_p2bf"].ap()[:, hc * 128:(hc + 1) * 128],
                                         transpose=True))

    @block.scalar
    def _(sc):
        for hc in range(64):
            ldX.wait(sc, hc)
            stX.issue(hc, sc.dma_start(
                c["s_p2t"].ap()[hc * 128:(hc + 1) * 128, :], tb[hc % 2][:]))
        sc.wait_ge(stX.s[0].h, stX.s[0].n)
        sc.wait_ge(stX.s[1].h, stX.s[1].n)


def ph_ln0(c, block):
    _ln_sweep(c, block, c["x"], c["s_xn0"], c["st_m0"], c["st_r0"])


def ph_bu(c, block):
    arena = c["arena"]
    sem_ld2, sem_pe, sem_dve = c["sem_ld2"], c["sem_pe"], c["sem_dve"]
    ldP = PSem(c, "sem_ld", "sem_ldB")
    psq = [c["psA"], c["psB"]]
    btre = arena.alloc(128, 16 * NST, BF16)
    btim = arena.alloc(128, 16 * NST, BF16)
    xf = [arena.alloc(128, 16 * 512, BF16) for _ in range(2)]
    pe0, dv0 = sem_pe.n, sem_dve.n
    cnt = [ldP.s[0].n, ldP.s[1].n]
    for s in range(NSUP):
        cnt[s % 2] += 16 * 16
        ldP.tgt[s] = (s % 2, cnt[s % 2])

    @block.sync
    def _(sync):
        for ci in range(16):
            sem_ld2.dinc(sync.dma_start(btre[:, ci * NST:(ci + 1) * NST],
                                        c["s_brebf"].ap()[:, ci * 128:(ci + 1) * 128],
                                        transpose=True))
            sem_ld2.dinc(sync.dma_start(btim[:, ci * NST:(ci + 1) * NST],
                                        c["s_bimbf"].ap()[:, ci * 128:(ci + 1) * 128],
                                        transpose=True))
        for s in range(NSUP):
            if s >= 2:
                sync.wait_ge(sem_pe.h, pe0 + 4 * (s - 1))
            for ci in range(16):
                ldP.issue(s, sync.dma_start(
                    xf[s % 2][:, ci * 512:(ci + 1) * 512],
                    c["s_xn0"].ap()[s * SUP:(s + 1) * SUP, ci * 128:(ci + 1) * 128],
                    transpose=True))

    LD2_ALL = sem_ld2.n

    @block.tensor
    def _(te):
        te.wait_ge(sem_ld2.h, LD2_ALL)
        for s in range(NSUP):
            ldP.wait(te, s)
            if s >= 2:
                te.wait_ge(sem_dve.h, dv0 + (s - 1))
            ps = psq[s % 2]
            for q, (nch, bt) in enumerate([(0, btre), (0, btim), (1, btre), (1, btim)]):
                for ci in range(16):
                    inst = te.matmul(ps[:, q * 512:(q + 1) * 512],
                                     bt[:, ci * NST + nch * 128: ci * NST + nch * 128 + 128],
                                     xf[s % 2][:, ci * 512:(ci + 1) * 512],
                                     start=(ci == 0), stop=(ci == 15))
                sem_pe.einc(inst)

    @block.vector
    def _(v):
        slab = c["slab"]
        for s in range(NSUP):
            v.wait_ge(sem_pe.h, pe0 + 4 * (s + 1))
            ps = psq[s % 2]
            for q, (nch, comp) in enumerate([(0, 0), (0, 1), (1, 0), (1, 1)]):
                k = 2 * nch + comp
                inst = v.tensor_scalar(slab[k][:, s * SUP:(s + 1) * SUP],
                                       ps[:, q * 512:(q + 1) * 512],
                                       c["v_gam"][:, nch:nch + 1], None,
                                       OP.mult, OP.bypass)
            sem_dve.einc(inst)


def ph_scan(c, block):
    arena = c["arena"]
    sem_ld2, sem_dve, sem_act = c["sem_ld2"], c["sem_dve"], c["sem_act"]
    trow = arena.alloc(128, T)
    COS = arena.alloc(128, T)
    SIN = arena.alloc(128, T)
    PH = arena.alloc(128, T)
    PHR = arena.alloc(128, T)
    RT = arena.alloc(128, T)
    CRE = arena.alloc(128, T)
    CIM = arena.alloc(128, T)
    TMP = arena.alloc(128, T)
    dv0, a0 = sem_dve.n, sem_act.n
    slab, Hsb = c["slab"], c["Hsb"]

    @block.sync
    def _(sync):
        sem_ld2.dinc(sync.dma_start(trow[:], c["TROW"].ap()))
    LD2_ALL = sem_ld2.n

    def table(v, V, base_shift):
        V(v.tensor_scalar(PHR[:], PH[:], float(base_shift), None, OP.add, OP.bypass))
        V(v.tensor_scalar(TMP[:], PHR[:], 1.0 / TWO_PI, MAGIC, OP.mult, OP.add))
        V(v.tensor_scalar(TMP[:], TMP[:], MAGIC, None, OP.subtract, OP.bypass))
        V(v.scalar_tensor_tensor(PHR[:], TMP[:], -TWO_PI, PHR[:], OP.mult, OP.add))
        V(v.tensor_scalar(PHR[:], PHR[:], float(math.pi), None, OP.min, OP.bypass))
        return v.tensor_scalar(PHR[:], PHR[:], -float(math.pi), None, OP.max, OP.bypass)

    for nch in range(2):
        @block.vector
        def _(v, nch=nch):
            V = mk_chain(v, c["sem_vself"])
            if nch == 0:
                v.wait_ge(sem_ld2.h, LD2_ALL)
            else:
                v.wait_ge(sem_act.h, a0 + 2)
            V(v.tensor_scalar(PH[:], trow[:], c["v_th"][:, nch:nch + 1], None,
                              OP.mult, OP.bypass))
            sem_dve.einc(table(v, V, 0.0))
            v.wait_ge(sem_act.h, a0 + 2 * nch + 1)
            sem_dve.einc(table(v, V, math.pi / 2.0))
            V(v.tensor_scalar(RT[:], trow[:], 0.0, c["v_r"][:, nch:nch + 1],
                              OP.mult, OP.add))
            v.wait_ge(sem_act.h, a0 + 2 * nch + 2)
            for seq in range(SEQS):
                cols = slice(seq * T, (seq + 1) * T)
                bu_re = slab[2 * nch + 0][:, cols]
                bu_im = slab[2 * nch + 1][:, cols]
                V(v.tensor_tensor(CRE[:], COS[:], bu_re, OP.mult))
                V(v.tensor_tensor(TMP[:], SIN[:], bu_im, OP.mult))
                V(v.tensor_tensor(CRE[:], CRE[:], TMP[:], OP.add))
                V(v.tensor_tensor(CIM[:], COS[:], bu_im, OP.mult))
                V(v.tensor_tensor(TMP[:], SIN[:], bu_re, OP.mult))
                V(v.tensor_tensor(CIM[:], CIM[:], TMP[:], OP.subtract))
                V(v.tensor_tensor_scan(bu_re, RT[:], CRE[:], 0.0, OP.mult, OP.add))
                V(v.tensor_tensor_scan(bu_im, RT[:], CIM[:], 0.0, OP.mult, OP.add))
                g_re, g_im = bu_re, bu_im
                V(v.tensor_tensor(CRE[:], COS[:], g_re, OP.mult))
                V(v.tensor_tensor(TMP[:], SIN[:], g_im, OP.mult))
                V(v.tensor_tensor(Hsb[2 * nch + 0][:, cols], CRE[:], TMP[:], OP.subtract))
                V(v.tensor_tensor(CIM[:], COS[:], g_im, OP.mult))
                V(v.tensor_tensor(TMP[:], SIN[:], g_re, OP.mult))
                V(v.scalar_tensor_tensor(Hsb[2 * nch + 1][:, cols], CIM[:], -1.0,
                                         TMP[:], OP.mult, OP.subtract))

        @block.scalar
        def _(sc, nch=nch):
            sc.wait_ge(sem_dve.h, dv0 + 2 * nch + 1)
            sem_act.einc(sc.activation(SIN[:], PHR[:], AF.Sin))
            sc.wait_ge(sem_dve.h, dv0 + 2 * nch + 2)
            sem_act.einc(sc.activation(COS[:], PHR[:], AF.Sin))


def ph_ffn0(c, block):
    arena = c["arena"]
    sem_pe, sem_act = c["sem_pe"], c["sem_act"]
    ldP = PSem(c, "sem_ld", "sem_ldB")
    p1P = PSem(c, "sem_p1a", "sem_p1b")
    p2P = PSem(c, "sem_p2a", "sem_p2b")
    p3P = PSem(c, "sem_p3a", "sem_p3b")
    stP = PSem(c, "sem_st", "sem_stB")
    slab = c["slab"]
    hsl = [t[:].bitcast(BF16) for t in slab]

    def hch(hc):
        return hsl[hc // 16][:, (hc % 16) * 512:(hc % 16 + 1) * 512]

    xnb = [arena.alloc(128, 16 * 512, BF16) for _ in range(2)]
    p1b = [arena.alloc(128, 16 * 512, BF16) for _ in range(2)]
    p2b = [arena.alloc(128, 2048, BF16) for _ in range(4)]
    gb = [arena.alloc(128, 2048)]
    psA, psB = c["psA"], c["psB"]
    p0, a0 = sem_pe.n, sem_act.n
    PE_S = 194
    ACT_S = 68
    ldP.pre([s for s in range(NSUP) for _ in range(16)])
    p1P.pre([g for g in range(NSUP * 16) for _ in range(16)])
    p2P.pre([g for g in range(NSUP * 2 * 64) if (g // 2) % 2 == 0])
    p3P.pre([g for g in range(NSUP * 2 * 64) if (g // 2) % 2 == 1])
    stP.pre(range(NSUP * 4))

    def pe_t(s, k):
        return p0 + s * PE_S + k

    def act_t(s, k):
        return a0 + s * ACT_S + k

    @block.sync
    def _(sync):
        for s in range(NSUP):
            if s >= 2:
                sync.wait_ge(sem_pe.h, pe_t(s - 2, 64))
            for dc in range(16):
                ldP.issue(s, sync.dma_start(
                    xnb[s % 2][:, dc * 512:(dc + 1) * 512],
                    c["s_xn0"].ap()[s * SUP:(s + 1) * SUP, dc * 128:(dc + 1) * 128],
                    transpose=True))
            for hg in range(16):
                g = s * 16 + hg
                if g >= 2:
                    sync.wait_ge(sem_pe.h, pe_t((g - 2) // 16, 4 * ((g - 2) % 16) + 4))
                for dc in range(16):
                    p1P.issue(g, sync.dma_start(
                        p1b[g % 2][:, dc * 512:(dc + 1) * 512],
                        c["s_p1bf"].ap()[hg * 512:(hg + 1) * 512, dc * 128:(dc + 1) * 128],
                        transpose=True))
            for pair in range(2):
                for hc in range(64):
                    g2 = (s * 2 + pair) * 64 + hc
                    if g2 >= 4:
                        gp = g2 - 4
                        s_, r_ = gp // 128, gp % 128
                        pair_, hc_ = r_ // 64, r_ % 64
                        sync.wait_ge(sem_pe.h,
                                     pe_t(s_, 64 + pair_ * 65 + hc_ + 1))
                    (p2P if (g2 // 2) % 2 == 0 else p3P).issue(g2, sync.dma_start(
                        p2b[g2 % 4][:],
                        c["s_p2t"].ap()[hc * 128:(hc + 1) * 128, :]))
                for jj in range(2):
                    j = pair * 2 + jj
                    sj = s * 4 + j
                    sync.wait_ge(sem_act.h, act_t(s, 64 + j + 1))
                    stP.issue(sj, sync.dma_start(
                        c["s_g0"].ap()[s * SUP + j * TS: s * SUP + (j + 1) * TS, :],
                        gb[0][:]))
        sync.wait_ge(stP.s[0].h, stP.s[0].n)
        sync.wait_ge(stP.s[1].h, stP.s[1].n)

    @block.tensor
    def _(te):
        for s in range(NSUP):
            ldP.wait(te, s)
            if s >= 1:
                te.wait_ge(sem_act.h, act_t(s - 1, 68))
            for hg in range(16):
                g = s * 16 + hg
                p1P.wait(te, g)
                for k in range(4):
                    hc = hg * 4 + k
                    if hc >= 2:
                        te.wait_ge(sem_act.h, act_t(s, hc - 1))
                    for dc in range(16):
                        inst = te.matmul(psA[:, (hc % 2) * 512:(hc % 2 + 1) * 512],
                                         p1b[g % 2][:, dc * 512 + k * 128: dc * 512 + k * 128 + 128],
                                         xnb[s % 2][:, dc * 512:(dc + 1) * 512],
                                         start=(dc == 0), stop=(dc == 15))
                    sem_pe.einc(inst)
            te.wait_ge(sem_act.h, act_t(s, 64))
            for pair in range(2):
                ps_j = [psB, psA]
                if pair >= 1:
                    te.wait_ge(sem_act.h, act_t(s, 64 + 2 * pair))
                for hc in range(64):
                    g2 = (s * 2 + pair) * 64 + hc
                    (p2P if (g2 // 2) % 2 == 0 else p3P).wait(te, g2)
                    for jj in range(2):
                        j = pair * 2 + jj
                        for bs in range(4):
                            inst = te.matmul(
                                ps_j[jj][:, bs * 512:(bs + 1) * 512],
                                hch(hc)[:, j * 128:(j + 1) * 128],
                                p2b[g2 % 4][:, bs * 512:(bs + 1) * 512],
                                start=(hc == 0), stop=False)
                    sem_pe.einc(inst)
                for jj in range(2):
                    for bs in range(4):
                        inst = te.matmul(ps_j[jj][:, bs * 512:(bs + 1) * 512],
                                         c["k_ones_bf"][0:1, 0:128],
                                         c["b2row"][0:1, bs * 512:(bs + 1) * 512],
                                         start=False, stop=True)
                sem_pe.einc(inst)


    @block.scalar
    def _(sc):
        for s in range(NSUP):
            for hc in range(64):
                sc.wait_ge(sem_pe.h, pe_t(s, hc + 1))
                sem_act.einc(sc.activation(hch(hc), psA[:, (hc % 2) * 512:(hc % 2 + 1) * 512],
                                           AF.Gelu, bias=c["b1_sb"][:, hc:hc + 1]))
            for pair in range(2):
                sc.wait_ge(sem_pe.h, pe_t(s, 64 + (pair + 1) * 65))
                for jj in range(2):
                    j = pair * 2 + jj
                    sj = s * 4 + j
                    if sj >= 1:
                        stP.wait(sc, sj - 1)
                    sem_act.einc(sc.activation(gb[0][:],
                                               [psB, psA][jj][:],
                                               AF.Silu))


def ph_cm(c, block):
    arena = c["arena"]
    sem_ld2, sem_pe, sem_dve = c["sem_ld2"], c["sem_pe"], c["sem_dve"]
    sem_vself = c["sem_vself"]
    ldP = PSem(c, "sem_ld", "sem_ldB")
    stP = PSem(c, "sem_st", "sem_stB")
    Hsb, psA, psB = c["Hsb"], c["psA"], c["psB"]
    ct = arena.alloc(128, 4 * D, BF16)
    xb = [arena.alloc(128, D) for _ in range(2)]
    gbuf = [arena.alloc(128, D) for _ in range(2)]
    x1b = [arena.alloc(128, D) for _ in range(2)]
    XN = arena.alloc(128, D)
    T1 = arena.alloc(128, D)
    dbc = arena.alloc(128, D)
    pe0, dv0 = sem_pe.n, sem_dve.n
    ldP.pre([i for i in range(NSLICE) for _ in range(2)])
    stP.pre(range(NSLICE))

    @block.sync
    def _(sync):
        for nch in range(2):
            for comp, srcn in enumerate(["s_crebf", "s_cimbf"]):
                k = 2 * nch + comp
                sem_ld2.dinc(sync.dma_start(
                    ct[:, k * D:(k + 1) * D],
                    c[srcn].ap()[:, nch * 128:(nch + 1) * 128],
                    transpose=True))
        for i in range(NSLICE):
            if i >= 2:
                sync.wait_ge(sem_dve.h, dv0 + i)
            ldP.issue(i, sync.dma_start(xb[i % 2][:], c["x"].ap()[i * TS:(i + 1) * TS, :]))
            ldP.issue(i, sync.dma_start(gbuf[i % 2][:], c["s_g0"].ap()[i * TS:(i + 1) * TS, :]))
            if i >= 1:
                sync.wait_ge(sem_dve.h, dv0 + i + 1)
                stP.issue(i - 1, sync.dma_start(
                    c["s_x1"].ap()[(i - 1) * TS:i * TS, :], x1b[(i - 1) % 2][:]))
        sync.wait_ge(sem_dve.h, dv0 + NSLICE + 1)
        stP.issue(NSLICE - 1, sync.dma_start(
            c["s_x1"].ap()[(NSLICE - 1) * TS:NSLICE * TS, :], x1b[(NSLICE - 1) % 2][:]))
        sync.wait_ge(stP.s[0].h, stP.s[0].n)
        sync.wait_ge(stP.s[1].h, stP.s[1].n)

    LD2_ALL = sem_ld2.n

    @block.tensor
    def _(te):
        te.wait_ge(sem_ld2.h, LD2_ALL)
        for bs in range(4):
            inst = te.matmul(psA[:, bs * 512:(bs + 1) * 512],
                             c["k_ones_f"][0:1, :],
                             c["drow"][0:1, bs * 512:(bs + 1) * 512],
                             start=True, stop=True)
        sem_pe.einc(inst)
        for i in range(NSLICE):
            if i == 0:
                te.wait_ge(sem_dve.h, dv0 + 1)
            if i >= 2:
                te.wait_ge(sem_dve.h, dv0 + i)
            ps = [psA, psB][i % 2]
            for k in range(4):
                for bs in range(4):
                    inst = te.matmul(ps[:, bs * 512:(bs + 1) * 512],
                                     Hsb[k][:, i * TS:(i + 1) * TS],
                                     ct[:, k * D + bs * 512: k * D + (bs + 1) * 512],
                                     start=(k == 0), stop=(k == 3))
            sem_pe.einc(inst)

    @block.vector
    def _(v):
        V = mk_chain(v, sem_vself)
        v.wait_ge(sem_pe.h, pe0 + 1)
        sem_dve.einc(v.tensor_copy(dbc[:], psA[:]))
        for i in range(NSLICE):
            ldP.wait(v, i)
            v.wait_ge(sem_pe.h, pe0 + i + 2)
            if i >= 2:
                stP.wait(v, i - 2)
            ps = [psA, psB][i % 2]
            V(v.tensor_scalar(XN[:], xb[i % 2][:], c["st_m0"][:, i:i + 1],
                              c["st_r0"][:, i:i + 1], OP.subtract, OP.mult))
            V(v.tensor_tensor(T1[:], XN[:], dbc[:], OP.mult))
            V(v.tensor_tensor(T1[:], T1[:], ps[:], OP.add))
            V(v.tensor_tensor(T1[:], T1[:], gbuf[i % 2][:], OP.mult))
            sem_dve.einc(v.tensor_tensor(x1b[i % 2][:], T1[:], xb[i % 2][:], OP.add))


def ph_ln1(c, block):
    _ln_sweep(c, block, c["s_x1"], c["s_xn1"], c["st_m1"], c["st_r1"])


def ph_wkvbfold(c, block):
    arena = c["arena"]
    sem_dve = c["sem_dve"]
    ldP = PSem(c, "sem_ld", "sem_ldB")
    stP = PSem(c, "sem_st", "sem_stB")
    wb = [arena.alloc(128, NH * (DNOPE + DV)) for _ in range(2)]
    ob = [arena.alloc(128, NH * (DNOPE + DV), BF16) for _ in range(2)]
    dv0 = sem_dve.n
    ldP.pre(range(4))
    stP.pre(range(4))

    @block.sync
    def _(sync):
        for lc in range(4):
            if lc >= 2:
                sync.wait_ge(sem_dve.h, dv0 + lc - 1)
            ldP.issue(lc, sync.dma_start(wb[lc % 2][:],
                                         c["Wkv_b"].ap()[lc * 128:(lc + 1) * 128, :]))
            if lc >= 1:
                sync.wait_ge(sem_dve.h, dv0 + lc)
                stP.issue(lc - 1, sync.dma_start(
                    c["s_wkvbbf"].ap()[(lc - 1) * 128:lc * 128, :], ob[(lc - 1) % 2][:]))
        sync.wait_ge(sem_dve.h, dv0 + 4)
        stP.issue(3, sync.dma_start(c["s_wkvbbf"].ap()[3 * 128:4 * 128, :], ob[1][:]))
        sync.wait_ge(stP.s[0].h, stP.s[0].n)
        sync.wait_ge(stP.s[1].h, stP.s[1].n)

    @block.vector
    def _(v):
        for lc in range(4):
            ldP.wait(v, lc)
            if lc >= 2:
                stP.wait(v, lc - 2)
            sem_dve.einc(v.tensor_scalar(ob[lc % 2][:], wb[lc % 2][:],
                                         c["kvnw_sb"][:, lc:lc + 1], None,
                                         OP.mult, OP.bypass))


def ph_q(c, block):
    arena = c["arena"]
    sem_pe, sem_act, sem_dve2 = c["sem_pe"], c["sem_act"], c["sem_dve2"]
    sem_vself = c["sem_vself"]
    ldP = PSem(c, "sem_ld", "sem_ldB")
    wqP = PSem(c, "sem_p1a", "sem_p1b")
    tbP = PSem(c, "sem_p2a", "sem_p2b")
    stP = PSem(c, "sem_st", "sem_stB")
    st2P = PSem(c, "sem_st2a", "sem_st2b")
    psA, psB = c["psA"], c["psB"]
    xf = [arena.alloc(128, 16 * 512, BF16) for _ in range(2)]
    wqn = [arena.alloc(128, 16 * 128, BF16) for _ in range(2)]
    hb16 = [t[:].bitcast(BF16) for t in c["Hsb"]]

    def wqr_dc(dc):
        return hb16[dc // 4][:, (dc % 4) * 1024:(dc % 4 + 1) * 1024]
    qnb = [arena.alloc(128, 512, BF16) for _ in range(2)]
    qrb = [arena.alloc(128, 2048, BF16) for _ in range(2)]
    tabc = [arena.alloc(128, 512) for _ in range(2)]
    tabs = [arena.alloc(128, 512) for _ in range(2)]
    rt1 = arena.alloc(128, 512)
    rt2 = arena.alloc(128, 512)
    pe0, a0, dv20 = sem_pe.n, sem_act.n, sem_dve2.n
    ldP.pre([s for s in range(NSUP) for _ in range(16)])
    wqP.pre([g for g in range(NSUP * 16) for _ in range(2)])
    tbP.pre([i for i in range(NSLICE) for _ in range(2)])
    stP.pre(range(NSUP * 16))
    st2P.pre(range(NSLICE))
    PE_S = 20
    ACT_S = 16

    @block.sync
    def _(sync):
        for dc in range(16):
            for h in range(16):
                sem_ld2 = c["sem_ld2"]
                sem_ld2.dinc(sync.dma_start(
                    wqr_dc(dc)[:, h * 64:(h + 1) * 64],
                    c["s_wqbf"].ap()[dc * 128:(dc + 1) * 128,
                                     h * DQK + DNOPE: h * DQK + DQK]))
        for s in range(NSUP):
            if s >= 2:
                sync.wait_ge(sem_pe.h, pe0 + PE_S * (s - 1))
            for dc in range(16):
                ldP.issue(s, sync.dma_start(
                    xf[s % 2][:, dc * 512:(dc + 1) * 512],
                    c["s_xn1"].ap()[s * SUP:(s + 1) * SUP, dc * 128:(dc + 1) * 128],
                    transpose=True))
            def qn_store(h):
                k = s * 16 + h
                sync.wait_ge(sem_act.h, a0 + ACT_S * s + h + 1)
                stP.issue(k, sync.dma_start(
                    c["s_qnope"].ap()[h * 128:(h + 1) * 128, s * SUP:(s + 1) * SUP],
                    qnb[h % 2][:]))

            for h in range(16):
                g = s * 16 + h
                if h >= 2:
                    qn_store(h - 2)
                if g >= 2:
                    sync.wait_ge(sem_pe.h, pe0 + PE_S * ((g - 2) // 16) + ((g - 2) % 16) + 1)
                for half2 in range(2):
                    wqP.issue(g, sync.dma_start(
                        wqn[g % 2][:].rearrange("p (dc z) -> p dc z", z=128)[
                            :, half2 * 8:(half2 + 1) * 8, :],
                        c["s_wqbf"].ap()[half2 * 1024:(half2 + 1) * 1024,
                                         h * DQK: h * DQK + DNOPE].rearrange(
                            "(dc p) z -> p dc z", p=128)))
            qn_store(14)
            qn_store(15)
            for j in range(4):
                i = s * 4 + j
                if i >= 2:
                    sync.wait_ge(sem_dve2.h, dv20 + (i - 1))
                pos0 = (i % 16) * 128
                tbP.issue(i, sync.dma_start(tabc[i % 2][:],
                                            c["QROPEC"].ap()[pos0:pos0 + 128, :]))
                tbP.issue(i, sync.dma_start(tabs[i % 2][:],
                                            c["QROPES"].ap()[pos0:pos0 + 128, :]))
            for j in range(4):
                i = s * 4 + j
                sync.wait_ge(sem_dve2.h, dv20 + i + 1)
                st2P.issue(i, sync.dma_start(
                    c["s_qrope"].ap()[i * TS:(i + 1) * TS, :], qrb[j % 2][:]))
        sync.wait_ge(stP.s[0].h, stP.s[0].n)
        sync.wait_ge(stP.s[1].h, stP.s[1].n)
        sync.wait_ge(st2P.s[0].h, st2P.s[0].n)
        sync.wait_ge(st2P.s[1].h, st2P.s[1].n)

    LD2_ALL = c["sem_ld2"].n

    @block.tensor
    def _(te):
        te.wait_ge(c["sem_ld2"].h, LD2_ALL)
        for s in range(NSUP):
            ldP.wait(te, s)
            for h in range(16):
                g = s * 16 + h
                wqP.wait(te, g)
                reg = [psA[:, 0:512], psA[:, 512:1024]][h % 2]
                if g >= 2:
                    te.wait_ge(sem_act.h, a0 + ACT_S * ((g - 2) // 16) + ((g - 2) % 16) + 1)
                for dc in range(16):
                    inst = te.matmul(reg,
                                     wqn[g % 2][:, dc * 128:(dc + 1) * 128],
                                     xf[s % 2][:, dc * 512:(dc + 1) * 512],
                                     start=(dc == 0), stop=(dc == 15))
                sem_pe.einc(inst)
            for j in range(4):
                i = s * 4 + j
                reg = [psB[:, 0:1024], psB[:, 1024:2048]][j % 2]
                if i >= 2:
                    te.wait_ge(sem_dve2.h, dv20 + (i - 1))
                for half in range(2):
                    for dc in range(16):
                        inst = te.matmul(
                            reg[:, half * 512:(half + 1) * 512],
                            xf[s % 2][:, dc * 512 + j * 128: dc * 512 + j * 128 + 128],
                            wqr_dc(dc)[:, half * 512:(half + 1) * 512],
                            start=(dc == 0), stop=(dc == 15))
                sem_pe.einc(inst)

    @block.scalar
    def _(sc):
        for s in range(NSUP):
            for h in range(16):
                g = s * 16 + h
                k = s * 16 + h
                sc.wait_ge(sem_pe.h, pe0 + PE_S * s + h + 1)
                if g >= 2:
                    kp = k - 2 if h >= 2 else (s - 1) * 16 + 14 + h
                    stP.wait(sc, kp)
                reg = [psA[:, 0:512], psA[:, 512:1024]][h % 2]
                sem_act.einc(sc.activation(qnb[h % 2][:], reg, AF.Copy,
                                           bias=0.0, scale=np.float32(SCALE_QK)))

    @block.vector
    def _(v):
        V = mk_chain(v, sem_vself)
        V(v.memset(qrb[0][:], 0.0))
        V(v.memset(qrb[1][:], 0.0))
        for s in range(NSUP):
            for j in range(4):
                i = s * 4 + j
                v.wait_ge(sem_pe.h, pe0 + PE_S * s + 16 + j + 1)
                tbP.wait(v, i)
                if i >= 2:
                    st2P.wait(v, i - 2)
                reg = [psB[:, 0:1024], psB[:, 1024:2048]][j % 2]
                rr = reg.rearrange("p (h q) -> p h q", q=64)
                x1 = rr.rearrange("p h (j t) -> p h j t", t=2)[:, :, :, 0:1].squeeze()
                x2 = rr.rearrange("p h (j t) -> p h j t", t=2)[:, :, :, 1:2].squeeze()
                ctab = tabc[i % 2][:].rearrange("p (h j) -> p h j", j=32)
                stab = tabs[i % 2][:].rearrange("p (h j) -> p h j", j=32)
                r1v = rt1[:].rearrange("p (h j) -> p h j", j=32)
                r2v = rt2[:].rearrange("p (h j) -> p h j", j=32)
                ov = qrb[j % 2][:].rearrange("p (h q) -> p h q", q=128)[:, :, 0:64]
                o1 = ov.rearrange("p h (j t) -> p h j t", t=2)[:, :, :, 0:1].squeeze()
                o2 = ov.rearrange("p h (j t) -> p h j t", t=2)[:, :, :, 1:2].squeeze()
                V(v.tensor_tensor(r1v, x1, ctab, OP.mult))
                V(v.tensor_tensor(r2v, x2, stab, OP.mult))
                V(v.tensor_tensor(o1, r1v, r2v, OP.subtract))
                V(v.tensor_tensor(r1v, x1, stab, OP.mult))
                V(v.tensor_tensor(r2v, x2, ctab, OP.mult))
                n_ = sem_dve2.einc(v.tensor_tensor(o2, r1v, r2v, OP.add))
                v.wait_ge(sem_dve2.h, n_)


def ph_kv(c, block):
    arena = c["arena"]
    sem_pe, sem_act, sem_dve, sem_dve2 = (c["sem_pe"], c["sem_act"],
                                          c["sem_dve"], c["sem_dve2"])
    sem_vself, sem_sself = c["sem_vself"], c["sem_sself"]
    ldP = PSem(c, "sem_ld", "sem_ldB")
    tbP = PSem(c, "sem_p2a", "sem_p2b")
    stP = PSem(c, "sem_st", "sem_stB")
    st2P = PSem(c, "sem_st2a", "sem_st2b")
    sem_ld2 = c["sem_ld2"]
    psA, psB = c["psA"], c["psB"]
    xf = [arena.alloc(128, 16 * 512, BF16) for _ in range(2)]
    wkva = arena.alloc(128, 16 * (KVL + DROPE), BF16)
    ckb = [arena.alloc(128, KVL, BF16) for _ in range(2)]
    krb = [arena.alloc(128, 128, BF16) for _ in range(2)]
    tabc = [arena.alloc(128, 32) for _ in range(2)]
    tabs = [arena.alloc(128, 32) for _ in range(2)]
    scr = arena.alloc(128, KVL)
    ssq = arena.alloc(128, 1)
    rstd = arena.alloc(128, 1)
    rt1 = arena.alloc(128, 32)
    rt2 = arena.alloc(128, 32)
    pe0, a0, dv0, dv20 = sem_pe.n, sem_act.n, sem_dve.n, sem_dve2.n
    ldP.pre([s for s in range(NSUP) for _ in range(16)])
    tbP.pre([i for i in range(NSLICE) for _ in range(2)])
    stP.pre(range(NSLICE))
    st2P.pre(range(NSLICE))

    @block.gpsimd
    def _(gp):
        for dc in range(16):
            c["sem_gp"].dinc(gp.dma_start(
                wkva[:, dc * 576:(dc + 1) * 576],
                c["Wkv_a"].ap()[dc * 128:(dc + 1) * 128, :]))
    GP_ALL = c["sem_gp"].n

    @block.sync
    def _(sync):
        for s in range(NSUP):
            if s >= 2:
                sync.wait_ge(sem_pe.h, pe0 + 4 * (s - 1))
            for dc in range(16):
                ldP.issue(s, sync.dma_start(
                    xf[s % 2][:, dc * 512:(dc + 1) * 512],
                    c["s_xn1"].ap()[s * SUP:(s + 1) * SUP, dc * 128:(dc + 1) * 128],
                    transpose=True))
            for j in range(4):
                i = s * 4 + j
                if i >= 2:
                    sync.wait_ge(sem_dve2.h, dv20 + (i - 1))
                pos0 = (i % 16) * 128
                tbP.issue(i, sync.dma_start(tabc[i % 2][:],
                                            c["ROPEC"].ap()[pos0:pos0 + 128, 0:32]))
                tbP.issue(i, sync.dma_start(tabs[i % 2][:],
                                            c["ROPES"].ap()[pos0:pos0 + 128, 0:32]))
            for j in range(4):
                i = s * 4 + j
                sync.wait_ge(sem_dve.h, dv0 + 2 * (i + 1))
                stP.issue(i, sync.dma_start(
                    c["s_ckvn"].ap()[i * TS:(i + 1) * TS, :], ckb[j % 2][:]))
                sync.wait_ge(sem_dve2.h, dv20 + i + 1)
                st2P.issue(i, sync.dma_start(
                    c["s_krope"].ap()[i * TS:(i + 1) * TS, :], krb[j % 2][:]))
        sync.wait_ge(stP.s[0].h, stP.s[0].n)
        sync.wait_ge(stP.s[1].h, stP.s[1].n)
        sync.wait_ge(st2P.s[0].h, st2P.s[0].n)
        sync.wait_ge(st2P.s[1].h, st2P.s[1].n)

    @block.tensor
    def _(te):
        te.wait_ge(c["sem_gp"].h, GP_ALL)
        for s in range(NSUP):
            ldP.wait(te, s)
            for j in range(4):
                i = s * 4 + j
                ps = [psA, psB][j % 2]
                if i >= 2:
                    te.wait_ge(sem_dve.h, dv0 + 2 * (i - 1))
                    te.wait_ge(sem_dve2.h, dv20 + i - 1)
                for grp, (c0, cw) in enumerate([(0, 512), (512, 64)]):
                    for dc in range(16):
                        inst = te.matmul(
                            ps[:, c0:c0 + cw],
                            xf[s % 2][:, dc * 512 + j * 128: dc * 512 + j * 128 + 128],
                            wkva[:, dc * 576 + c0: dc * 576 + c0 + cw],
                            start=(dc == 0), stop=(dc == 15))
                sem_pe.einc(inst)

    @block.scalar
    def _(sc):
        SS = mk_chain(sc, sem_sself)
        for s in range(NSUP):
            for j in range(4):
                i = s * 4 + j
                ps = [psA, psB][j % 2]
                sc.wait_ge(sem_pe.h, pe0 + i + 1)
                if i >= 1:
                    sc.wait_ge(sem_dve.h, dv0 + 2 * (i - 1) + 1)
                sem_act.einc(sc.activation(scr[:], ps[:, 0:KVL], AF.Square))
                sc.wait_ge(sem_dve.h, dv0 + 2 * i + 1)
                SS(sc.activation(rstd[:], rstd[:], AF.Ln))
                sem_act.einc(sc.activation(rstd[:], rstd[:], AF.Exp,
                                           scale=np.float32(-0.5)))

    @block.vector
    def _(v):
        V = mk_chain(v, sem_vself)
        V(v.memset(krb[0][:], 0.0))
        V(v.memset(krb[1][:], 0.0))
        for s in range(NSUP):
            for j in range(4):
                i = s * 4 + j
                ps = [psA, psB][j % 2]
                v.wait_ge(sem_act.h, a0 + 2 * i + 1)
                V(v.tensor_reduce(ssq[:], scr[:], AX.X, OP.add))
                n_ = sem_dve.einc(v.tensor_scalar(rstd[:], ssq[:], 1.0 / KVL,
                                                  float(EPS_RMS), OP.mult, OP.add))
                v.wait_ge(sem_dve.h, n_)
                v.wait_ge(sem_act.h, a0 + 2 * i + 2)
                if i >= 2:
                    stP.wait(v, i - 2)
                sem_dve.einc(v.tensor_scalar(ckb[j % 2][:], ps[:, 0:KVL],
                                             rstd[:], None, OP.mult, OP.bypass))
                tbP.wait(v, i)
                if i >= 2:
                    st2P.wait(v, i - 2)
                rr = ps[:, KVL:KVL + DROPE]
                x1 = rr.rearrange("p (j t) -> p j t", t=2)[:, :, 0:1].squeeze()
                x2 = rr.rearrange("p (j t) -> p j t", t=2)[:, :, 1:2].squeeze()
                ov = krb[j % 2][:, 0:DROPE]
                o1 = ov.rearrange("p (j t) -> p j t", t=2)[:, :, 0:1].squeeze()
                o2 = ov.rearrange("p (j t) -> p j t", t=2)[:, :, 1:2].squeeze()
                V(v.tensor_tensor(rt1[:], x1, tabc[i % 2][:], OP.mult))
                V(v.tensor_tensor(rt2[:], x2, tabs[i % 2][:], OP.mult))
                V(v.tensor_tensor(o1, rt1[:], rt2[:], OP.subtract))
                V(v.tensor_tensor(rt1[:], x1, tabs[i % 2][:], OP.mult))
                V(v.tensor_tensor(rt2[:], x2, tabc[i % 2][:], OP.mult))
                n_ = sem_dve2.einc(v.tensor_tensor(o2, rt1[:], rt2[:], OP.add))
                v.wait_ge(sem_dve2.h, n_)


def ph_kvb(c, block):
    arena = c["arena"]
    sem_pe, sem_act, sem_ld2 = c["sem_pe"], c["sem_act"], c["sem_ld2"]
    ldP = PSem(c, "sem_ld", "sem_ldB")
    stP = PSem(c, "sem_st", "sem_stB")
    st2P = PSem(c, "sem_st2a", "sem_st2b")
    psA, psB = c["psA"], c["psB"]
    wkvb = arena.alloc(128, 4 * 4096, BF16)
    ckf = [arena.alloc(128, 4 * 512, BF16) for _ in range(2)]
    knb = [arena.alloc(128, 512, BF16) for _ in range(2)]
    vb = [arena.alloc(128, 2048, BF16) for _ in range(2)]
    pe0, a0 = sem_pe.n, sem_act.n
    ldP.pre([s for s in range(NSUP) for _ in range(4)])
    stP.pre(range(NSUP * 16))
    st2P.pre(range(NSUP * 4))
    PE_S = 20
    ACT_S = 20

    @block.sync
    def _(sync):
        for lc in range(4):
            sem_ld2.dinc(sync.dma_start(wkvb[:, lc * 4096:(lc + 1) * 4096],
                                        c["s_wkvbbf"].ap()[lc * 128:(lc + 1) * 128, :]))
        for s in range(NSUP):
            if s >= 2:
                sync.wait_ge(sem_pe.h, pe0 + PE_S * (s - 1))
            for lc in range(4):
                ldP.issue(s, sync.dma_start(
                    ckf[s % 2][:, lc * 512:(lc + 1) * 512],
                    c["s_ckvn"].ap()[s * SUP:(s + 1) * SUP, lc * 128:(lc + 1) * 128],
                    transpose=True))
            seq, pos = s // 4, (s % 4) * SUP
            for h in range(16):
                k = s * 16 + h
                sync.wait_ge(sem_act.h, a0 + ACT_S * s + h + 1)
                stP.issue(k, sync.dma_start(
                    c["s_knope"].ap()[(seq * NH + h) * DNOPE:(seq * NH + h + 1) * DNOPE,
                                      pos:pos + SUP],
                    knb[h % 2][:]))
            for j in range(4):
                k = s * 4 + j
                sync.wait_ge(sem_act.h, a0 + ACT_S * s + 16 + j + 1)
                st2P.issue(k, sync.dma_start(
                    c["s_v"].ap()[s * SUP + j * TS: s * SUP + (j + 1) * TS, :],
                    vb[j % 2][:]))
        sync.wait_ge(stP.s[0].h, stP.s[0].n)
        sync.wait_ge(stP.s[1].h, stP.s[1].n)
        sync.wait_ge(st2P.s[0].h, st2P.s[0].n)
        sync.wait_ge(st2P.s[1].h, st2P.s[1].n)

    LD2_ALL = sem_ld2.n

    @block.tensor
    def _(te):
        te.wait_ge(sem_ld2.h, LD2_ALL)
        for s in range(NSUP):
            ldP.wait(te, s)
            for h in range(16):
                if h >= 4 or s >= 1:
                    kk = h - 4 if h >= 4 else 12 + h
                    ss = s if h >= 4 else s - 1
                    te.wait_ge(sem_act.h, a0 + ACT_S * ss + kk + 1)
                for lc in range(4):
                    inst = te.matmul(
                        psA[:, (h % 4) * 512:(h % 4 + 1) * 512],
                        wkvb[:, lc * 4096 + h * 256: lc * 4096 + h * 256 + 128],
                        ckf[s % 2][:, lc * 512:(lc + 1) * 512],
                        start=(lc == 0), stop=(lc == 3))
                sem_pe.einc(inst)
            for j in range(4):
                if j >= 1 or s >= 1:
                    jj = j - 1 if j >= 1 else 3
                    ss = s if j >= 1 else s - 1
                    te.wait_ge(sem_act.h, a0 + ACT_S * ss + 16 + jj + 1)
                for bs in range(4):
                    for lc in range(4):
                        vview = wkvb[:, lc * 4096: (lc + 1) * 4096].rearrange(
                            "p (h z) -> p h z", z=256)[:, 4 * bs:4 * bs + 4, 128:256]
                        inst = te.matmul(
                            psB[:, bs * 512:(bs + 1) * 512],
                            ckf[s % 2][:, lc * 512 + j * 128: lc * 512 + j * 128 + 128],
                            vview,
                            start=(lc == 0), stop=(lc == 3))
                sem_pe.einc(inst)

    @block.scalar
    def _(sc):
        for s in range(NSUP):
            for h in range(16):
                k = s * 16 + h
                sc.wait_ge(sem_pe.h, pe0 + PE_S * s + h + 1)
                if k >= 2:
                    kp = k - 2 if h >= 2 else (s - 1) * 16 + 14 + h
                    stP.wait(sc, kp)
                sem_act.einc(sc.activation(knb[h % 2][:],
                                           psA[:, (h % 4) * 512:(h % 4 + 1) * 512],
                                           AF.Copy, bias=0.0))
            for j in range(4):
                k = s * 4 + j
                sc.wait_ge(sem_pe.h, pe0 + PE_S * s + 16 + j + 1)
                if k >= 2:
                    st2P.wait(sc, k - 2)
                sem_act.einc(sc.activation(vb[j % 2][:], psB[:], AF.Copy, bias=0.0))


def _attn_sched():
    seq = []
    nd = d = n = 0
    for qs in range(4):
        for kc in range(4 * qs + 4):
            diag = kc >= 4 * qs
            seq.append((qs, kc, diag, nd, d, n))
            if diag:
                d += 1
            else:
                nd += 1
            n += 1
    return seq

ATTN_SEQ = _attn_sched()
N_EXP = len(ATTN_SEQ)
ACT_G = 3 + N_EXP
POOL_G = 2 + 16


def _pe_vals():
    vals = {}
    v = 0
    for name in ("qs2", "ks2", "bcast"):
        v += 1
        vals[name] = v
    for qs in range(4):
        K = 4 * qs + 4
        evs = [e for e in ATTN_SEQ if e[0] == qs]
        for e in evs:
            v += 1
            vals[("S", e[5])] = v
        for e in evs:
            last = e[1] == K - 1
            v += 2 if last else 1
            vals[("RP", e[5])] = v
            if last:
                vals[("qsdone", qs)] = v
    return vals, v

PE_VALS, PE_G = _pe_vals()


def ph_attn(c, block):
    arena = c["arena"]
    sem_pe, sem_act, sem_dve = c["sem_pe"], c["sem_act"], c["sem_dve"]
    sem_dve2, sem_pool = c["sem_dve2"], c["sem_pool"]
    sem_vself = c["sem_vself"]
    sem_ld2 = c["sem_ld2"]
    ldP = PSem(c, "sem_ld", "sem_ldB")
    stP = PSem(c, "sem_st", "sem_stB")
    psA, psB = c["psA"], c["psB"]
    qn = [arena.alloc(128, T, BF16) for _ in range(2)]
    qr = [arena.alloc(128, T, BF16) for _ in range(2)]
    kn = [arena.alloc(128, T, BF16) for _ in range(2)]
    vv = [arena.alloc(128, 16 * DV, BF16) for _ in range(2)]
    krf = [arena.alloc(128, T, BF16) for _ in range(2)]
    tri = arena.alloc(128, 1024)
    qsq = arena.alloc(128, T, BF16)
    qsq2 = arena.alloc(64, T, BF16)
    attnq = arena.alloc(128, 16 * 512, BF16)
    smsk = [arena.alloc(128, 512) for _ in range(4)]
    rb = arena.alloc(128, 512)
    osb = [arena.alloc(128, 512, BF16) for _ in range(2)]
    mq = arena.alloc(1, 1)
    mk = arena.alloc(1, 1)
    nb = arena.alloc(1, 1)
    nbB = arena.alloc(128, 1)
    pe0, a0, dv0, dv20, p0 = (sem_pe.n, sem_act.n, sem_dve.n, sem_dve2.n,
                              sem_pool.n)
    ldP.pre([g for g in range(32) for _ in range(7)])
    stP.pre(range(32 * 4))

    def actT(g, name, n=0):
        base = a0 + ACT_G * g
        return base + {"sqq": 1, "sqk": 2, "sqrt": 3}.get(name, 4 + n)

    def peT(g, name, n=0, qs=0):
        base = pe0 + PE_G * g
        if name in ("qs2", "ks2", "bcast"):
            return base + PE_VALS[name]
        if name == "qsdone":
            return base + PE_VALS[("qsdone", qs)]
        return base + PE_VALS[(name, n)]

    def poolT(g, name, d=0):
        base = p0 + POOL_G * g
        return base + {"b1": 1, "b2": 2}.get(name, 3 + d)

    def qsdone_before(n):
        cnt = 0
        ncum = 0
        for q in range(4):
            ncum += 4 * q + 4
            if n >= ncum:
                cnt += 1
        return cnt

    @block.sync
    def _(sync):
        sem_ld2.dinc(sync.dma_start(tri[:], c["TRI"].ap()))
        for sq in range(2):
            sem_ld2.dinc(sync.dma_start(
                krf[sq][:], c["s_krope"].ap()[sq * T:(sq + 1) * T, :],
                transpose=True))
        for g in range(32):
            sq, h = g // 16, g % 16
            if g >= 2:
                sync.wait_ge(sem_dve.h, dv0 + 4 * (g - 2) + 4)
            ldP.issue(g, sync.dma_start(
                qn[g % 2][:], c["s_qnope"].ap()[h * 128:(h + 1) * 128,
                                                sq * T:(sq + 1) * T]))
            ldP.issue(g, sync.dma_start(
                qr[g % 2][:], c["s_qrope"].ap()[sq * T:(sq + 1) * T,
                                                h * 128:(h + 1) * 128],
                transpose=True))
            ldP.issue(g, sync.dma_start(
                kn[g % 2][:],
                c["s_knope"].ap()[(sq * NH + h) * DNOPE:(sq * NH + h + 1) * DNOPE, :]))
            for part in range(4):
                ldP.issue(g, sync.dma_start(
                    vv[g % 2][:].rearrange("p (kc z) -> p kc z", z=DV)[
                        :, part * 4:(part + 1) * 4, :],
                    c["s_v"].ap()[sq * T + part * 512: sq * T + (part + 1) * 512,
                                  h * DV:(h + 1) * DV].rearrange(
                        "(kc p) z -> p kc z", p=128)))
            for qs in range(4):
                sync.wait_ge(sem_dve.h, dv0 + 4 * g + qs + 1)
                stP.issue(4 * g + qs, sync.dma_start(
                    c["s_o"].ap()[(sq * NH + h) * DV:(sq * NH + h + 1) * DV,
                                  qs * 512:(qs + 1) * 512],
                    osb[qs % 2][:]))
        sync.wait_ge(stP.s[0].h, stP.s[0].n)
        sync.wait_ge(stP.s[1].h, stP.s[1].n)

    LD2_ALL = sem_ld2.n

    @block.tensor
    def _(te):
        te.wait_ge(sem_ld2.h, LD2_ALL)
        for g in range(32):
            sq = g // 16
            ldP.wait(te, g)
            te.wait_ge(sem_act.h, actT(g, "sqq"))
            for sl in range(4):
                te.matmul(psA[0:1, sl * 512:(sl + 1) * 512],
                          c["k_ones_bf"][:, 0:1],
                          qsq[:, sl * 512:(sl + 1) * 512], start=True, stop=False)
                inst = te.matmul(psA[0:1, sl * 512:(sl + 1) * 512],
                                 c["k_ones_bf"][0:64, 0:1],
                                 qsq2[:, sl * 512:(sl + 1) * 512],
                                 start=False, stop=True)
            sem_pe.einc(inst)
            te.wait_ge(sem_act.h, actT(g, "sqk"))
            for sl in range(4):
                te.matmul(psB[0:1, sl * 512:(sl + 1) * 512],
                          c["k_ones_bf"][:, 0:1],
                          qsq[:, sl * 512:(sl + 1) * 512], start=True, stop=False)
                inst = te.matmul(psB[0:1, sl * 512:(sl + 1) * 512],
                                 c["k_ones_bf"][0:64, 0:1],
                                 qsq2[:, sl * 512:(sl + 1) * 512],
                                 start=False, stop=True)
            sem_pe.einc(inst)
            te.wait_ge(sem_pool.h, poolT(g, "b2"))
            sem_pe.einc(te.matmul(psB[:, 0:1], c["k_ones_f"][0:1, :],
                                  nb[:], start=True, stop=True))
            for qs in range(4):
                K = 4 * qs + 4
                evs = [e for e in ATTN_SEQ if e[0] == qs]
                for (qs_, kc, diag, nd, d, n) in evs:
                    reg = psA[:, (n % 4) * 512: (n % 4 + 1) * 512]
                    if n >= 4:
                        te.wait_ge(sem_act.h, actT(g, "exp", n - 4))
                    te.matmul(reg, kn[g % 2][:, kc * 128:(kc + 1) * 128],
                              qn[g % 2][:, qs * 512:(qs + 1) * 512],
                              start=True, stop=False)
                    sem_pe.einc(te.matmul(reg, krf[sq][0:64, kc * 128:(kc + 1) * 128],
                                          qr[g % 2][0:64, qs * 512:(qs + 1) * 512],
                                          start=False, stop=True))
                sumb = psB[:, (qs % 2) * 1024: (qs % 2) * 1024 + 512]
                ot = psB[:, (qs % 2) * 1024 + 512: (qs % 2) * 1024 + 1024]
                for (qs_, kc, diag, nd, d, n) in evs:
                    te.wait_ge(sem_act.h, actT(g, "exp", n))
                    if qs >= 2 and kc == 0:
                        te.wait_ge(sem_dve.h, dv0 + 4 * g + (qs - 2) + 1)
                    last = kc == K - 1
                    ab = attnq[:, kc * 512:(kc + 1) * 512]
                    te.matmul(sumb, c["k_ones_bf"][:, 0:128], ab,
                              start=(kc == 0), stop=last)
                    inst = te.matmul(
                        ot, vv[g % 2][:, kc * DV:(kc + 1) * DV], ab,
                        start=(kc == 0), stop=last)
                    inst.then_inc(sem_pe.h, 2 if last else 1)
                    sem_pe.n += 2 if last else 1

    @block.scalar
    def _(sc):
        SS = mk_chain(sc, c["sem_sself"])
        for g in range(32):
            ldP.wait(sc, g)
            sc.activation(qsq[:], qn[g % 2][:], AF.Square)
            sem_act.einc(sc.activation(qsq2[:], qr[g % 2][0:64, :], AF.Square))
            te_dummy = None
            sc.wait_ge(sem_pe.h, peT(g, "qs2"))
            sc.activation(qsq[:], kn[g % 2][:], AF.Square)
            sem_act.einc(sc.activation(qsq2[:], krf[g // 16][0:64, :], AF.Square))
            sc.wait_ge(sem_pool.h, poolT(g, "b1"))
            sem_act.einc(sc.activation(nb[:], nb[:], AF.Sqrt))
            for (qs, kc, diag, nd, d, n) in ATTN_SEQ:
                sc.wait_ge(sem_pe.h, peT(g, "S", n))
                sc.wait_ge(sem_dve2.h, dv20 + g + 1)
                if qs >= 1 and kc < 4 * qs:
                    prev_n = [e[5] for e in ATTN_SEQ
                              if e[0] == qs - 1 and e[1] == kc][0]
                    sc.wait_ge(sem_pe.h, peT(g, "RP", prev_n))
                if diag:
                    sc.wait_ge(sem_pool.h, poolT(g, "m", d))
                    src_ap = smsk[d % 4][:]
                else:
                    src_ap = psA[:, (n % 4) * 512: (n % 4 + 1) * 512]
                sem_act.einc(sc.activation(attnq[:, kc * 512:(kc + 1) * 512],
                                           src_ap, AF.Exp, bias=nbB[:]))

    @block.vector
    def _(v):
        V = mk_chain(v, sem_vself)
        for g in range(32):
            v.wait_ge(sem_pe.h, peT(g, "qs2"))
            V(v.tensor_reduce(mq[:], psA[0:1, 0:2048], AX.X, OP.max))
            v.wait_ge(sem_pe.h, peT(g, "ks2"))
            V(v.tensor_reduce(mk[:], psB[0:1, 0:2048], AX.X, OP.max))
            V(v.tensor_tensor(nb[:], mq[:], mk[:], OP.mult))
            n_ = sem_pool.einc(v.tensor_scalar(nb[:], nb[:], 1.02, None,
                                               OP.mult, OP.bypass))
            v.wait_ge(sem_pool.h, n_)
            v.wait_ge(sem_act.h, actT(g, "sqrt"))
            n_ = sem_pool.einc(v.tensor_scalar(nb[:], nb[:], -1.0, None,
                                               OP.mult, OP.bypass))
            v.wait_ge(sem_pool.h, n_)
            v.wait_ge(sem_pe.h, peT(g, "bcast"))
            n_ = sem_dve2.einc(v.tensor_copy(nbB[:], psB[:, 0:1]))
            v.wait_ge(sem_dve2.h, n_)
            for (qs, kc, diag, nd, d, n) in ATTN_SEQ:
                if diag:
                    v.wait_ge(sem_pe.h, peT(g, "S", n))
                    if qs >= 1:
                        prev_n = [e[5] for e in ATTN_SEQ
                                  if e[0] == qs - 1 and e[2] and e[4] % 4 == d % 4][0]
                        v.wait_ge(sem_act.h, actT(g, "exp", prev_n))
                    delta = kc * 128 - qs * 512
                    n_ = sem_pool.einc(v.tensor_tensor(
                        smsk[d % 4][:],
                        psA[:, (n % 4) * 512: (n % 4 + 1) * 512],
                        tri[:, 384 - delta: 384 - delta + 512], OP.add))
                    v.wait_ge(sem_pool.h, n_)
                if kc == 4 * qs + 3:
                    v.wait_ge(sem_pe.h, peT(g, "qsdone", qs=qs))
                    if 4 * g + qs >= 2:
                        stP.wait(v, 4 * g + qs - 2)
                    sumb = psB[:, (qs % 2) * 1024: (qs % 2) * 1024 + 512]
                    ot = psB[:, (qs % 2) * 1024 + 512: (qs % 2) * 1024 + 1024]
                    V(v.reciprocal(rb[:], sumb))
                    sem_dve.einc(v.tensor_tensor(osb[qs % 2][:], ot, rb[:], OP.mult))


def ph_wo(c, block):
    arena = c["arena"]
    sem_pe, sem_act, sem_gp = c["sem_pe"], c["sem_act"], c["sem_gp"]
    ldP = PSem(c, "sem_ld", "sem_ldB")
    stP = PSem(c, "sem_st", "sem_stB")
    psA, psB = c["psA"], c["psB"]
    wo = arena.alloc(128, 16 * 2048, BF16)
    ob = [arena.alloc(128, 16 * 128, BF16) for _ in range(2)]
    s1b = [arena.alloc(128, D) for _ in range(2)]
    pe0, a0 = sem_pe.n, sem_act.n
    ldP.pre([i for i in range(NSLICE) for _ in range(16)])
    stP.pre(range(NSLICE))

    @block.gpsimd
    def _(gp):
        for hc in range(16):
            sem_gp.dinc(gp.dma_start(wo[:, hc * 2048:(hc + 1) * 2048],
                                     c["Wo"].ap()[hc * 128:(hc + 1) * 128, :]))
    GP_ALL = sem_gp.n

    @block.sync
    def _(sync):
        for i in range(NSLICE):
            sq, pos = i // 16, (i % 16) * TS
            if i >= 2:
                sync.wait_ge(sem_pe.h, pe0 + (i - 1))
            for h in range(16):
                ldP.issue(i, sync.dma_start(
                    ob[i % 2][:, h * 128:(h + 1) * 128],
                    c["s_o"].ap()[(sq * NH + h) * DV:(sq * NH + h + 1) * DV,
                                  pos:pos + TS]))
            if i >= 1:
                sync.wait_ge(sem_act.h, a0 + i)
                stP.issue(i - 1, sync.dma_start(
                    c["s_s1"].ap()[(i - 1) * TS:i * TS, :], s1b[(i - 1) % 2][:]))
        sync.wait_ge(sem_act.h, a0 + NSLICE)
        stP.issue(NSLICE - 1, sync.dma_start(
            c["s_s1"].ap()[(NSLICE - 1) * TS:NSLICE * TS, :], s1b[(NSLICE - 1) % 2][:]))
        sync.wait_ge(stP.s[0].h, stP.s[0].n)
        sync.wait_ge(stP.s[1].h, stP.s[1].n)

    @block.tensor
    def _(te):
        te.wait_ge(sem_gp.h, GP_ALL)
        for i in range(NSLICE):
            ldP.wait(te, i)
            if i >= 2:
                te.wait_ge(sem_act.h, a0 + (i - 1))
            ps = [psA, psB][i % 2]
            for hc in range(16):
                for bs in range(4):
                    inst = te.matmul(ps[:, bs * 512:(bs + 1) * 512],
                                     ob[i % 2][:, hc * 128:(hc + 1) * 128],
                                     wo[:, hc * 2048 + bs * 512: hc * 2048 + (bs + 1) * 512],
                                     start=(hc == 0), stop=(hc == 15))
            sem_pe.einc(inst)

    @block.scalar
    def _(sc):
        for i in range(NSLICE):
            sc.wait_ge(sem_pe.h, pe0 + i + 1)
            if i >= 2:
                stP.wait(sc, i - 2)
            ps = [psA, psB][i % 2]
            sem_act.einc(sc.activation(s1b[i % 2][:], ps[:], AF.Copy, bias=0.0))


def ph_ffn1(c, block):
    arena = c["arena"]
    sem_pe, sem_act, sem_dve, sem_dve2 = (c["sem_pe"], c["sem_act"],
                                          c["sem_dve"], c["sem_dve2"])
    sem_vself, sem_sself = c["sem_vself"], c["sem_sself"]
    sem_ld2, sem_gp = c["sem_ld2"], c["sem_gp"]
    ldP = PSem(c, "sem_ld", "sem_ldB")
    stP = PSem(c, "sem_st", "sem_stB")
    psA, psB = c["psA"], c["psB"]
    slabbf = [t[:].bitcast(BF16) for t in c["slab"]]
    hbf = [t[:].bitcast(BF16) for t in c["Hsb"]]

    def bwT_dc(dc):
        return slabbf[dc // 4][:, (dc % 4) * 2048:(dc % 4 + 1) * 2048]

    laA = hbf[0][:, 0:2048]
    laB = hbf[1][:, 0:2048]
    rwT = arena.alloc(128, 16 * 16, BF16)
    xf = [arena.alloc(128, 16 * 512, BF16) for _ in range(2)]
    g1b = [arena.alloc(128, D) for _ in range(2)]
    gmax = arena.alloc(128, 1)
    nmax = arena.alloc(128, 1)
    m2_ = arena.alloc(128, 1)
    den = arena.alloc(128, 1)
    gsum = arena.alloc(128, 1)
    grec = arena.alloc(128, 1)
    gexp = arena.alloc(128, NE)
    gprob = arena.alloc(128, NE)
    gmask = arena.alloc(128, NE)
    ge2 = arena.alloc(128, NE)
    gate = arena.alloc(128, NE)
    gateT = arena.alloc(NE, 128)
    gateB = arena.alloc(128, 128)
    mids = arena.alloc(128, 128, BF16)
    pe0, a0, dv0, dv20 = sem_pe.n, sem_act.n, sem_dve.n, sem_dve2.n
    ldP.pre([s for s in range(NSUP) for _ in range(16)])
    stP.pre(range(NSLICE))
    PE_I, DVE_I, ACT_I = 5, 4, 2

    @block.gpsimd
    def _(gp):
        for dc in range(16):
            sem_gp.dinc(gp.dma_start(
                laA[:, dc * 128:(dc + 1) * 128].rearrange("p (e r) -> p e r", r=RL),
                c["lora_A"].ap().transpose([1, 0, 2])[dc * 128:(dc + 1) * 128, :, :]))
        sem_gp.dinc(gp.dma_start(
            laB[:], c["lora_B"].ap().rearrange("e r d -> (e r) d")))
    GP_ALL = sem_gp.n

    @block.sync
    def _(sync):
        for dc in range(16):
            sem_ld2.dinc(sync.dma_start(bwT_dc(dc),
                                        c["s_bwbf"].ap()[:, dc * 128:(dc + 1) * 128],
                                        transpose=True))
            sem_ld2.dinc(sync.dma_start(rwT[:, dc * 16:(dc + 1) * 16],
                                        c["s_rwbf"].ap()[:, dc * 128:(dc + 1) * 128],
                                        transpose=True))
        for s in range(NSUP):
            if s >= 2:
                sync.wait_ge(sem_pe.h, pe0 + PE_I * 4 * (s - 1))
            for dc in range(16):
                ldP.issue(s, sync.dma_start(
                    xf[s % 2][:, dc * 512:(dc + 1) * 512],
                    c["s_xn1"].ap()[s * SUP:(s + 1) * SUP, dc * 128:(dc + 1) * 128],
                    transpose=True))
            for j in range(4):
                i = s * 4 + j
                sync.wait_ge(sem_act.h, a0 + ACT_I * i + 2)
                stP.issue(i, sync.dma_start(
                    c["s_g1"].ap()[i * TS:(i + 1) * TS, :], g1b[i % 2][:]))
        sync.wait_ge(stP.s[0].h, stP.s[0].n)
        sync.wait_ge(stP.s[1].h, stP.s[1].n)

    LD2_ALL = sem_ld2.n

    @block.tensor
    def _(te):
        te.wait_ge(sem_ld2.h, LD2_ALL)
        te.wait_ge(sem_gp.h, GP_ALL)
        for s in range(NSUP):
            ldP.wait(te, s)
            for j in range(4):
                i = s * 4 + j
                xsl = xf[s % 2]
                if i >= 1:
                    te.wait_ge(sem_act.h, a0 + ACT_I * (i - 1) + 1)
                for dc in range(16):
                    te.matmul(psB[:, 0:8],
                              xsl[:, dc * 512 + j * 128: dc * 512 + j * 128 + 128],
                              rwT[:, dc * 16: dc * 16 + 8],
                              start=(dc == 0), stop=False)
                sem_pe.einc(te.matmul(psB[:, 0:8], c["k_ones_bf"][0:1, 0:128],
                                      c["rbrow"][0:1, :], start=False, stop=True))
                te.wait_ge(sem_dve.h, dv0 + DVE_I * i + 1)
                sem_pe.einc(te.transpose(psB[0:8, 512:640], gate[:], c["k_id_f"][:]))
                te.wait_ge(sem_dve.h, dv0 + DVE_I * i + 2)
                sem_pe.einc(te.matmul(psB[:, 640:768], c["k_sel2f"][:],
                                      gateT[:], start=True, stop=True))
                if i >= 1:
                    te.wait_ge(sem_act.h, a0 + ACT_I * (i - 1) + 2)
                for dc in range(16):
                    for bs in range(4):
                        te.matmul(psA[:, bs * 512:(bs + 1) * 512],
                                  xsl[:, dc * 512 + j * 128: dc * 512 + j * 128 + 128],
                                  bwT_dc(dc)[:, bs * 512:(bs + 1) * 512],
                                  start=(dc == 0), stop=False)
                te.wait_ge(sem_dve.h, dv0 + DVE_I * i + 3)
                for dc in range(16):
                    inst = te.matmul(psB[:, 768:896], laA[:, dc * 128:(dc + 1) * 128],
                                     xsl[:, dc * 512 + j * 128: dc * 512 + j * 128 + 128],
                                     start=(dc == 0), stop=(dc == 15))
                sem_pe.einc(inst)
                te.wait_ge(sem_dve.h, dv0 + DVE_I * i + 4)
                for bs in range(4):
                    te.matmul(psA[:, bs * 512:(bs + 1) * 512], mids[:],
                              laB[:, bs * 512:(bs + 1) * 512], start=False, stop=False)
                for bs in range(4):
                    inst = te.matmul(psA[:, bs * 512:(bs + 1) * 512],
                                     c["k_ones_bf"][0:1, 0:128],
                                     c["bbrow"][0:1, bs * 512:(bs + 1) * 512],
                                     start=False, stop=True)
                sem_pe.einc(inst)

    @block.scalar
    def _(sc):
        for i in range(NSLICE):
            sc.wait_ge(sem_dve2.h, dv20 + i + 1)
            sem_act.einc(sc.activation(gexp[:], psB[:, 0:8], AF.Exp, bias=nmax[:]))
            sc.wait_ge(sem_pe.h, pe0 + PE_I * i + 5)
            if i >= 2:
                stP.wait(sc, i - 2)
            sem_act.einc(sc.activation(g1b[i % 2][:], psA[:], AF.Silu))

    @block.vector
    def _(v):
        V = mk_chain(v, sem_vself)
        for i in range(NSLICE):
            v.wait_ge(sem_pe.h, pe0 + PE_I * i + 1)
            V(v.tensor_reduce(gmax[:], psB[:, 0:8], AX.X, OP.max))
            n_ = sem_dve2.einc(v.tensor_scalar(nmax[:], gmax[:], -1.0, None,
                                               OP.mult, OP.bypass))
            v.wait_ge(sem_dve2.h, n_)
            v.wait_ge(sem_act.h, a0 + ACT_I * i + 1)
            V(v.tensor_reduce(gsum[:], gexp[:], AX.X, OP.add))
            V(v.reciprocal(grec[:], gsum[:]))
            V(v.tensor_scalar(gprob[:], gexp[:], grec[:], None, OP.mult, OP.bypass))
            V(v.tensor_reduce(gmax[:], gprob[:], AX.X, OP.max))
            V(v.tensor_scalar(gmask[:], gprob[:], gmax[:], None, OP.is_ge, OP.bypass))
            V(v.tensor_tensor(gmask[:], gprob[:], gmask[:], OP.subtract))
            V(v.tensor_reduce(m2_[:], gmask[:], AX.X, OP.max))
            V(v.tensor_tensor(den[:], gmax[:], m2_[:], OP.add))
            V(v.reciprocal(den[:], den[:]))
            V(v.tensor_scalar(ge2[:], gprob[:], m2_[:], None, OP.is_ge, OP.bypass))
            V(v.tensor_tensor(gate[:], gprob[:], ge2[:], OP.mult))
            n_ = sem_dve.einc(v.tensor_scalar(gate[:], gate[:], den[:], None,
                                              OP.mult, OP.bypass))
            v.wait_ge(sem_dve.h, n_)
            v.wait_ge(sem_pe.h, pe0 + PE_I * i + 2)
            n_ = sem_dve.einc(v.tensor_copy(gateT[:], psB[0:8, 512:640]))
            v.wait_ge(sem_dve.h, n_)
            v.wait_ge(sem_pe.h, pe0 + PE_I * i + 3)
            n_ = sem_dve.einc(v.tensor_copy(gateB[:], psB[:, 640:768]))
            v.wait_ge(sem_dve.h, n_)
            v.wait_ge(sem_pe.h, pe0 + PE_I * i + 4)
            n_ = sem_dve.einc(v.tensor_tensor(mids[:], psB[:, 768:896], gateB[:],
                                              OP.mult))
            v.wait_ge(sem_dve.h, n_)


def ph_out(c, block):
    arena = c["arena"]
    sem_dve = c["sem_dve"]
    sem_vself = c["sem_vself"]
    ldP = PSem(c, "sem_ld", "sem_ldB")
    stP = PSem(c, "sem_st", "sem_stB")
    x1b = [arena.alloc(128, D) for _ in range(2)]
    s1b = [arena.alloc(128, D) for _ in range(2)]
    g1b = [arena.alloc(128, D) for _ in range(2)]
    outb = [arena.alloc(128, D) for _ in range(2)]
    dv0 = sem_dve.n
    ldP.pre([i for i in range(NSLICE) for _ in range(3)])
    stP.pre(range(NSLICE))

    @block.sync
    def _(sync):
        for i in range(NSLICE):
            if i >= 2:
                sync.wait_ge(sem_dve.h, dv0 + i - 1)
            ldP.issue(i, sync.dma_start(x1b[i % 2][:], c["s_x1"].ap()[i * TS:(i + 1) * TS, :]))
            ldP.issue(i, sync.dma_start(s1b[i % 2][:], c["s_s1"].ap()[i * TS:(i + 1) * TS, :]))
            ldP.issue(i, sync.dma_start(g1b[i % 2][:], c["s_g1"].ap()[i * TS:(i + 1) * TS, :]))
            if i >= 1:
                sync.wait_ge(sem_dve.h, dv0 + i)
                stP.issue(i - 1, sync.dma_start(
                    c["out"].ap()[(i - 1) * TS:i * TS, :], outb[(i - 1) % 2][:]))
        sync.wait_ge(sem_dve.h, dv0 + NSLICE)
        stP.issue(NSLICE - 1, sync.dma_start(
            c["out"].ap()[(NSLICE - 1) * TS:NSLICE * TS, :], outb[(NSLICE - 1) % 2][:]))
        sync.wait_ge(stP.s[0].h, stP.s[0].n)
        sync.wait_ge(stP.s[1].h, stP.s[1].n)

    @block.vector
    def _(v):
        V = mk_chain(v, sem_vself)
        for i in range(NSLICE):
            ldP.wait(v, i)
            if i >= 2:
                stP.wait(v, i - 2)
            V(v.tensor_tensor(outb[i % 2][:], s1b[i % 2][:], g1b[i % 2][:], OP.mult))
            sem_dve.einc(v.tensor_tensor(outb[i % 2][:], outb[i % 2][:],
                                         x1b[i % 2][:], OP.add))


def kernel(**inputs):
    consts = host_constants()
    x = np.ascontiguousarray(inputs["L0_attn_x"])
    nc = build_program(n_phases=PHASES_LIMIT, dump=DBG_DUMP)
    base = {
        "nu_log": inputs["L0_attn_nu_log"], "theta_log": inputs["L0_attn_theta_log"],
        "B_re": inputs["L0_attn_B_re"], "B_im": inputs["L0_attn_B_im"],
        "C_re": inputs["L0_attn_C_re"], "C_im": inputs["L0_attn_C_im"],
        "Dvec": inputs["L0_attn_D"],
        "proj1": inputs["L0_ffn_proj1"], "proj1_bias": inputs["L0_ffn_proj1_bias"],
        "proj2": inputs["L0_ffn_proj2"], "proj2_bias": inputs["L0_ffn_proj2_bias"],
        "Wq": inputs["L1_attn_Wq"], "Wkv_a": inputs["L1_attn_Wkv_a"],
        "kv_norm_w": inputs["L1_attn_kv_norm_w"], "Wkv_b": inputs["L1_attn_Wkv_b"],
        "Wo": inputs["L1_attn_Wo"],
        "base_w": inputs["L1_ffn_base_w"], "base_b": inputs["L1_ffn_base_b"],
        "router_w": inputs["L1_ffn_router_w"], "router_b": inputs["L1_ffn_router_b"],
        "lora_A": inputs["L1_ffn_lora_A"], "lora_B": inputs["L1_ffn_lora_B"],
    }
    base.update(consts)
    in_maps = []
    for cid in range(NCORE):
        m = dict(base)
        m["x"] = np.ascontiguousarray(x[cid * SEQS:(cid + 1) * SEQS].reshape(NTOK, D))
        in_maps.append(m)
    res = run_bass_kernel_spmd(nc, in_maps, core_ids=list(range(NCORE)))
    out = np.empty((BATCH, T, D), dtype=np.float32)
    for cid in range(NCORE):
        out[cid * SEQS:(cid + 1) * SEQS] = res.results[cid]["out"].reshape(SEQS, T, D)
    return out


# version 04 (final)
# speedup vs baseline: 1.3342x; 1.0137x over previous
import math
import numpy as np
import ml_dtypes
import concourse.bass as bass
import concourse.mybir as mybir
from concourse.bass_utils import run_bass_kernel_spmd
from concourse.alu_op_type import AluOpType as OP

F32 = mybir.dt.float32
BF16 = mybir.dt.bfloat16
AF = mybir.ActivationFunctionType
AX = mybir.AxisListType
f32 = np.float32
bf16 = ml_dtypes.bfloat16

NCORE = 8
BATCH = 16
SEQS = 2
T = 2048
NTOK = SEQS * T
D = 2048
TS = 128
NSLICE = NTOK // TS
SUP = 512
NSUP = NTOK // SUP
NST = 256
HFF = 8192
NH = 16
DQK = 192
DROPE = 64
DNOPE = 128
KVL = 512
DV = 128
NE = 8
RL = 16
SCALE_QK = 1.0 / math.sqrt(DQK)
EPS_LN = 1e-5
EPS_RMS = 1.1920928955078125e-07
NEG_BIG = -1e30
TWO_PI = 2.0 * math.pi
MAGIC = 12582912.0

DBG_DUMP = None
PHASES_LIMIT = 99


_const_cache = {}

def host_constants():
    if _const_cache:
        return _const_cache
    trow = np.broadcast_to(np.arange(T, dtype=np.float64), (128, T)).astype(f32).copy()
    freqs = 1.0 / (10000.0 ** (np.arange(0, DROPE, 2, dtype=np.float64) / DROPE))
    tpos = np.arange(T, dtype=np.float64)[:, None] * freqs[None, :]
    ropec = np.tile(np.cos(tpos), (1, NH)).astype(f32)
    ropes = np.tile(np.sin(tpos), (1, NH)).astype(f32)
    m = np.arange(1024) - 384
    tri = np.where(np.arange(128)[:, None] > m[None, :], f32(NEG_BIG), f32(0.0)).astype(f32)
    onesbf = np.ones((128, 128), dtype=bf16)
    onesf = np.ones((128, 128), dtype=f32)
    identbf = np.eye(128, dtype=bf16)
    identf = np.eye(128, dtype=f32)
    sel2 = np.zeros((NE, 128), dtype=bf16)
    for e in range(NE):
        sel2[e, e * RL:(e + 1) * RL] = 2.0
    sel2f = sel2.astype(f32)
    qropec = (ropec.astype(np.float64) * SCALE_QK).astype(f32)
    qropes = (ropes.astype(np.float64) * SCALE_QK).astype(f32)
    _const_cache.update(dict(QROPEC=qropec, QROPES=qropes,
                             TROW=trow, ROPEC=ropec, ROPES=ropes, TRI=tri,
                             ONESBF=onesbf, ONESF=onesf, IDENTBF=identbf,
                             IDENTF=identf, SEL2=sel2, SEL2F=sel2f))
    return _const_cache


class Sem:
    def __init__(self, nc, name):
        self.h = nc.alloc_semaphore(name)
        self.n = 0

    def dinc(self, inst):
        inst.then_inc(self.h, 16)
        self.n += 16
        return self.n

    def einc(self, inst):
        inst.then_inc(self.h, 1)
        self.n += 1
        return self.n


class PSem:

    def __init__(self, c, a, b):
        self.s = [c[a], c[b]]
        self.tgt = {}

    @staticmethod
    def _par(key):
        if isinstance(key, int):
            return key % 2
        if isinstance(key, str):
            return 0
        return key[0] % 2

    def pre(self, keys):
        cnt = [self.s[0].n, self.s[1].n]
        for key in keys:
            p = self._par(key)
            cnt[p] += 16
            self.tgt[key] = (p, cnt[p])

    def issue(self, key, inst):
        p = self._par(key)
        n = self.s[p].dinc(inst)
        if key in self.tgt:
            assert self.tgt[key] == (p, n) or self.tgt[key][1] >= n, (key, self.tgt[key], n)
        self.tgt[key] = (p, max(n, self.tgt.get(key, (p, 0))[1]))
        return n

    def wait(self, eng, key):
        p, n = self.tgt[key]
        eng.wait_ge(self.s[p].h, n)


def mk_chain(eng, sem):
    def _c(inst):
        n = sem.einc(inst)
        eng.wait_ge(sem.h, n)
        return inst
    return _c


class Arena:
    def __init__(self, nc, f32_cols):
        self.t = nc.alloc_sbuf_tensor("arena", [128, f32_cols], F32)
        self.cols = f32_cols
        self.off = 0

    def reset(self):
        self.off = 0

    def alloc(self, parts, cols, dt=F32):
        per = 4 if dt == F32 else 2
        ncol_f32 = (cols * per + 3) // 4
        self.off = (self.off + 7) & ~7
        assert self.off + ncol_f32 <= self.cols, \
            f"arena overflow {self.off}+{ncol_f32}>{self.cols}"
        base = self.off
        self.off += ncol_f32
        if dt != F32:
            return self.t[:].bitcast(dt)[0:parts, base * 2: base * 2 + cols]
        return self.t[0:parts, base: base + cols]


def build_program(n_phases=99, dump=None):
    nc = bass.Bass("TRN2", target_bir_lowering=False)
    c = {}
    c["nc"] = nc

    def dram(name, shape, dt, kind):
        h = nc.dram_tensor(name, shape, dt, kind=kind)
        c[name] = h
        return h

    dram("x", [NTOK, D], F32, "ExternalInput")
    dram("nu_log", [NST], F32, "ExternalInput")
    dram("theta_log", [NST], F32, "ExternalInput")
    dram("B_re", [NST, D], F32, "ExternalInput")
    dram("B_im", [NST, D], F32, "ExternalInput")
    dram("C_re", [D, NST], F32, "ExternalInput")
    dram("C_im", [D, NST], F32, "ExternalInput")
    dram("Dvec", [D], F32, "ExternalInput")
    dram("proj1", [HFF, D], F32, "ExternalInput")
    dram("proj1_bias", [HFF], F32, "ExternalInput")
    dram("proj2", [D, HFF], F32, "ExternalInput")
    dram("proj2_bias", [D], F32, "ExternalInput")
    dram("Wq", [D, NH * DQK], F32, "ExternalInput")
    dram("Wkv_a", [D, KVL + DROPE], F32, "ExternalInput")
    dram("kv_norm_w", [KVL], F32, "ExternalInput")
    dram("Wkv_b", [KVL, NH * (DNOPE + DV)], F32, "ExternalInput")
    dram("Wo", [NH * DV, D], F32, "ExternalInput")
    dram("base_w", [D, D], F32, "ExternalInput")
    dram("base_b", [D], F32, "ExternalInput")
    dram("router_w", [NE, D], F32, "ExternalInput")
    dram("router_b", [NE], F32, "ExternalInput")
    dram("lora_A", [NE, D, RL], F32, "ExternalInput")
    dram("lora_B", [NE, RL, D], F32, "ExternalInput")
    dram("TROW", [128, T], F32, "ExternalInput")
    dram("ROPEC", [T, 512], F32, "ExternalInput")
    dram("QROPEC", [T, 512], F32, "ExternalInput")
    dram("QROPES", [T, 512], F32, "ExternalInput")
    dram("ROPES", [T, 512], F32, "ExternalInput")
    dram("TRI", [128, 1024], F32, "ExternalInput")
    dram("ONESBF", [128, 128], BF16, "ExternalInput")
    dram("ONESF", [128, 128], F32, "ExternalInput")
    dram("IDENTBF", [128, 128], BF16, "ExternalInput")
    dram("IDENTF", [128, 128], F32, "ExternalInput")
    dram("SEL2", [NE, 128], BF16, "ExternalInput")
    dram("SEL2F", [NE, 128], F32, "ExternalInput")
    dram("s_p1bf", [HFF, D], BF16, "Internal")
    dram("s_p2bf", [D, HFF], BF16, "Internal")
    dram("s_p2t", [HFF, D], BF16, "Internal")
    dram("s_brebf", [NST, D], BF16, "Internal")
    dram("s_bimbf", [NST, D], BF16, "Internal")
    dram("s_crebf", [D, NST], BF16, "Internal")
    dram("s_cimbf", [D, NST], BF16, "Internal")
    dram("s_bwbf", [D, D], BF16, "Internal")
    dram("s_rwbf", [16, D], BF16, "Internal")
    dram("s_wqbf", [D, NH * DQK], BF16, "Internal")
    dram("s_wkvbbf", [KVL, NH * (DNOPE + DV)], BF16, "Internal")
    dram("s_xn0", [NTOK, D], BF16, "Internal")
    dram("s_g0", [NTOK, D], F32, "Internal")
    dram("s_x1", [NTOK, D], F32, "Internal")
    dram("s_xn1", [NTOK, D], BF16, "Internal")
    dram("s_qnope", [NH * DNOPE, NTOK], BF16, "Internal")
    dram("s_qrope", [NTOK, NH * 128], BF16, "Internal")
    dram("s_krope", [NTOK, 128], BF16, "Internal")
    dram("s_ckvn", [NTOK, KVL], BF16, "Internal")
    dram("s_knope", [SEQS * NH * DNOPE, T], BF16, "Internal")
    dram("s_v", [NTOK, NH * DV], BF16, "Internal")
    dram("s_o", [SEQS * NH * DV, T], BF16, "Internal")
    dram("s_s1", [NTOK, D], F32, "Internal")
    dram("s_g1", [NTOK, D], F32, "Internal")
    dram("out", [NTOK, D], F32, "ExternalOutput")

    c["slab"] = [nc.alloc_sbuf_tensor(f"slab{i}", [128, NTOK], F32) for i in range(4)]
    c["Hsb"] = [nc.alloc_sbuf_tensor(f"hsb{i}", [128, NTOK], BF16) for i in range(4)]
    for nm, shape, dt in [("st_m0", [128, NSLICE], F32), ("st_r0", [128, NSLICE], F32),
                          ("st_m1", [128, NSLICE], F32), ("st_r1", [128, NSLICE], F32),
                          ("v_r", [128, 2], F32), ("v_gam", [128, 2], F32),
                          ("v_th", [128, 2], F32),
                          ("b1_sb", [128, HFF // 128], F32),
                          ("b2row", [1, D], BF16), ("bbrow", [1, D], BF16),
                          ("rbrow", [1, NE], BF16), ("drow", [1, D], F32),
                          ("kvnw_sb", [128, KVL // 128], F32),
                          ("k_ones_bf", [128, 128], BF16), ("k_ones_f", [128, 128], F32),
                          ("k_id_bf", [128, 128], BF16), ("k_id_f", [128, 128], F32),
                          ("k_sel2", [NE, 128], BF16),
                          ("k_sel2f", [NE, 128], F32)]:
        c[nm] = nc.alloc_sbuf_tensor(nm, shape, dt)
    c["psA"] = nc.alloc_psum_tensor("psA", [128, 2048], F32)
    c["psB"] = nc.alloc_psum_tensor("psB", [128, 2048], F32)
    c["arena"] = Arena(nc, 23700)

    for s in ["sem_ld", "sem_ldB", "sem_ld2", "sem_gp", "sem_gpB", "sem_st",
              "sem_stB", "sem_pe", "sem_dve", "sem_dve2", "sem_act", "sem_pool",
              "sem_vself", "sem_sself", "sem_p1a", "sem_p1b", "sem_p2a", "sem_p2b",
              "sem_st2a", "sem_st2b", "sem_p3a", "sem_p3b"]:
        c[s] = Sem(nc, s)

    dump_map = {"xn0": "s_xn0", "g0": "s_g0", "x1": "s_x1", "xn1": "s_xn1",
                "qnope": "s_qnope", "qrope": "s_qrope", "g1": "s_g1",
                "out": "out", "krope": "s_krope", "ckvn": "s_ckvn",
                "knope": "s_knope", "v": "s_v", "o": "s_o", "s1": "s_s1",
                "p1bf": "s_p1bf", "brebf": "s_brebf"}

    phases = [ph_prologue, ph_wkvbfold, ph_ln0, ph_p2t, ph_bu, ph_scan, ph_ffn0,
              ph_cm, ph_ln1, ph_q, ph_kv, ph_kvb, ph_attn, ph_wo, ph_ffn1,
              ph_out]
    for ph in phases[:n_phases]:
        c["arena"].reset()
        with nc.named_scope(ph.__name__):
            with nc.Block() as block:
                ph(c, block)

    if dump is not None:
        sem_st = c["sem_st"]
        if dump in ("bu", "hsb"):
            tl = c["slab"] if dump == "bu" else c["Hsb"]
            dt_ = F32 if dump == "bu" else BF16
            ddbg = nc.dram_tensor("dbg", [4 * 128, NTOK], dt_, kind="ExternalOutput")
            with nc.Block() as block:
                @block.sync
                def _(sync):
                    for k in range(4):
                        sem_st.dinc(sync.dma_start(
                            ddbg.ap()[k * 128:(k + 1) * 128, :], tl[k][:]))
                    sync.wait_ge(sem_st.h, sem_st.n)
        else:
            src = c[dump_map[dump]]
            ddbg = nc.dram_tensor("dbg", list(src.shape), src.dtype, kind="ExternalOutput")
            with nc.Block() as block:
                @block.sync
                def _(sync):
                    sem_st.dinc(sync.dma_start(ddbg.ap(), src.ap()))
                    sync.wait_ge(sem_st.h, sem_st.n)
    return nc


def ph_prologue(c, block):
    arena = c["arena"]
    sem_ld2 = c["sem_ld2"]
    sem_dve, sem_act, sem_pe = c["sem_dve"], c["sem_act"], c["sem_pe"]
    gpP = PSem(c, "sem_gp", "sem_gpB")
    stP = PSem(c, "sem_st", "sem_stB")
    psA = c["psA"]

    casts = [(c["Wq"], c["s_wqbf"], D, NH * DQK),
             (c["proj1"], c["s_p1bf"], HFF, D),
             (c["proj2"], c["s_p2bf"], D, HFF),
             (c["base_w"], c["s_bwbf"], D, D),
             (c["B_re"], c["s_brebf"], NST, D),
             (c["B_im"], c["s_bimbf"], NST, D),
             (c["C_re"], c["s_crebf"], D, NST),
             (c["C_im"], c["s_cimbf"], D, NST),
             (c["router_w"], c["s_rwbf"], NE, D)]
    jobs = []
    for (src_, dst, rows, cols) in casts:
        for r0 in range(0, rows, 128):
            rr = min(128, rows - r0)
            jobs.append((src_, dst, r0, rr, cols))
    bufs = [arena.alloc(128, 8192, BF16) for _ in range(2)]
    gpP.pre(["z"] + list(range(len(jobs))))
    stP.pre(range(len(jobs)))

    zpad = arena.alloc(8, D, BF16)

    @block.gpsimd
    def _(gp):
        n_ = c["sem_pool"].einc(gp.memset(zpad[:], 0.0))
        gp.wait_ge(c["sem_pool"].h, n_)
        gpP.issue("z", gp.dma_start(c["s_rwbf"].ap()[8:16, :], zpad[:]))
        for i, (src_, dst, r0, rr, cols) in enumerate(jobs):
            if i >= 2:
                stP.wait(gp, i - 2)
            gpP.issue(i, gp.dma_start(bufs[i % 2][0:rr, 0:cols],
                                      src_.ap()[r0:r0 + rr, :]))

    @block.sync
    def _(sync):
        for nm, src_ in [("k_ones_bf", "ONESBF"), ("k_ones_f", "ONESF"),
                         ("k_id_bf", "IDENTBF"), ("k_id_f", "IDENTF"),
                         ("k_sel2", "SEL2"), ("k_sel2f", "SEL2F")]:
            sem_ld2.dinc(sync.dma_start(c[nm][:], c[src_].ap()))
        for i, (src_, dst, r0, rr, cols) in enumerate(jobs):
            gpP.wait(sync, i)
            stP.issue(i, sync.dma_start(dst.ap()[r0:r0 + rr, :],
                                        bufs[i % 2][0:rr, 0:cols]))
        sync.wait_ge(stP.s[0].h, stP.s[0].n)
        sync.wait_ge(stP.s[1].h, stP.s[1].n)

    nu2 = arena.alloc(2, 128)
    th2 = arena.alloc(2, 128)
    b1n = arena.alloc(64, 128)
    kvn4 = arena.alloc(4, 128)
    b2f = arena.alloc(1, D)
    bbf_ = arena.alloc(1, D)
    rbf_ = arena.alloc(1, NE)
    tmp = arena.alloc(128, 2)
    tmp2 = arena.alloc(128, 2)
    tmp3 = arena.alloc(128, 2)
    dv0, a0, pe0 = sem_dve.n, sem_act.n, sem_pe.n

    @block.scalar
    def _(sc):
        sem_ld2.dinc(sc.dma_start(nu2[:], c["nu_log"].ap().rearrange("(a b) -> a b", b=128)))
        sem_ld2.dinc(sc.dma_start(th2[:], c["theta_log"].ap().rearrange("(a b) -> a b", b=128)))
        sem_ld2.dinc(sc.dma_start(b1n[:], c["proj1_bias"].ap().rearrange("(a b) -> a b", b=128)))
        sem_ld2.dinc(sc.dma_start(kvn4[:], c["kv_norm_w"].ap().rearrange("(a b) -> a b", b=128)))
        sem_ld2.dinc(sc.dma_start(b2f[:], c["proj2_bias"].ap().rearrange("(o a) -> o a", o=1)))
        sem_ld2.dinc(sc.dma_start(bbf_[:], c["base_b"].ap().rearrange("(o a) -> o a", o=1)))
        sem_ld2.dinc(sc.dma_start(rbf_[:], c["router_b"].ap().rearrange("(o a) -> o a", o=1)))
        sem_ld2.dinc(sc.dma_start(c["drow"][:], c["Dvec"].ap().rearrange("(o a) -> o a", o=1)))

    LD2_ALL = sem_ld2.n

    @block.tensor
    def _(te):
        te.wait_ge(sem_ld2.h, LD2_ALL)
        sem_pe.einc(te.transpose(psA[:, 0:2], nu2[:], c["k_id_f"][0:2, 0:2]))
        sem_pe.einc(te.transpose(psA[:, 8:10], th2[:], c["k_id_f"][0:2, 0:2]))
        sem_pe.einc(te.transpose(psA[:, 16:80], b1n[:], c["k_id_f"][0:64, 0:64]))
        sem_pe.einc(te.transpose(psA[:, 96:100], kvn4[:], c["k_id_f"][0:4, 0:4]))

    @block.vector
    def _(v):
        v.wait_ge(sem_pe.h, pe0 + 4)
        v.tensor_copy(c["b1_sb"][:], psA[:, 16:80])
        v.tensor_copy(c["kvnw_sb"][:], psA[:, 96:100])
        v.tensor_copy(c["b2row"][:], b2f[:])
        v.tensor_copy(c["bbrow"][:], bbf_[:])
        v.tensor_copy(c["rbrow"][:], rbf_[:])
        v.tensor_copy(tmp[:], psA[:, 0:2])
        sem_dve.einc(v.tensor_copy(tmp3[:], psA[:, 8:10]))

    sem_ss = c["sem_sself"]

    def ssw(sc, inst):
        n = sem_ss.einc(inst)
        sc.wait_ge(sem_ss.h, n)

    @block.scalar
    def _(sc):
        sc.wait_ge(sem_dve.h, dv0 + 1)
        ssw(sc, sc.activation(tmp2[:], tmp[:], AF.Abs))
        ssw(sc, sc.activation(tmp2[:], tmp2[:], AF.Exp, scale=np.float32(-1.0)))
        sem_act.einc(sc.activation(tmp2[:], tmp2[:], AF.Ln, bias=np.float32(1.0)))

    @block.vector
    def _(v):
        v.wait_ge(sem_act.h, a0 + 1)
        n_ = c["sem_vself"].einc(v.tensor_scalar(tmp[:], tmp[:], 0.0, None,
                                                 OP.max, OP.bypass))
        v.wait_ge(c["sem_vself"].h, n_)
        sem_dve.einc(v.tensor_tensor(tmp[:], tmp[:], tmp2[:], OP.add))

    @block.scalar
    def _(sc):
        sc.wait_ge(sem_dve.h, dv0 + 2)
        ssw(sc, sc.activation(tmp[:], tmp[:], AF.Exp))
        ssw(sc, sc.activation(c["v_r"][:], tmp[:], AF.Exp, scale=np.float32(-1.0)))
        ssw(sc, sc.activation(c["v_th"][:], tmp3[:], AF.Exp))
        sem_act.einc(sc.activation(tmp2[:], c["v_r"][:], AF.Square))

    @block.vector
    def _(v):
        v.wait_ge(sem_act.h, a0 + 2)
        n_ = c["sem_vself"].einc(v.tensor_scalar(tmp2[:], tmp2[:], -1.0, 1.0,
                                                 OP.mult, OP.add))
        v.wait_ge(c["sem_vself"].h, n_)
        sem_dve.einc(v.tensor_scalar(tmp2[:], tmp2[:], 1e-30, None, OP.max, OP.bypass))

    @block.scalar
    def _(sc):
        sc.wait_ge(sem_dve.h, dv0 + 3)
        sem_act.einc(sc.activation(c["v_gam"][:], tmp2[:], AF.Sqrt))


def _ln_sweep(c, block, src, dst, st_m, st_r):
    arena = c["arena"]
    sem_dve, sem_dve2 = c["sem_dve"], c["sem_dve2"]
    sem_act, sem_pool = c["sem_act"], c["sem_pool"]
    sem_vself = c["sem_vself"]
    ldP = PSem(c, "sem_ld", "sem_ldB")
    stP = PSem(c, "sem_st", "sem_stB")
    xb = [arena.alloc(128, D) for _ in range(2)]
    xnb = [arena.alloc(128, D, BF16) for _ in range(2)]
    sqscr = [arena.alloc(128, D) for _ in range(2)]
    sums = arena.alloc(128, 1)
    sqs = arena.alloc(128, 1)
    mm_ = arena.alloc(128, 1)
    vtmp = arena.alloc(128, NSLICE)
    dv0, dv20, a0, p0 = sem_dve.n, sem_dve2.n, sem_act.n, sem_pool.n
    ldP.pre(range(NSLICE))
    stP.pre(range(NSLICE))

    @block.sync
    def _(sync):
        for i in range(NSLICE + 1):
            if i < NSLICE:
                if i >= 2:
                    sync.wait_ge(sem_dve2.h, dv20 + (i - 1))
                    sync.wait_ge(sem_act.h, a0 + (i - 1))
                ldP.issue(i, sync.dma_start(xb[i % 2][:],
                                            src.ap()[i * TS:(i + 1) * TS, :]))
            if i >= 1:
                sync.wait_ge(sem_dve2.h, dv20 + i)
                stP.issue(i - 1, sync.dma_start(dst.ap()[(i - 1) * TS:i * TS, :],
                                                xnb[(i - 1) % 2][:]))
        sync.wait_ge(stP.s[0].h, stP.s[0].n)
        sync.wait_ge(stP.s[1].h, stP.s[1].n)

    @block.scalar
    def _(sc):
        for i in range(NSLICE):
            ldP.wait(sc, i)
            if i >= 2:
                sc.wait_ge(sem_dve.h, dv0 + (i - 1))
            sem_act.einc(sc.activation(sqscr[i % 2][:], xb[i % 2][:], AF.Square))
            if i >= 1:
                sc.wait_ge(sem_dve.h, dv0 + i)
                n_ = c["sem_sself"].einc(sc.activation(vtmp[:, i - 1:i],
                                                       vtmp[:, i - 1:i], AF.Ln))
                sc.wait_ge(c["sem_sself"].h, n_)
                sem_pool.einc(sc.activation(st_r[:, i - 1:i], vtmp[:, i - 1:i],
                                            AF.Exp, scale=np.float32(-0.5)))
        sc.wait_ge(sem_dve.h, dv0 + NSLICE)
        n_ = c["sem_sself"].einc(sc.activation(vtmp[:, NSLICE - 1:NSLICE],
                                               vtmp[:, NSLICE - 1:NSLICE], AF.Ln))
        sc.wait_ge(c["sem_sself"].h, n_)
        sem_pool.einc(sc.activation(st_r[:, NSLICE - 1:NSLICE],
                                    vtmp[:, NSLICE - 1:NSLICE],
                                    AF.Exp, scale=np.float32(-0.5)))

    def vsw(v, inst):
        n = sem_vself.einc(inst)
        v.wait_ge(sem_vself.h, n)

    def xn_emit(v, i):
        v.wait_ge(sem_pool.h, p0 + i + 1)
        if i >= 2:
            stP.wait(v, i - 2)
        sem_dve2.einc(v.tensor_scalar(xnb[i % 2][:], xb[i % 2][:],
                                      st_m[:, i:i + 1], st_r[:, i:i + 1],
                                      OP.subtract, OP.mult))

    @block.vector
    def _(v):
        for i in range(NSLICE):
            ldP.wait(v, i)
            vsw(v, v.tensor_reduce(sums[:], xb[i % 2][:], AX.X, OP.add))
            if i >= 1:
                xn_emit(v, i - 1)
            v.wait_ge(sem_act.h, a0 + i + 1)
            vsw(v, v.tensor_reduce(sqs[:], sqscr[i % 2][:], AX.X, OP.add))
            vsw(v, v.tensor_scalar(st_m[:, i:i + 1], sums[:], 1.0 / D, None,
                                   OP.mult, OP.bypass))
            vsw(v, v.tensor_tensor(mm_[:], st_m[:, i:i + 1], st_m[:, i:i + 1], OP.mult))
            vsw(v, v.tensor_scalar(vtmp[:, i:i + 1], sqs[:], 1.0 / D, None,
                                   OP.mult, OP.bypass))
            sem_dve.einc(v.tensor_scalar(vtmp[:, i:i + 1],
                                         vtmp[:, i:i + 1], mm_[:], -float(EPS_LN),
                                         OP.subtract, OP.subtract))
        xn_emit(v, NSLICE - 1)


def ph_p2t(c, block):
    arena = c["arena"]
    ldX = PSem(c, "sem_p1a", "sem_p1b")
    stX = PSem(c, "sem_p2a", "sem_p2b")
    tb = [arena.alloc(128, 2048, BF16) for _ in range(2)]
    ldX.pre(range(64))
    stX.pre(range(64))

    @block.sync
    def _(sync):
        for hc in range(64):
            if hc >= 2:
                stX.wait(sync, hc - 2)
            ldX.issue(hc, sync.dma_start(tb[hc % 2][:],
                                         c["s_p2bf"].ap()[:, hc * 128:(hc + 1) * 128],
                                         transpose=True))

    @block.scalar
    def _(sc):
        for hc in range(64):
            ldX.wait(sc, hc)
            stX.issue(hc, sc.dma_start(
                c["s_p2t"].ap()[hc * 128:(hc + 1) * 128, :], tb[hc % 2][:]))
        sc.wait_ge(stX.s[0].h, stX.s[0].n)
        sc.wait_ge(stX.s[1].h, stX.s[1].n)


def ph_ln0(c, block):
    _ln_sweep(c, block, c["x"], c["s_xn0"], c["st_m0"], c["st_r0"])


def ph_bu(c, block):
    arena = c["arena"]
    sem_ld2, sem_pe, sem_dve = c["sem_ld2"], c["sem_pe"], c["sem_dve"]
    ldP = PSem(c, "sem_ld", "sem_ldB")
    psq = [c["psA"], c["psB"]]
    btre = arena.alloc(128, 16 * NST, BF16)
    btim = arena.alloc(128, 16 * NST, BF16)
    xf = [arena.alloc(128, 16 * 512, BF16) for _ in range(2)]
    pe0, dv0 = sem_pe.n, sem_dve.n
    cnt = [ldP.s[0].n, ldP.s[1].n]
    for s in range(NSUP):
        cnt[s % 2] += 16 * 16
        ldP.tgt[s] = (s % 2, cnt[s % 2])

    @block.sync
    def _(sync):
        for ci in range(16):
            sem_ld2.dinc(sync.dma_start(btre[:, ci * NST:(ci + 1) * NST],
                                        c["s_brebf"].ap()[:, ci * 128:(ci + 1) * 128],
                                        transpose=True))
            sem_ld2.dinc(sync.dma_start(btim[:, ci * NST:(ci + 1) * NST],
                                        c["s_bimbf"].ap()[:, ci * 128:(ci + 1) * 128],
                                        transpose=True))
        for s in range(NSUP):
            if s >= 2:
                sync.wait_ge(sem_pe.h, pe0 + 4 * (s - 1))
            for ci in range(16):
                ldP.issue(s, sync.dma_start(
                    xf[s % 2][:, ci * 512:(ci + 1) * 512],
                    c["s_xn0"].ap()[s * SUP:(s + 1) * SUP, ci * 128:(ci + 1) * 128],
                    transpose=True))

    LD2_ALL = sem_ld2.n

    @block.tensor
    def _(te):
        te.wait_ge(sem_ld2.h, LD2_ALL)
        for s in range(NSUP):
            ldP.wait(te, s)
            if s >= 2:
                te.wait_ge(sem_dve.h, dv0 + (s - 1))
            ps = psq[s % 2]
            for q, (nch, bt) in enumerate([(0, btre), (0, btim), (1, btre), (1, btim)]):
                for ci in range(16):
                    inst = te.matmul(ps[:, q * 512:(q + 1) * 512],
                                     bt[:, ci * NST + nch * 128: ci * NST + nch * 128 + 128],
                                     xf[s % 2][:, ci * 512:(ci + 1) * 512],
                                     start=(ci == 0), stop=(ci == 15))
                sem_pe.einc(inst)

    @block.vector
    def _(v):
        slab = c["slab"]
        for s in range(NSUP):
            v.wait_ge(sem_pe.h, pe0 + 4 * (s + 1))
            ps = psq[s % 2]
            for q, (nch, comp) in enumerate([(0, 0), (0, 1), (1, 0), (1, 1)]):
                k = 2 * nch + comp
                inst = v.tensor_scalar(slab[k][:, s * SUP:(s + 1) * SUP],
                                       ps[:, q * 512:(q + 1) * 512],
                                       c["v_gam"][:, nch:nch + 1], None,
                                       OP.mult, OP.bypass)
            sem_dve.einc(inst)


def ph_scan(c, block):
    arena = c["arena"]
    sem_ld2, sem_dve, sem_act = c["sem_ld2"], c["sem_dve"], c["sem_act"]
    trow = arena.alloc(128, T)
    COS = arena.alloc(128, T)
    SIN = arena.alloc(128, T)
    PH = arena.alloc(128, T)
    PHR = arena.alloc(128, T)
    RT = arena.alloc(128, T)
    CRE = arena.alloc(128, T)
    CIM = arena.alloc(128, T)
    TMP = arena.alloc(128, T)
    dv0, a0 = sem_dve.n, sem_act.n
    slab, Hsb = c["slab"], c["Hsb"]

    @block.sync
    def _(sync):
        sem_ld2.dinc(sync.dma_start(trow[:], c["TROW"].ap()))
    LD2_ALL = sem_ld2.n

    def table(v, V, base_shift):
        V(v.tensor_scalar(PHR[:], PH[:], float(base_shift), None, OP.add, OP.bypass))
        V(v.tensor_scalar(TMP[:], PHR[:], 1.0 / TWO_PI, MAGIC, OP.mult, OP.add))
        V(v.tensor_scalar(TMP[:], TMP[:], MAGIC, None, OP.subtract, OP.bypass))
        V(v.scalar_tensor_tensor(PHR[:], TMP[:], -TWO_PI, PHR[:], OP.mult, OP.add))
        V(v.tensor_scalar(PHR[:], PHR[:], float(math.pi), None, OP.min, OP.bypass))
        return v.tensor_scalar(PHR[:], PHR[:], -float(math.pi), None, OP.max, OP.bypass)

    for nch in range(2):
        @block.vector
        def _(v, nch=nch):
            V = mk_chain(v, c["sem_vself"])
            if nch == 0:
                v.wait_ge(sem_ld2.h, LD2_ALL)
            else:
                v.wait_ge(sem_act.h, a0 + 2)
            V(v.tensor_scalar(PH[:], trow[:], c["v_th"][:, nch:nch + 1], None,
                              OP.mult, OP.bypass))
            sem_dve.einc(table(v, V, 0.0))
            v.wait_ge(sem_act.h, a0 + 2 * nch + 1)
            sem_dve.einc(table(v, V, math.pi / 2.0))
            V(v.tensor_scalar(RT[:], trow[:], 0.0, c["v_r"][:, nch:nch + 1],
                              OP.mult, OP.add))
            v.wait_ge(sem_act.h, a0 + 2 * nch + 2)
            for seq in range(SEQS):
                cols = slice(seq * T, (seq + 1) * T)
                bu_re = slab[2 * nch + 0][:, cols]
                bu_im = slab[2 * nch + 1][:, cols]
                V(v.tensor_tensor(CRE[:], COS[:], bu_re, OP.mult))
                V(v.tensor_tensor(TMP[:], SIN[:], bu_im, OP.mult))
                V(v.tensor_tensor(CRE[:], CRE[:], TMP[:], OP.add))
                V(v.tensor_tensor(CIM[:], COS[:], bu_im, OP.mult))
                V(v.tensor_tensor(TMP[:], SIN[:], bu_re, OP.mult))
                V(v.tensor_tensor(CIM[:], CIM[:], TMP[:], OP.subtract))
                V(v.tensor_tensor_scan(bu_re, RT[:], CRE[:], 0.0, OP.mult, OP.add))
                V(v.tensor_tensor_scan(bu_im, RT[:], CIM[:], 0.0, OP.mult, OP.add))
                g_re, g_im = bu_re, bu_im
                V(v.tensor_tensor(CRE[:], COS[:], g_re, OP.mult))
                V(v.tensor_tensor(TMP[:], SIN[:], g_im, OP.mult))
                V(v.tensor_tensor(Hsb[2 * nch + 0][:, cols], CRE[:], TMP[:], OP.subtract))
                V(v.tensor_tensor(CIM[:], COS[:], g_im, OP.mult))
                V(v.tensor_tensor(TMP[:], SIN[:], g_re, OP.mult))
                V(v.scalar_tensor_tensor(Hsb[2 * nch + 1][:, cols], CIM[:], -1.0,
                                         TMP[:], OP.mult, OP.subtract))

        @block.scalar
        def _(sc, nch=nch):
            sc.wait_ge(sem_dve.h, dv0 + 2 * nch + 1)
            sem_act.einc(sc.activation(SIN[:], PHR[:], AF.Sin))
            sc.wait_ge(sem_dve.h, dv0 + 2 * nch + 2)
            sem_act.einc(sc.activation(COS[:], PHR[:], AF.Sin))


def ph_ffn0(c, block):
    arena = c["arena"]
    sem_pe, sem_act = c["sem_pe"], c["sem_act"]
    ldP = PSem(c, "sem_ld", "sem_ldB")
    p1P = PSem(c, "sem_p1a", "sem_p1b")
    p2P = PSem(c, "sem_p2a", "sem_p2b")
    p3P = PSem(c, "sem_p3a", "sem_p3b")
    stP = PSem(c, "sem_st", "sem_stB")
    slab = c["slab"]
    hsl = [t[:].bitcast(BF16) for t in slab]

    def hch(hc):
        return hsl[hc // 16][:, (hc % 16) * 512:(hc % 16 + 1) * 512]

    xnb = [arena.alloc(128, 16 * 512, BF16) for _ in range(2)]
    p1b = [arena.alloc(128, 16 * 512, BF16) for _ in range(2)]
    p2b = [arena.alloc(128, 2048, BF16) for _ in range(4)]
    gb = [arena.alloc(128, 2048)]
    psA, psB = c["psA"], c["psB"]
    p0, a0 = sem_pe.n, sem_act.n
    PE_S = 194
    ACT_S = 68
    ldP.pre([s for s in range(NSUP) for _ in range(16)])
    p1P.pre([g for g in range(NSUP * 16) for _ in range(16)])
    p2P.pre([g for g in range(NSUP * 2 * 64) if (g // 2) % 2 == 0])
    p3P.pre([g for g in range(NSUP * 2 * 64) if (g // 2) % 2 == 1])
    stP.pre(range(NSUP * 4))

    def pe_t(s, k):
        return p0 + s * PE_S + k

    def act_t(s, k):
        return a0 + s * ACT_S + k

    @block.sync
    def _(sync):
        for s in range(NSUP):
            if s >= 2:
                sync.wait_ge(sem_pe.h, pe_t(s - 2, 64))
            for dc in range(16):
                ldP.issue(s, sync.dma_start(
                    xnb[s % 2][:, dc * 512:(dc + 1) * 512],
                    c["s_xn0"].ap()[s * SUP:(s + 1) * SUP, dc * 128:(dc + 1) * 128],
                    transpose=True))
            for hg in range(16):
                g = s * 16 + hg
                if g >= 2:
                    sync.wait_ge(sem_pe.h, pe_t((g - 2) // 16, 4 * ((g - 2) % 16) + 4))
                for dc in range(16):
                    p1P.issue(g, sync.dma_start(
                        p1b[g % 2][:, dc * 512:(dc + 1) * 512],
                        c["s_p1bf"].ap()[hg * 512:(hg + 1) * 512, dc * 128:(dc + 1) * 128],
                        transpose=True))
            for pair in range(2):
                for hc in range(64):
                    g2 = (s * 2 + pair) * 64 + hc
                    if g2 >= 4:
                        gp = g2 - 4
                        s_, r_ = gp // 128, gp % 128
                        pair_, hc_ = r_ // 64, r_ % 64
                        sync.wait_ge(sem_pe.h,
                                     pe_t(s_, 64 + pair_ * 65 + hc_ + 1))
                    (p2P if (g2 // 2) % 2 == 0 else p3P).issue(g2, sync.dma_start(
                        p2b[g2 % 4][:],
                        c["s_p2t"].ap()[hc * 128:(hc + 1) * 128, :]))
                for jj in range(2):
                    j = pair * 2 + jj
                    sj = s * 4 + j
                    sync.wait_ge(sem_act.h, act_t(s, 64 + j + 1))
                    stP.issue(sj, sync.dma_start(
                        c["s_g0"].ap()[s * SUP + j * TS: s * SUP + (j + 1) * TS, :],
                        gb[0][:]))
        sync.wait_ge(stP.s[0].h, stP.s[0].n)
        sync.wait_ge(stP.s[1].h, stP.s[1].n)

    @block.tensor
    def _(te):
        for s in range(NSUP):
            ldP.wait(te, s)
            if s >= 1:
                te.wait_ge(sem_act.h, act_t(s - 1, 68))
            for hg in range(16):
                g = s * 16 + hg
                p1P.wait(te, g)
                for k in range(4):
                    hc = hg * 4 + k
                    if hc >= 2:
                        te.wait_ge(sem_act.h, act_t(s, hc - 1))
                    for dc in range(16):
                        inst = te.matmul(psA[:, (hc % 2) * 512:(hc % 2 + 1) * 512],
                                         p1b[g % 2][:, dc * 512 + k * 128: dc * 512 + k * 128 + 128],
                                         xnb[s % 2][:, dc * 512:(dc + 1) * 512],
                                         start=(dc == 0), stop=(dc == 15))
                    sem_pe.einc(inst)
            te.wait_ge(sem_act.h, act_t(s, 64))
            for pair in range(2):
                ps_j = [psB, psA]
                if pair >= 1:
                    te.wait_ge(sem_act.h, act_t(s, 64 + 2 * pair))
                for hc in range(64):
                    g2 = (s * 2 + pair) * 64 + hc
                    (p2P if (g2 // 2) % 2 == 0 else p3P).wait(te, g2)
                    for jj in range(2):
                        j = pair * 2 + jj
                        for bs in range(4):
                            inst = te.matmul(
                                ps_j[jj][:, bs * 512:(bs + 1) * 512],
                                hch(hc)[:, j * 128:(j + 1) * 128],
                                p2b[g2 % 4][:, bs * 512:(bs + 1) * 512],
                                start=(hc == 0), stop=False)
                    sem_pe.einc(inst)
                for jj in range(2):
                    for bs in range(4):
                        inst = te.matmul(ps_j[jj][:, bs * 512:(bs + 1) * 512],
                                         c["k_ones_bf"][0:1, 0:128],
                                         c["b2row"][0:1, bs * 512:(bs + 1) * 512],
                                         start=False, stop=True)
                sem_pe.einc(inst)


    @block.scalar
    def _(sc):
        for s in range(NSUP):
            for hc in range(64):
                sc.wait_ge(sem_pe.h, pe_t(s, hc + 1))
                sem_act.einc(sc.activation(hch(hc), psA[:, (hc % 2) * 512:(hc % 2 + 1) * 512],
                                           AF.Gelu, bias=c["b1_sb"][:, hc:hc + 1]))
            for pair in range(2):
                sc.wait_ge(sem_pe.h, pe_t(s, 64 + (pair + 1) * 65))
                for jj in range(2):
                    j = pair * 2 + jj
                    sj = s * 4 + j
                    if sj >= 1:
                        stP.wait(sc, sj - 1)
                    sem_act.einc(sc.activation(gb[0][:],
                                               [psB, psA][jj][:],
                                               AF.Silu))


def ph_cm(c, block):
    arena = c["arena"]
    sem_ld2, sem_pe, sem_dve = c["sem_ld2"], c["sem_pe"], c["sem_dve"]
    sem_vself = c["sem_vself"]
    ldP = PSem(c, "sem_ld", "sem_ldB")
    stP = PSem(c, "sem_st", "sem_stB")
    Hsb, psA, psB = c["Hsb"], c["psA"], c["psB"]
    ct = arena.alloc(128, 4 * D, BF16)
    xb = [arena.alloc(128, D) for _ in range(2)]
    gbuf = [arena.alloc(128, D) for _ in range(2)]
    x1b = [arena.alloc(128, D) for _ in range(2)]
    XN = arena.alloc(128, D)
    T1 = arena.alloc(128, D)
    dbc = arena.alloc(128, D)
    pe0, dv0 = sem_pe.n, sem_dve.n
    ldP.pre([i for i in range(NSLICE) for _ in range(2)])
    stP.pre(range(NSLICE))

    @block.sync
    def _(sync):
        for nch in range(2):
            for comp, srcn in enumerate(["s_crebf", "s_cimbf"]):
                k = 2 * nch + comp
                sem_ld2.dinc(sync.dma_start(
                    ct[:, k * D:(k + 1) * D],
                    c[srcn].ap()[:, nch * 128:(nch + 1) * 128],
                    transpose=True))
        for i in range(NSLICE):
            if i >= 2:
                sync.wait_ge(sem_dve.h, dv0 + i)
            ldP.issue(i, sync.dma_start(xb[i % 2][:], c["x"].ap()[i * TS:(i + 1) * TS, :]))
            ldP.issue(i, sync.dma_start(gbuf[i % 2][:], c["s_g0"].ap()[i * TS:(i + 1) * TS, :]))
            if i >= 1:
                sync.wait_ge(sem_dve.h, dv0 + i + 1)
                stP.issue(i - 1, sync.dma_start(
                    c["s_x1"].ap()[(i - 1) * TS:i * TS, :], x1b[(i - 1) % 2][:]))
        sync.wait_ge(sem_dve.h, dv0 + NSLICE + 1)
        stP.issue(NSLICE - 1, sync.dma_start(
            c["s_x1"].ap()[(NSLICE - 1) * TS:NSLICE * TS, :], x1b[(NSLICE - 1) % 2][:]))
        sync.wait_ge(stP.s[0].h, stP.s[0].n)
        sync.wait_ge(stP.s[1].h, stP.s[1].n)

    LD2_ALL = sem_ld2.n

    @block.tensor
    def _(te):
        te.wait_ge(sem_ld2.h, LD2_ALL)
        for bs in range(4):
            inst = te.matmul(psA[:, bs * 512:(bs + 1) * 512],
                             c["k_ones_f"][0:1, :],
                             c["drow"][0:1, bs * 512:(bs + 1) * 512],
                             start=True, stop=True)
        sem_pe.einc(inst)
        for i in range(NSLICE):
            if i == 0:
                te.wait_ge(sem_dve.h, dv0 + 1)
            if i >= 2:
                te.wait_ge(sem_dve.h, dv0 + i)
            ps = [psA, psB][i % 2]
            for k in range(4):
                for bs in range(4):
                    inst = te.matmul(ps[:, bs * 512:(bs + 1) * 512],
                                     Hsb[k][:, i * TS:(i + 1) * TS],
                                     ct[:, k * D + bs * 512: k * D + (bs + 1) * 512],
                                     start=(k == 0), stop=(k == 3))
            sem_pe.einc(inst)

    @block.vector
    def _(v):
        V = mk_chain(v, sem_vself)
        v.wait_ge(sem_pe.h, pe0 + 1)
        sem_dve.einc(v.tensor_copy(dbc[:], psA[:]))
        for i in range(NSLICE):
            ldP.wait(v, i)
            v.wait_ge(sem_pe.h, pe0 + i + 2)
            if i >= 2:
                stP.wait(v, i - 2)
            ps = [psA, psB][i % 2]
            V(v.tensor_scalar(XN[:], xb[i % 2][:], c["st_m0"][:, i:i + 1],
                              c["st_r0"][:, i:i + 1], OP.subtract, OP.mult))
            V(v.tensor_tensor(T1[:], XN[:], dbc[:], OP.mult))
            V(v.tensor_tensor(T1[:], T1[:], ps[:], OP.add))
            V(v.tensor_tensor(T1[:], T1[:], gbuf[i % 2][:], OP.mult))
            sem_dve.einc(v.tensor_tensor(x1b[i % 2][:], T1[:], xb[i % 2][:], OP.add))


def ph_ln1(c, block):
    _ln_sweep(c, block, c["s_x1"], c["s_xn1"], c["st_m1"], c["st_r1"])


def ph_wkvbfold(c, block):
    arena = c["arena"]
    sem_dve = c["sem_dve"]
    ldP = PSem(c, "sem_ld", "sem_ldB")
    stP = PSem(c, "sem_st", "sem_stB")
    wb = [arena.alloc(128, NH * (DNOPE + DV)) for _ in range(2)]
    ob = [arena.alloc(128, NH * (DNOPE + DV), BF16) for _ in range(2)]
    dv0 = sem_dve.n
    ldP.pre(range(4))
    stP.pre(range(4))

    @block.sync
    def _(sync):
        for lc in range(4):
            if lc >= 2:
                sync.wait_ge(sem_dve.h, dv0 + lc - 1)
            ldP.issue(lc, sync.dma_start(wb[lc % 2][:],
                                         c["Wkv_b"].ap()[lc * 128:(lc + 1) * 128, :]))
            if lc >= 1:
                sync.wait_ge(sem_dve.h, dv0 + lc)
                stP.issue(lc - 1, sync.dma_start(
                    c["s_wkvbbf"].ap()[(lc - 1) * 128:lc * 128, :], ob[(lc - 1) % 2][:]))
        sync.wait_ge(sem_dve.h, dv0 + 4)
        stP.issue(3, sync.dma_start(c["s_wkvbbf"].ap()[3 * 128:4 * 128, :], ob[1][:]))
        sync.wait_ge(stP.s[0].h, stP.s[0].n)
        sync.wait_ge(stP.s[1].h, stP.s[1].n)

    @block.vector
    def _(v):
        for lc in range(4):
            ldP.wait(v, lc)
            if lc >= 2:
                stP.wait(v, lc - 2)
            sem_dve.einc(v.tensor_scalar(ob[lc % 2][:], wb[lc % 2][:],
                                         c["kvnw_sb"][:, lc:lc + 1], None,
                                         OP.mult, OP.bypass))


def ph_q(c, block):
    arena = c["arena"]
    sem_pe, sem_act, sem_dve2 = c["sem_pe"], c["sem_act"], c["sem_dve2"]
    sem_vself = c["sem_vself"]
    ldP = PSem(c, "sem_ld", "sem_ldB")
    wqP = PSem(c, "sem_p1a", "sem_p1b")
    tbP = PSem(c, "sem_p2a", "sem_p2b")
    stP = PSem(c, "sem_st", "sem_stB")
    st2P = PSem(c, "sem_st2a", "sem_st2b")
    psA, psB = c["psA"], c["psB"]
    xf = [arena.alloc(128, 16 * 512, BF16) for _ in range(2)]
    wqn = [arena.alloc(128, 16 * 128, BF16) for _ in range(2)]
    hb16 = [t[:].bitcast(BF16) for t in c["Hsb"]]

    def wqr_dc(dc):
        return hb16[dc // 4][:, (dc % 4) * 1024:(dc % 4 + 1) * 1024]
    qnb = [arena.alloc(128, 512, BF16) for _ in range(2)]
    qrb = [arena.alloc(128, 2048, BF16) for _ in range(2)]
    tabc = [arena.alloc(128, 512) for _ in range(2)]
    tabs = [arena.alloc(128, 512) for _ in range(2)]
    rt1 = arena.alloc(128, 512)
    rt2 = arena.alloc(128, 512)
    pe0, a0, dv20 = sem_pe.n, sem_act.n, sem_dve2.n
    ldP.pre([s for s in range(NSUP) for _ in range(16)])
    wqP.pre([g for g in range(NSUP * 16) for _ in range(2)])
    tbP.pre([i for i in range(NSLICE) for _ in range(2)])
    stP.pre(range(NSUP * 16))
    st2P.pre(range(NSLICE))
    PE_S = 20
    ACT_S = 16

    @block.sync
    def _(sync):
        for dc in range(16):
            for h in range(16):
                sem_ld2 = c["sem_ld2"]
                sem_ld2.dinc(sync.dma_start(
                    wqr_dc(dc)[:, h * 64:(h + 1) * 64],
                    c["s_wqbf"].ap()[dc * 128:(dc + 1) * 128,
                                     h * DQK + DNOPE: h * DQK + DQK]))
        for s in range(NSUP):
            if s >= 2:
                sync.wait_ge(sem_pe.h, pe0 + PE_S * (s - 1))
            for dc in range(16):
                ldP.issue(s, sync.dma_start(
                    xf[s % 2][:, dc * 512:(dc + 1) * 512],
                    c["s_xn1"].ap()[s * SUP:(s + 1) * SUP, dc * 128:(dc + 1) * 128],
                    transpose=True))
            def qn_store(h):
                k = s * 16 + h
                sync.wait_ge(sem_act.h, a0 + ACT_S * s + h + 1)
                stP.issue(k, sync.dma_start(
                    c["s_qnope"].ap()[h * 128:(h + 1) * 128, s * SUP:(s + 1) * SUP],
                    qnb[h % 2][:]))

            for h in range(16):
                g = s * 16 + h
                if h >= 2:
                    qn_store(h - 2)
                if g >= 2:
                    sync.wait_ge(sem_pe.h, pe0 + PE_S * ((g - 2) // 16) + ((g - 2) % 16) + 1)
                for half2 in range(2):
                    wqP.issue(g, sync.dma_start(
                        wqn[g % 2][:].rearrange("p (dc z) -> p dc z", z=128)[
                            :, half2 * 8:(half2 + 1) * 8, :],
                        c["s_wqbf"].ap()[half2 * 1024:(half2 + 1) * 1024,
                                         h * DQK: h * DQK + DNOPE].rearrange(
                            "(dc p) z -> p dc z", p=128)))
            qn_store(14)
            qn_store(15)
            for j in range(4):
                i = s * 4 + j
                if i >= 2:
                    sync.wait_ge(sem_dve2.h, dv20 + (i - 1))
                pos0 = (i % 16) * 128
                tbP.issue(i, sync.dma_start(tabc[i % 2][:],
                                            c["QROPEC"].ap()[pos0:pos0 + 128, :]))
                tbP.issue(i, sync.dma_start(tabs[i % 2][:],
                                            c["QROPES"].ap()[pos0:pos0 + 128, :]))
            for j in range(4):
                i = s * 4 + j
                sync.wait_ge(sem_dve2.h, dv20 + i + 1)
                st2P.issue(i, sync.dma_start(
                    c["s_qrope"].ap()[i * TS:(i + 1) * TS, :], qrb[j % 2][:]))
        sync.wait_ge(stP.s[0].h, stP.s[0].n)
        sync.wait_ge(stP.s[1].h, stP.s[1].n)
        sync.wait_ge(st2P.s[0].h, st2P.s[0].n)
        sync.wait_ge(st2P.s[1].h, st2P.s[1].n)

    LD2_ALL = c["sem_ld2"].n

    @block.tensor
    def _(te):
        te.wait_ge(c["sem_ld2"].h, LD2_ALL)
        for s in range(NSUP):
            ldP.wait(te, s)
            for h in range(16):
                g = s * 16 + h
                wqP.wait(te, g)
                reg = [psA[:, 0:512], psA[:, 512:1024]][h % 2]
                if g >= 2:
                    te.wait_ge(sem_act.h, a0 + ACT_S * ((g - 2) // 16) + ((g - 2) % 16) + 1)
                for dc in range(16):
                    inst = te.matmul(reg,
                                     wqn[g % 2][:, dc * 128:(dc + 1) * 128],
                                     xf[s % 2][:, dc * 512:(dc + 1) * 512],
                                     start=(dc == 0), stop=(dc == 15))
                sem_pe.einc(inst)
            for j in range(4):
                i = s * 4 + j
                reg = [psB[:, 0:1024], psB[:, 1024:2048]][j % 2]
                if i >= 2:
                    te.wait_ge(sem_dve2.h, dv20 + (i - 1))
                for half in range(2):
                    for dc in range(16):
                        inst = te.matmul(
                            reg[:, half * 512:(half + 1) * 512],
                            xf[s % 2][:, dc * 512 + j * 128: dc * 512 + j * 128 + 128],
                            wqr_dc(dc)[:, half * 512:(half + 1) * 512],
                            start=(dc == 0), stop=(dc == 15))
                sem_pe.einc(inst)

    @block.scalar
    def _(sc):
        for s in range(NSUP):
            for h in range(16):
                g = s * 16 + h
                k = s * 16 + h
                sc.wait_ge(sem_pe.h, pe0 + PE_S * s + h + 1)
                if g >= 2:
                    kp = k - 2 if h >= 2 else (s - 1) * 16 + 14 + h
                    stP.wait(sc, kp)
                reg = [psA[:, 0:512], psA[:, 512:1024]][h % 2]
                sem_act.einc(sc.activation(qnb[h % 2][:], reg, AF.Copy,
                                           bias=0.0, scale=np.float32(SCALE_QK)))

    @block.vector
    def _(v):
        V = mk_chain(v, sem_vself)
        V(v.memset(qrb[0][:], 0.0))
        V(v.memset(qrb[1][:], 0.0))
        for s in range(NSUP):
            for j in range(4):
                i = s * 4 + j
                v.wait_ge(sem_pe.h, pe0 + PE_S * s + 16 + j + 1)
                tbP.wait(v, i)
                if i >= 2:
                    st2P.wait(v, i - 2)
                reg = [psB[:, 0:1024], psB[:, 1024:2048]][j % 2]
                rr = reg.rearrange("p (h q) -> p h q", q=64)
                x1 = rr.rearrange("p h (j t) -> p h j t", t=2)[:, :, :, 0:1].squeeze()
                x2 = rr.rearrange("p h (j t) -> p h j t", t=2)[:, :, :, 1:2].squeeze()
                ctab = tabc[i % 2][:].rearrange("p (h j) -> p h j", j=32)
                stab = tabs[i % 2][:].rearrange("p (h j) -> p h j", j=32)
                r1v = rt1[:].rearrange("p (h j) -> p h j", j=32)
                r2v = rt2[:].rearrange("p (h j) -> p h j", j=32)
                ov = qrb[j % 2][:].rearrange("p (h q) -> p h q", q=128)[:, :, 0:64]
                o1 = ov.rearrange("p h (j t) -> p h j t", t=2)[:, :, :, 0:1].squeeze()
                o2 = ov.rearrange("p h (j t) -> p h j t", t=2)[:, :, :, 1:2].squeeze()
                V(v.tensor_tensor(r1v, x1, ctab, OP.mult))
                V(v.tensor_tensor(r2v, x2, stab, OP.mult))
                V(v.tensor_tensor(o1, r1v, r2v, OP.subtract))
                V(v.tensor_tensor(r1v, x1, stab, OP.mult))
                V(v.tensor_tensor(r2v, x2, ctab, OP.mult))
                n_ = sem_dve2.einc(v.tensor_tensor(o2, r1v, r2v, OP.add))
                v.wait_ge(sem_dve2.h, n_)


def ph_kv(c, block):
    arena = c["arena"]
    sem_pe, sem_act, sem_dve, sem_dve2 = (c["sem_pe"], c["sem_act"],
                                          c["sem_dve"], c["sem_dve2"])
    sem_vself, sem_sself = c["sem_vself"], c["sem_sself"]
    ldP = PSem(c, "sem_ld", "sem_ldB")
    tbP = PSem(c, "sem_p2a", "sem_p2b")
    stP = PSem(c, "sem_st", "sem_stB")
    st2P = PSem(c, "sem_st2a", "sem_st2b")
    sem_ld2 = c["sem_ld2"]
    psA, psB = c["psA"], c["psB"]
    xf = [arena.alloc(128, 16 * 512, BF16) for _ in range(2)]
    wkva = arena.alloc(128, 16 * (KVL + DROPE), BF16)
    ckb = [arena.alloc(128, KVL, BF16) for _ in range(2)]
    krb = [arena.alloc(128, 128, BF16) for _ in range(2)]
    tabc = [arena.alloc(128, 32) for _ in range(2)]
    tabs = [arena.alloc(128, 32) for _ in range(2)]
    scr = arena.alloc(128, KVL)
    ssq = arena.alloc(128, 1)
    rstd = arena.alloc(128, 1)
    rt1 = arena.alloc(128, 32)
    rt2 = arena.alloc(128, 32)
    pe0, a0, dv0, dv20 = sem_pe.n, sem_act.n, sem_dve.n, sem_dve2.n
    ldP.pre([s for s in range(NSUP) for _ in range(16)])
    tbP.pre([i for i in range(NSLICE) for _ in range(2)])
    stP.pre(range(NSLICE))
    st2P.pre(range(NSLICE))

    @block.gpsimd
    def _(gp):
        for dc in range(16):
            c["sem_gp"].dinc(gp.dma_start(
                wkva[:, dc * 576:(dc + 1) * 576],
                c["Wkv_a"].ap()[dc * 128:(dc + 1) * 128, :]))
    GP_ALL = c["sem_gp"].n

    @block.sync
    def _(sync):
        for s in range(NSUP):
            if s >= 2:
                sync.wait_ge(sem_pe.h, pe0 + 4 * (s - 1))
            for dc in range(16):
                ldP.issue(s, sync.dma_start(
                    xf[s % 2][:, dc * 512:(dc + 1) * 512],
                    c["s_xn1"].ap()[s * SUP:(s + 1) * SUP, dc * 128:(dc + 1) * 128],
                    transpose=True))
            for j in range(4):
                i = s * 4 + j
                if i >= 2:
                    sync.wait_ge(sem_dve2.h, dv20 + (i - 1))
                pos0 = (i % 16) * 128
                tbP.issue(i, sync.dma_start(tabc[i % 2][:],
                                            c["ROPEC"].ap()[pos0:pos0 + 128, 0:32]))
                tbP.issue(i, sync.dma_start(tabs[i % 2][:],
                                            c["ROPES"].ap()[pos0:pos0 + 128, 0:32]))
            for j in range(4):
                i = s * 4 + j
                sync.wait_ge(sem_dve.h, dv0 + 2 * (i + 1))
                stP.issue(i, sync.dma_start(
                    c["s_ckvn"].ap()[i * TS:(i + 1) * TS, :], ckb[j % 2][:]))
                sync.wait_ge(sem_dve2.h, dv20 + i + 1)
                st2P.issue(i, sync.dma_start(
                    c["s_krope"].ap()[i * TS:(i + 1) * TS, :], krb[j % 2][:]))
        sync.wait_ge(stP.s[0].h, stP.s[0].n)
        sync.wait_ge(stP.s[1].h, stP.s[1].n)
        sync.wait_ge(st2P.s[0].h, st2P.s[0].n)
        sync.wait_ge(st2P.s[1].h, st2P.s[1].n)

    @block.tensor
    def _(te):
        te.wait_ge(c["sem_gp"].h, GP_ALL)
        for s in range(NSUP):
            ldP.wait(te, s)
            for j in range(4):
                i = s * 4 + j
                ps = [psA, psB][j % 2]
                if i >= 2:
                    te.wait_ge(sem_dve.h, dv0 + 2 * (i - 1))
                    te.wait_ge(sem_dve2.h, dv20 + i - 1)
                for grp, (c0, cw) in enumerate([(0, 512), (512, 64)]):
                    for dc in range(16):
                        inst = te.matmul(
                            ps[:, c0:c0 + cw],
                            xf[s % 2][:, dc * 512 + j * 128: dc * 512 + j * 128 + 128],
                            wkva[:, dc * 576 + c0: dc * 576 + c0 + cw],
                            start=(dc == 0), stop=(dc == 15))
                sem_pe.einc(inst)

    @block.scalar
    def _(sc):
        SS = mk_chain(sc, sem_sself)
        for s in range(NSUP):
            for j in range(4):
                i = s * 4 + j
                ps = [psA, psB][j % 2]
                sc.wait_ge(sem_pe.h, pe0 + i + 1)
                if i >= 1:
                    sc.wait_ge(sem_dve.h, dv0 + 2 * (i - 1) + 1)
                sem_act.einc(sc.activation(scr[:], ps[:, 0:KVL], AF.Square))
                sc.wait_ge(sem_dve.h, dv0 + 2 * i + 1)
                SS(sc.activation(rstd[:], rstd[:], AF.Ln))
                sem_act.einc(sc.activation(rstd[:], rstd[:], AF.Exp,
                                           scale=np.float32(-0.5)))

    @block.vector
    def _(v):
        V = mk_chain(v, sem_vself)
        V(v.memset(krb[0][:], 0.0))
        V(v.memset(krb[1][:], 0.0))
        for s in range(NSUP):
            for j in range(4):
                i = s * 4 + j
                ps = [psA, psB][j % 2]
                v.wait_ge(sem_act.h, a0 + 2 * i + 1)
                V(v.tensor_reduce(ssq[:], scr[:], AX.X, OP.add))
                n_ = sem_dve.einc(v.tensor_scalar(rstd[:], ssq[:], 1.0 / KVL,
                                                  float(EPS_RMS), OP.mult, OP.add))
                v.wait_ge(sem_dve.h, n_)
                v.wait_ge(sem_act.h, a0 + 2 * i + 2)
                if i >= 2:
                    stP.wait(v, i - 2)
                sem_dve.einc(v.tensor_scalar(ckb[j % 2][:], ps[:, 0:KVL],
                                             rstd[:], None, OP.mult, OP.bypass))
                tbP.wait(v, i)
                if i >= 2:
                    st2P.wait(v, i - 2)
                rr = ps[:, KVL:KVL + DROPE]
                x1 = rr.rearrange("p (j t) -> p j t", t=2)[:, :, 0:1].squeeze()
                x2 = rr.rearrange("p (j t) -> p j t", t=2)[:, :, 1:2].squeeze()
                ov = krb[j % 2][:, 0:DROPE]
                o1 = ov.rearrange("p (j t) -> p j t", t=2)[:, :, 0:1].squeeze()
                o2 = ov.rearrange("p (j t) -> p j t", t=2)[:, :, 1:2].squeeze()
                V(v.tensor_tensor(rt1[:], x1, tabc[i % 2][:], OP.mult))
                V(v.tensor_tensor(rt2[:], x2, tabs[i % 2][:], OP.mult))
                V(v.tensor_tensor(o1, rt1[:], rt2[:], OP.subtract))
                V(v.tensor_tensor(rt1[:], x1, tabs[i % 2][:], OP.mult))
                V(v.tensor_tensor(rt2[:], x2, tabc[i % 2][:], OP.mult))
                n_ = sem_dve2.einc(v.tensor_tensor(o2, rt1[:], rt2[:], OP.add))
                v.wait_ge(sem_dve2.h, n_)


def ph_kvb(c, block):
    arena = c["arena"]
    sem_pe, sem_act, sem_ld2 = c["sem_pe"], c["sem_act"], c["sem_ld2"]
    ldP = PSem(c, "sem_ld", "sem_ldB")
    stP = PSem(c, "sem_st", "sem_stB")
    st2P = PSem(c, "sem_st2a", "sem_st2b")
    psA, psB = c["psA"], c["psB"]
    wkvb = arena.alloc(128, 4 * 4096, BF16)
    ckf = [arena.alloc(128, 4 * 512, BF16) for _ in range(2)]
    knb = [arena.alloc(128, 512, BF16) for _ in range(2)]
    vb = [arena.alloc(128, 2048, BF16) for _ in range(2)]
    pe0, a0 = sem_pe.n, sem_act.n
    ldP.pre([s for s in range(NSUP) for _ in range(4)])
    stP.pre(range(NSUP * 16))
    st2P.pre(range(NSUP * 4))
    PE_S = 20
    ACT_S = 20

    @block.sync
    def _(sync):
        for lc in range(4):
            sem_ld2.dinc(sync.dma_start(wkvb[:, lc * 4096:(lc + 1) * 4096],
                                        c["s_wkvbbf"].ap()[lc * 128:(lc + 1) * 128, :]))
        for s in range(NSUP):
            if s >= 2:
                sync.wait_ge(sem_pe.h, pe0 + PE_S * (s - 1))
            for lc in range(4):
                ldP.issue(s, sync.dma_start(
                    ckf[s % 2][:, lc * 512:(lc + 1) * 512],
                    c["s_ckvn"].ap()[s * SUP:(s + 1) * SUP, lc * 128:(lc + 1) * 128],
                    transpose=True))
            seq, pos = s // 4, (s % 4) * SUP
            for h in range(16):
                k = s * 16 + h
                sync.wait_ge(sem_act.h, a0 + ACT_S * s + h + 1)
                stP.issue(k, sync.dma_start(
                    c["s_knope"].ap()[(seq * NH + h) * DNOPE:(seq * NH + h + 1) * DNOPE,
                                      pos:pos + SUP],
                    knb[h % 2][:]))
            for j in range(4):
                k = s * 4 + j
                sync.wait_ge(sem_act.h, a0 + ACT_S * s + 16 + j + 1)
                st2P.issue(k, sync.dma_start(
                    c["s_v"].ap()[s * SUP + j * TS: s * SUP + (j + 1) * TS, :],
                    vb[j % 2][:]))
        sync.wait_ge(stP.s[0].h, stP.s[0].n)
        sync.wait_ge(stP.s[1].h, stP.s[1].n)
        sync.wait_ge(st2P.s[0].h, st2P.s[0].n)
        sync.wait_ge(st2P.s[1].h, st2P.s[1].n)

    LD2_ALL = sem_ld2.n

    @block.tensor
    def _(te):
        te.wait_ge(sem_ld2.h, LD2_ALL)
        for s in range(NSUP):
            ldP.wait(te, s)
            for h in range(16):
                if h >= 4 or s >= 1:
                    kk = h - 4 if h >= 4 else 12 + h
                    ss = s if h >= 4 else s - 1
                    te.wait_ge(sem_act.h, a0 + ACT_S * ss + kk + 1)
                for lc in range(4):
                    inst = te.matmul(
                        psA[:, (h % 4) * 512:(h % 4 + 1) * 512],
                        wkvb[:, lc * 4096 + h * 256: lc * 4096 + h * 256 + 128],
                        ckf[s % 2][:, lc * 512:(lc + 1) * 512],
                        start=(lc == 0), stop=(lc == 3))
                sem_pe.einc(inst)
            for j in range(4):
                if j >= 1 or s >= 1:
                    jj = j - 1 if j >= 1 else 3
                    ss = s if j >= 1 else s - 1
                    te.wait_ge(sem_act.h, a0 + ACT_S * ss + 16 + jj + 1)
                for bs in range(4):
                    for lc in range(4):
                        vview = wkvb[:, lc * 4096: (lc + 1) * 4096].rearrange(
                            "p (h z) -> p h z", z=256)[:, 4 * bs:4 * bs + 4, 128:256]
                        inst = te.matmul(
                            psB[:, bs * 512:(bs + 1) * 512],
                            ckf[s % 2][:, lc * 512 + j * 128: lc * 512 + j * 128 + 128],
                            vview,
                            start=(lc == 0), stop=(lc == 3))
                sem_pe.einc(inst)

    @block.scalar
    def _(sc):
        for s in range(NSUP):
            for h in range(16):
                k = s * 16 + h
                sc.wait_ge(sem_pe.h, pe0 + PE_S * s + h + 1)
                if k >= 2:
                    kp = k - 2 if h >= 2 else (s - 1) * 16 + 14 + h
                    stP.wait(sc, kp)
                sem_act.einc(sc.activation(knb[h % 2][:],
                                           psA[:, (h % 4) * 512:(h % 4 + 1) * 512],
                                           AF.Copy, bias=0.0))
            for j in range(4):
                k = s * 4 + j
                sc.wait_ge(sem_pe.h, pe0 + PE_S * s + 16 + j + 1)
                if k >= 2:
                    st2P.wait(sc, k - 2)
                sem_act.einc(sc.activation(vb[j % 2][:], psB[:], AF.Copy, bias=0.0))


def _attn_sched():
    seq = []
    nd = d = n = 0
    for qs in range(4):
        for kc in range(4 * qs + 4):
            diag = kc >= 4 * qs
            seq.append((qs, kc, diag, nd, d, n))
            if diag:
                d += 1
            else:
                nd += 1
            n += 1
    return seq

ATTN_SEQ = _attn_sched()
N_EXP = len(ATTN_SEQ)
ACT_G = 3 + N_EXP
POOL_G = 2 + 16


def _pe_vals():
    vals = {}
    v = 0
    for name in ("qs2", "ks2", "bcast"):
        v += 1
        vals[name] = v
    for qs in range(4):
        K = 4 * qs + 4
        evs = [e for e in ATTN_SEQ if e[0] == qs]
        for e in evs:
            v += 1
            vals[("S", e[5])] = v
        for e in evs:
            last = e[1] == K - 1
            v += 2 if last else 1
            vals[("RP", e[5])] = v
            if last:
                vals[("qsdone", qs)] = v
    return vals, v

PE_VALS, PE_G = _pe_vals()


def ph_attn(c, block):
    arena = c["arena"]
    sem_pe, sem_act, sem_dve = c["sem_pe"], c["sem_act"], c["sem_dve"]
    sem_dve2, sem_pool = c["sem_dve2"], c["sem_pool"]
    sem_vself = c["sem_vself"]
    sem_ld2 = c["sem_ld2"]
    ldP = PSem(c, "sem_ld", "sem_ldB")
    stP = PSem(c, "sem_st", "sem_stB")
    psA, psB = c["psA"], c["psB"]
    qn = [arena.alloc(128, T, BF16) for _ in range(2)]
    qr = [arena.alloc(128, T, BF16) for _ in range(2)]
    kn = [arena.alloc(128, T, BF16) for _ in range(2)]
    vv = [arena.alloc(128, 16 * DV, BF16) for _ in range(2)]
    krf = [arena.alloc(128, T, BF16) for _ in range(2)]
    tri = arena.alloc(128, 1024)
    qsq = arena.alloc(128, T, BF16)
    qsq2 = arena.alloc(64, T, BF16)
    attnq = arena.alloc(128, 16 * 512, BF16)
    smsk = [arena.alloc(128, 512) for _ in range(4)]
    rb = arena.alloc(128, 512)
    osb = [arena.alloc(128, 512, BF16) for _ in range(2)]
    mq = arena.alloc(1, 1)
    mk = arena.alloc(1, 1)
    nb = arena.alloc(1, 1)
    nbB = arena.alloc(128, 1)
    pe0, a0, dv0, dv20, p0 = (sem_pe.n, sem_act.n, sem_dve.n, sem_dve2.n,
                              sem_pool.n)
    ldP.pre([g for g in range(32) for _ in range(7)])
    stP.pre(range(32 * 4))

    def actT(g, name, n=0):
        base = a0 + ACT_G * g
        return base + {"sqq": 1, "sqk": 2, "sqrt": 3}.get(name, 4 + n)

    def peT(g, name, n=0, qs=0):
        base = pe0 + PE_G * g
        if name in ("qs2", "ks2", "bcast"):
            return base + PE_VALS[name]
        if name == "qsdone":
            return base + PE_VALS[("qsdone", qs)]
        return base + PE_VALS[(name, n)]

    def poolT(g, name, d=0):
        base = p0 + POOL_G * g
        return base + {"b1": 1, "b2": 2}.get(name, 3 + d)

    def qsdone_before(n):
        cnt = 0
        ncum = 0
        for q in range(4):
            ncum += 4 * q + 4
            if n >= ncum:
                cnt += 1
        return cnt

    @block.sync
    def _(sync):
        sem_ld2.dinc(sync.dma_start(tri[:], c["TRI"].ap()))
        for sq in range(2):
            sem_ld2.dinc(sync.dma_start(
                krf[sq][:], c["s_krope"].ap()[sq * T:(sq + 1) * T, :],
                transpose=True))
        for g in range(32):
            sq, h = g // 16, g % 16
            if g >= 2:
                sync.wait_ge(sem_dve.h, dv0 + 4 * (g - 2) + 4)
            ldP.issue(g, sync.dma_start(
                qn[g % 2][:], c["s_qnope"].ap()[h * 128:(h + 1) * 128,
                                                sq * T:(sq + 1) * T]))
            ldP.issue(g, sync.dma_start(
                qr[g % 2][:], c["s_qrope"].ap()[sq * T:(sq + 1) * T,
                                                h * 128:(h + 1) * 128],
                transpose=True))
            ldP.issue(g, sync.dma_start(
                kn[g % 2][:],
                c["s_knope"].ap()[(sq * NH + h) * DNOPE:(sq * NH + h + 1) * DNOPE, :]))
            for part in range(4):
                ldP.issue(g, sync.dma_start(
                    vv[g % 2][:].rearrange("p (kc z) -> p kc z", z=DV)[
                        :, part * 4:(part + 1) * 4, :],
                    c["s_v"].ap()[sq * T + part * 512: sq * T + (part + 1) * 512,
                                  h * DV:(h + 1) * DV].rearrange(
                        "(kc p) z -> p kc z", p=128)))
            for qs in range(4):
                sync.wait_ge(sem_dve.h, dv0 + 4 * g + qs + 1)
                stP.issue(4 * g + qs, sync.dma_start(
                    c["s_o"].ap()[(sq * NH + h) * DV:(sq * NH + h + 1) * DV,
                                  qs * 512:(qs + 1) * 512],
                    osb[qs % 2][:]))
        sync.wait_ge(stP.s[0].h, stP.s[0].n)
        sync.wait_ge(stP.s[1].h, stP.s[1].n)

    LD2_ALL = sem_ld2.n

    @block.tensor
    def _(te):
        te.wait_ge(sem_ld2.h, LD2_ALL)
        for g in range(32):
            sq = g // 16
            ldP.wait(te, g)
            te.wait_ge(sem_act.h, actT(g, "sqq"))
            for sl in range(4):
                te.matmul(psA[0:1, sl * 512:(sl + 1) * 512],
                          c["k_ones_bf"][:, 0:1],
                          qsq[:, sl * 512:(sl + 1) * 512], start=True, stop=False)
                inst = te.matmul(psA[0:1, sl * 512:(sl + 1) * 512],
                                 c["k_ones_bf"][0:64, 0:1],
                                 qsq2[:, sl * 512:(sl + 1) * 512],
                                 start=False, stop=True)
            sem_pe.einc(inst)
            te.wait_ge(sem_act.h, actT(g, "sqk"))
            for sl in range(4):
                te.matmul(psB[0:1, sl * 512:(sl + 1) * 512],
                          c["k_ones_bf"][:, 0:1],
                          qsq[:, sl * 512:(sl + 1) * 512], start=True, stop=False)
                inst = te.matmul(psB[0:1, sl * 512:(sl + 1) * 512],
                                 c["k_ones_bf"][0:64, 0:1],
                                 qsq2[:, sl * 512:(sl + 1) * 512],
                                 start=False, stop=True)
            sem_pe.einc(inst)
            te.wait_ge(sem_pool.h, poolT(g, "b2"))
            sem_pe.einc(te.matmul(psB[:, 0:1], c["k_ones_f"][0:1, :],
                                  nb[:], start=True, stop=True))
            for qs in range(4):
                K = 4 * qs + 4
                evs = [e for e in ATTN_SEQ if e[0] == qs]
                for (qs_, kc, diag, nd, d, n) in evs:
                    reg = psA[:, (n % 4) * 512: (n % 4 + 1) * 512]
                    if n >= 4:
                        te.wait_ge(sem_act.h, actT(g, "exp", n - 4))
                    te.matmul(reg, kn[g % 2][:, kc * 128:(kc + 1) * 128],
                              qn[g % 2][:, qs * 512:(qs + 1) * 512],
                              start=True, stop=False)
                    sem_pe.einc(te.matmul(reg, krf[sq][0:64, kc * 128:(kc + 1) * 128],
                                          qr[g % 2][0:64, qs * 512:(qs + 1) * 512],
                                          start=False, stop=True))
                sumb = psB[:, (qs % 2) * 1024: (qs % 2) * 1024 + 512]
                ot = psB[:, (qs % 2) * 1024 + 512: (qs % 2) * 1024 + 1024]
                for (qs_, kc, diag, nd, d, n) in evs:
                    te.wait_ge(sem_act.h, actT(g, "exp", n))
                    if qs >= 2 and kc == 0:
                        te.wait_ge(sem_dve.h, dv0 + 4 * g + (qs - 2) + 1)
                    last = kc == K - 1
                    ab = attnq[:, kc * 512:(kc + 1) * 512]
                    te.matmul(sumb, c["k_ones_bf"][:, 0:128], ab,
                              start=(kc == 0), stop=last)
                    inst = te.matmul(
                        ot, vv[g % 2][:, kc * DV:(kc + 1) * DV], ab,
                        start=(kc == 0), stop=last)
                    inst.then_inc(sem_pe.h, 2 if last else 1)
                    sem_pe.n += 2 if last else 1

    @block.scalar
    def _(sc):
        SS = mk_chain(sc, c["sem_sself"])
        for g in range(32):
            ldP.wait(sc, g)
            sc.activation(qsq[:], qn[g % 2][:], AF.Square)
            sem_act.einc(sc.activation(qsq2[:], qr[g % 2][0:64, :], AF.Square))
            te_dummy = None
            sc.wait_ge(sem_pe.h, peT(g, "qs2"))
            sc.activation(qsq[:], kn[g % 2][:], AF.Square)
            sem_act.einc(sc.activation(qsq2[:], krf[g // 16][0:64, :], AF.Square))
            sc.wait_ge(sem_pool.h, poolT(g, "b1"))
            sem_act.einc(sc.activation(nb[:], nb[:], AF.Sqrt))
            for (qs, kc, diag, nd, d, n) in ATTN_SEQ:
                sc.wait_ge(sem_pe.h, peT(g, "S", n))
                sc.wait_ge(sem_dve2.h, dv20 + g + 1)
                if qs >= 1 and kc < 4 * qs:
                    prev_n = [e[5] for e in ATTN_SEQ
                              if e[0] == qs - 1 and e[1] == kc][0]
                    sc.wait_ge(sem_pe.h, peT(g, "RP", prev_n))
                if diag:
                    sc.wait_ge(sem_pool.h, poolT(g, "m", d))
                    src_ap = smsk[d % 4][:]
                else:
                    src_ap = psA[:, (n % 4) * 512: (n % 4 + 1) * 512]
                sem_act.einc(sc.activation(attnq[:, kc * 512:(kc + 1) * 512],
                                           src_ap, AF.Exp, bias=nbB[:]))

    @block.vector
    def _(v):
        V = mk_chain(v, sem_vself)
        for g in range(32):
            v.wait_ge(sem_pe.h, peT(g, "qs2"))
            V(v.tensor_reduce(mq[:], psA[0:1, 0:2048], AX.X, OP.max))
            v.wait_ge(sem_pe.h, peT(g, "ks2"))
            V(v.tensor_reduce(mk[:], psB[0:1, 0:2048], AX.X, OP.max))
            V(v.tensor_tensor(nb[:], mq[:], mk[:], OP.mult))
            n_ = sem_pool.einc(v.tensor_scalar(nb[:], nb[:], 1.02, None,
                                               OP.mult, OP.bypass))
            v.wait_ge(sem_pool.h, n_)
            v.wait_ge(sem_act.h, actT(g, "sqrt"))
            n_ = sem_pool.einc(v.tensor_scalar(nb[:], nb[:], -1.0, None,
                                               OP.mult, OP.bypass))
            v.wait_ge(sem_pool.h, n_)
            v.wait_ge(sem_pe.h, peT(g, "bcast"))
            n_ = sem_dve2.einc(v.tensor_copy(nbB[:], psB[:, 0:1]))
            v.wait_ge(sem_dve2.h, n_)
            for (qs, kc, diag, nd, d, n) in ATTN_SEQ:
                if diag:
                    v.wait_ge(sem_pe.h, peT(g, "S", n))
                    if qs >= 1:
                        prev_n = [e[5] for e in ATTN_SEQ
                                  if e[0] == qs - 1 and e[2] and e[4] % 4 == d % 4][0]
                        v.wait_ge(sem_act.h, actT(g, "exp", prev_n))
                    delta = kc * 128 - qs * 512
                    n_ = sem_pool.einc(v.tensor_tensor(
                        smsk[d % 4][:],
                        psA[:, (n % 4) * 512: (n % 4 + 1) * 512],
                        tri[:, 384 - delta: 384 - delta + 512], OP.add))
                    v.wait_ge(sem_pool.h, n_)
                if kc == 4 * qs + 3:
                    v.wait_ge(sem_pe.h, peT(g, "qsdone", qs=qs))
                    if 4 * g + qs >= 2:
                        stP.wait(v, 4 * g + qs - 2)
                    sumb = psB[:, (qs % 2) * 1024: (qs % 2) * 1024 + 512]
                    ot = psB[:, (qs % 2) * 1024 + 512: (qs % 2) * 1024 + 1024]
                    V(v.reciprocal(rb[:], sumb))
                    sem_dve.einc(v.tensor_tensor(osb[qs % 2][:], ot, rb[:], OP.mult))


def ph_wo(c, block):
    arena = c["arena"]
    sem_pe, sem_act, sem_gp = c["sem_pe"], c["sem_act"], c["sem_gp"]
    ldP = PSem(c, "sem_ld", "sem_ldB")
    stP = PSem(c, "sem_st", "sem_stB")
    psA, psB = c["psA"], c["psB"]
    wo = arena.alloc(128, 16 * 2048, BF16)
    ob = [arena.alloc(128, 16 * 128, BF16) for _ in range(2)]
    s1b = [arena.alloc(128, D) for _ in range(2)]
    pe0, a0 = sem_pe.n, sem_act.n
    ldP.pre([i for i in range(NSLICE) for _ in range(16)])
    stP.pre(range(NSLICE))

    @block.gpsimd
    def _(gp):
        for hc in range(16):
            sem_gp.dinc(gp.dma_start(wo[:, hc * 2048:(hc + 1) * 2048],
                                     c["Wo"].ap()[hc * 128:(hc + 1) * 128, :]))
    GP_ALL = sem_gp.n

    @block.sync
    def _(sync):
        for i in range(NSLICE):
            sq, pos = i // 16, (i % 16) * TS
            if i >= 2:
                sync.wait_ge(sem_pe.h, pe0 + (i - 1))
            for h in range(16):
                ldP.issue(i, sync.dma_start(
                    ob[i % 2][:, h * 128:(h + 1) * 128],
                    c["s_o"].ap()[(sq * NH + h) * DV:(sq * NH + h + 1) * DV,
                                  pos:pos + TS]))
            if i >= 1:
                sync.wait_ge(sem_act.h, a0 + i)
                stP.issue(i - 1, sync.dma_start(
                    c["s_s1"].ap()[(i - 1) * TS:i * TS, :], s1b[(i - 1) % 2][:]))
        sync.wait_ge(sem_act.h, a0 + NSLICE)
        stP.issue(NSLICE - 1, sync.dma_start(
            c["s_s1"].ap()[(NSLICE - 1) * TS:NSLICE * TS, :], s1b[(NSLICE - 1) % 2][:]))
        sync.wait_ge(stP.s[0].h, stP.s[0].n)
        sync.wait_ge(stP.s[1].h, stP.s[1].n)

    @block.tensor
    def _(te):
        te.wait_ge(sem_gp.h, GP_ALL)
        for i in range(NSLICE):
            ldP.wait(te, i)
            if i >= 2:
                te.wait_ge(sem_act.h, a0 + (i - 1))
            ps = [psA, psB][i % 2]
            for hc in range(16):
                for bs in range(4):
                    inst = te.matmul(ps[:, bs * 512:(bs + 1) * 512],
                                     ob[i % 2][:, hc * 128:(hc + 1) * 128],
                                     wo[:, hc * 2048 + bs * 512: hc * 2048 + (bs + 1) * 512],
                                     start=(hc == 0), stop=(hc == 15))
            sem_pe.einc(inst)

    @block.scalar
    def _(sc):
        for i in range(NSLICE):
            sc.wait_ge(sem_pe.h, pe0 + i + 1)
            if i >= 2:
                stP.wait(sc, i - 2)
            ps = [psA, psB][i % 2]
            sem_act.einc(sc.activation(s1b[i % 2][:], ps[:], AF.Copy, bias=0.0))


def ph_ffn1(c, block):
    arena = c["arena"]
    sem_pe, sem_act, sem_dve, sem_dve2 = (c["sem_pe"], c["sem_act"],
                                          c["sem_dve"], c["sem_dve2"])
    sem_vself, sem_sself = c["sem_vself"], c["sem_sself"]
    sem_ld2, sem_gp = c["sem_ld2"], c["sem_gp"]
    ldP = PSem(c, "sem_ld", "sem_ldB")
    stP = PSem(c, "sem_st", "sem_stB")
    psA, psB = c["psA"], c["psB"]
    slabbf = [t[:].bitcast(BF16) for t in c["slab"]]
    hbf = [t[:].bitcast(BF16) for t in c["Hsb"]]

    def bwT_dc(dc):
        return slabbf[dc // 4][:, (dc % 4) * 2048:(dc % 4 + 1) * 2048]

    laA = hbf[0][:, 0:2048]
    laB = hbf[1][:, 0:2048]
    rwT = arena.alloc(128, 16 * 16, BF16)
    xf = [arena.alloc(128, 16 * 512, BF16) for _ in range(2)]
    g1b = [arena.alloc(128, D) for _ in range(2)]
    gmax = arena.alloc(128, 1)
    nmax = arena.alloc(128, 1)
    m2_ = arena.alloc(128, 1)
    den = arena.alloc(128, 1)
    gsum = arena.alloc(128, 1)
    grec = arena.alloc(128, 1)
    gexp = arena.alloc(128, NE)
    gprob = arena.alloc(128, NE)
    gmask = arena.alloc(128, NE)
    ge2 = arena.alloc(128, NE)
    gate = arena.alloc(128, NE)
    gateT = arena.alloc(NE, 128)
    gateB = arena.alloc(128, 128)
    mids = arena.alloc(128, 128, BF16)
    pe0, a0, dv0, dv20 = sem_pe.n, sem_act.n, sem_dve.n, sem_dve2.n
    ldP.pre([s for s in range(NSUP) for _ in range(16)])
    stP.pre(range(NSLICE))
    PE_I, DVE_I, ACT_I = 5, 4, 2

    @block.gpsimd
    def _(gp):
        for dc in range(16):
            sem_gp.dinc(gp.dma_start(
                laA[:, dc * 128:(dc + 1) * 128].rearrange("p (e r) -> p e r", r=RL),
                c["lora_A"].ap().transpose([1, 0, 2])[dc * 128:(dc + 1) * 128, :, :]))
        sem_gp.dinc(gp.dma_start(
            laB[:], c["lora_B"].ap().rearrange("e r d -> (e r) d")))
    GP_ALL = sem_gp.n

    @block.sync
    def _(sync):
        for dc in range(16):
            sem_ld2.dinc(sync.dma_start(bwT_dc(dc),
                                        c["s_bwbf"].ap()[:, dc * 128:(dc + 1) * 128],
                                        transpose=True))
            sem_ld2.dinc(sync.dma_start(rwT[:, dc * 16:(dc + 1) * 16],
                                        c["s_rwbf"].ap()[:, dc * 128:(dc + 1) * 128],
                                        transpose=True))
        for s in range(NSUP):
            if s >= 2:
                sync.wait_ge(sem_pe.h, pe0 + PE_I * 4 * (s - 1))
            for dc in range(16):
                ldP.issue(s, sync.dma_start(
                    xf[s % 2][:, dc * 512:(dc + 1) * 512],
                    c["s_xn1"].ap()[s * SUP:(s + 1) * SUP, dc * 128:(dc + 1) * 128],
                    transpose=True))
            for j in range(4):
                i = s * 4 + j
                sync.wait_ge(sem_act.h, a0 + ACT_I * i + 2)
                stP.issue(i, sync.dma_start(
                    c["s_g1"].ap()[i * TS:(i + 1) * TS, :], g1b[i % 2][:]))
        sync.wait_ge(stP.s[0].h, stP.s[0].n)
        sync.wait_ge(stP.s[1].h, stP.s[1].n)

    LD2_ALL = sem_ld2.n

    @block.tensor
    def _(te):
        te.wait_ge(sem_ld2.h, LD2_ALL)
        te.wait_ge(sem_gp.h, GP_ALL)
        for s in range(NSUP):
            ldP.wait(te, s)
            for j in range(4):
                i = s * 4 + j
                xsl = xf[s % 2]
                if i >= 1:
                    te.wait_ge(sem_act.h, a0 + ACT_I * (i - 1) + 1)
                for dc in range(16):
                    te.matmul(psB[:, 0:8],
                              xsl[:, dc * 512 + j * 128: dc * 512 + j * 128 + 128],
                              rwT[:, dc * 16: dc * 16 + 8],
                              start=(dc == 0), stop=False)
                sem_pe.einc(te.matmul(psB[:, 0:8], c["k_ones_bf"][0:1, 0:128],
                                      c["rbrow"][0:1, :], start=False, stop=True))
                te.wait_ge(sem_dve.h, dv0 + DVE_I * i + 1)
                sem_pe.einc(te.transpose(psB[0:8, 512:640], gate[:], c["k_id_f"][:]))
                te.wait_ge(sem_dve.h, dv0 + DVE_I * i + 2)
                sem_pe.einc(te.matmul(psB[:, 640:768], c["k_sel2f"][:],
                                      gateT[:], start=True, stop=True))
                if i >= 1:
                    te.wait_ge(sem_act.h, a0 + ACT_I * (i - 1) + 2)
                for dc in range(16):
                    for bs in range(4):
                        te.matmul(psA[:, bs * 512:(bs + 1) * 512],
                                  xsl[:, dc * 512 + j * 128: dc * 512 + j * 128 + 128],
                                  bwT_dc(dc)[:, bs * 512:(bs + 1) * 512],
                                  start=(dc == 0), stop=False)
                te.wait_ge(sem_dve.h, dv0 + DVE_I * i + 3)
                for dc in range(16):
                    inst = te.matmul(psB[:, 768:896], laA[:, dc * 128:(dc + 1) * 128],
                                     xsl[:, dc * 512 + j * 128: dc * 512 + j * 128 + 128],
                                     start=(dc == 0), stop=(dc == 15))
                sem_pe.einc(inst)
                te.wait_ge(sem_dve.h, dv0 + DVE_I * i + 4)
                for bs in range(4):
                    te.matmul(psA[:, bs * 512:(bs + 1) * 512], mids[:],
                              laB[:, bs * 512:(bs + 1) * 512], start=False, stop=False)
                for bs in range(4):
                    inst = te.matmul(psA[:, bs * 512:(bs + 1) * 512],
                                     c["k_ones_bf"][0:1, 0:128],
                                     c["bbrow"][0:1, bs * 512:(bs + 1) * 512],
                                     start=False, stop=True)
                sem_pe.einc(inst)

    @block.scalar
    def _(sc):
        for i in range(NSLICE):
            sc.wait_ge(sem_dve2.h, dv20 + i + 1)
            sem_act.einc(sc.activation(gexp[:], psB[:, 0:8], AF.Exp, bias=nmax[:]))
            sc.wait_ge(sem_pe.h, pe0 + PE_I * i + 5)
            if i >= 2:
                stP.wait(sc, i - 2)
            sem_act.einc(sc.activation(g1b[i % 2][:], psA[:], AF.Silu))

    @block.vector
    def _(v):
        V = mk_chain(v, sem_vself)
        for i in range(NSLICE):
            v.wait_ge(sem_pe.h, pe0 + PE_I * i + 1)
            V(v.tensor_reduce(gmax[:], psB[:, 0:8], AX.X, OP.max))
            n_ = sem_dve2.einc(v.tensor_scalar(nmax[:], gmax[:], -1.0, None,
                                               OP.mult, OP.bypass))
            v.wait_ge(sem_dve2.h, n_)
            v.wait_ge(sem_act.h, a0 + ACT_I * i + 1)
            V(v.tensor_reduce(gsum[:], gexp[:], AX.X, OP.add))
            V(v.reciprocal(grec[:], gsum[:]))
            V(v.tensor_scalar(gprob[:], gexp[:], grec[:], None, OP.mult, OP.bypass))
            V(v.tensor_reduce(gmax[:], gprob[:], AX.X, OP.max))
            V(v.tensor_scalar(gmask[:], gprob[:], gmax[:], None, OP.is_ge, OP.bypass))
            V(v.tensor_tensor(gmask[:], gprob[:], gmask[:], OP.subtract))
            V(v.tensor_reduce(m2_[:], gmask[:], AX.X, OP.max))
            V(v.tensor_tensor(den[:], gmax[:], m2_[:], OP.add))
            V(v.reciprocal(den[:], den[:]))
            V(v.tensor_scalar(ge2[:], gprob[:], m2_[:], None, OP.is_ge, OP.bypass))
            V(v.tensor_tensor(gate[:], gprob[:], ge2[:], OP.mult))
            n_ = sem_dve.einc(v.tensor_scalar(gate[:], gate[:], den[:], None,
                                              OP.mult, OP.bypass))
            v.wait_ge(sem_dve.h, n_)
            v.wait_ge(sem_pe.h, pe0 + PE_I * i + 2)
            n_ = sem_dve.einc(v.tensor_copy(gateT[:], psB[0:8, 512:640]))
            v.wait_ge(sem_dve.h, n_)
            v.wait_ge(sem_pe.h, pe0 + PE_I * i + 3)
            n_ = sem_dve.einc(v.tensor_copy(gateB[:], psB[:, 640:768]))
            v.wait_ge(sem_dve.h, n_)
            v.wait_ge(sem_pe.h, pe0 + PE_I * i + 4)
            n_ = sem_dve.einc(v.tensor_tensor(mids[:], psB[:, 768:896], gateB[:],
                                              OP.mult))
            v.wait_ge(sem_dve.h, n_)


def ph_out(c, block):
    arena = c["arena"]
    sem_dve = c["sem_dve"]
    sem_vself = c["sem_vself"]
    ldP = PSem(c, "sem_ld", "sem_ldB")
    stP = PSem(c, "sem_st", "sem_stB")
    x1b = [arena.alloc(128, D) for _ in range(2)]
    s1b = [arena.alloc(128, D) for _ in range(2)]
    g1b = [arena.alloc(128, D) for _ in range(2)]
    outb = [arena.alloc(128, D) for _ in range(2)]
    dv0 = sem_dve.n
    ldP.pre([i for i in range(NSLICE) for _ in range(3)])
    stP.pre(range(NSLICE))

    @block.sync
    def _(sync):
        for i in range(NSLICE):
            if i >= 2:
                sync.wait_ge(sem_dve.h, dv0 + i - 1)
            ldP.issue(i, sync.dma_start(x1b[i % 2][:], c["s_x1"].ap()[i * TS:(i + 1) * TS, :]))
            ldP.issue(i, sync.dma_start(s1b[i % 2][:], c["s_s1"].ap()[i * TS:(i + 1) * TS, :]))
            ldP.issue(i, sync.dma_start(g1b[i % 2][:], c["s_g1"].ap()[i * TS:(i + 1) * TS, :]))
            if i >= 1:
                sync.wait_ge(sem_dve.h, dv0 + i)
                stP.issue(i - 1, sync.dma_start(
                    c["out"].ap()[(i - 1) * TS:i * TS, :], outb[(i - 1) % 2][:]))
        sync.wait_ge(sem_dve.h, dv0 + NSLICE)
        stP.issue(NSLICE - 1, sync.dma_start(
            c["out"].ap()[(NSLICE - 1) * TS:NSLICE * TS, :], outb[(NSLICE - 1) % 2][:]))
        sync.wait_ge(stP.s[0].h, stP.s[0].n)
        sync.wait_ge(stP.s[1].h, stP.s[1].n)

    @block.vector
    def _(v):
        V = mk_chain(v, sem_vself)
        for i in range(NSLICE):
            ldP.wait(v, i)
            if i >= 2:
                stP.wait(v, i - 2)
            V(v.tensor_tensor(outb[i % 2][:], s1b[i % 2][:], g1b[i % 2][:], OP.mult))
            sem_dve.einc(v.tensor_tensor(outb[i % 2][:], outb[i % 2][:],
                                         x1b[i % 2][:], OP.add))


def kernel(**inputs):
    _ = inputs["L0_ffn_gate_w"]
    consts = host_constants()
    x = np.ascontiguousarray(inputs["L0_attn_x"])
    nc = build_program(n_phases=PHASES_LIMIT, dump=DBG_DUMP)
    base = {
        "nu_log": inputs["L0_attn_nu_log"], "theta_log": inputs["L0_attn_theta_log"],
        "B_re": inputs["L0_attn_B_re"], "B_im": inputs["L0_attn_B_im"],
        "C_re": inputs["L0_attn_C_re"], "C_im": inputs["L0_attn_C_im"],
        "Dvec": inputs["L0_attn_D"],
        "proj1": inputs["L0_ffn_proj1"], "proj1_bias": inputs["L0_ffn_proj1_bias"],
        "proj2": inputs["L0_ffn_proj2"], "proj2_bias": inputs["L0_ffn_proj2_bias"],
        "Wq": inputs["L1_attn_Wq"], "Wkv_a": inputs["L1_attn_Wkv_a"],
        "kv_norm_w": inputs["L1_attn_kv_norm_w"], "Wkv_b": inputs["L1_attn_Wkv_b"],
        "Wo": inputs["L1_attn_Wo"],
        "base_w": inputs["L1_ffn_base_w"], "base_b": inputs["L1_ffn_base_b"],
        "router_w": inputs["L1_ffn_router_w"], "router_b": inputs["L1_ffn_router_b"],
        "lora_A": inputs["L1_ffn_lora_A"], "lora_B": inputs["L1_ffn_lora_B"],
    }
    base.update(consts)
    in_maps = []
    for cid in range(NCORE):
        m = dict(base)
        m["x"] = np.ascontiguousarray(x[cid * SEQS:(cid + 1) * SEQS].reshape(NTOK, D))
        in_maps.append(m)
    res = run_bass_kernel_spmd(nc, in_maps, core_ids=list(range(NCORE)))
    out = np.empty((BATCH, T, D), dtype=np.float32)
    for cid in range(NCORE):
        out[cid * SEQS:(cid + 1) * SEQS] = res.results[cid]["out"].reshape(SEQS, T, D)
    return out
